# version 06
# baseline (speedup 1.0000x reference)
import os
import numpy as np
import concourse.bass as bass
import concourse.mybir as mybir
from concourse.bass_utils import run_bass_kernel_spmd

F32 = mybir.dt.float32
BF16 = mybir.dt.bfloat16
FP8 = mybir.dt.float8e4
ALU = mybir.AluOpType
AF = mybir.ActivationFunctionType
AX = mybir.AxisListType

B, TT, C = 4, 4096, 1024
H, T, K = 16, 128, 64
NCC = C // 128
OWN0 = 16
OWNTOK = 2048
FDIM = 2048
GEPS = 64e-5
EPS = 1e-5
TAU = 512
NTAU = OWNTOK // TAU

M_K, M_V, M_R, M_G = 0, 1, 2, 3
KPART = os.environ.get("KPART", "full")
KDBG = int(os.environ.get("KDBG", "0"))
KCTX = int(os.environ.get("KCTX", "16"))
KGRP = int(os.environ.get("KGRP", "8"))
KEXP = int(os.environ.get("KEXP", "9"))


def build_nc():
    nc = bass.Bass("TRN2", target_bir_lowering=False)

    xin = nc.dram_tensor("xin", [TT, C], F32, kind="ExternalInput").ap()
    wT = {}
    for nm in ("wr", "wk", "wv", "wg", "wo"):
        wT[nm] = nc.dram_tensor(nm + "T", [C, C], F32, kind="ExternalInput").ap()
    tmix_d = nc.dram_tensor("tmix", [128, 32], F32, kind="ExternalInput").ap()
    tdu_d = nc.dram_tensor("tdu", [1, 32], F32, kind="ExternalInput").ap()
    lnrow_d = nc.dram_tensor("lnrow", [1, 2048], F32, kind="ExternalInput").ap()
    gbrow_d = nc.dram_tensor("gbrow", [1, 2048], F32, kind="ExternalInput").ap()
    sw1T = nc.dram_tensor("sw1T", [C, FDIM], F32, kind="ExternalInput").ap()
    sw2T = nc.dram_tensor("sw2T", [FDIM, C], F32, kind="ExternalInput").ap()
    ew1T = nc.dram_tensor("ew1T", [8, C, FDIM], F32, kind="ExternalInput").ap()
    ew2T = nc.dram_tensor("ew2T", [8, FDIM, C], F32, kind="ExternalInput").ap()
    rwT_d = nc.dram_tensor("rwT", [C, 8], F32, kind="ExternalInput").ap()
    ident_d = nc.dram_tensor("ident", [128, 128], F32, kind="ExternalInput").ap()
    x1sp = nc.dram_tensor("x1sp", [OWNTOK, C], F32,
                          **({"kind": "ExternalOutput"} if KDBG else {})).ap()
    accd = nc.dram_tensor("accd", [128, 8 * OWNTOK], F32,
                          **({"kind": "ExternalOutput"} if KDBG else {})).ap()
    xout = nc.dram_tensor("xout", [OWNTOK, C], F32, kind="ExternalOutput").ap()

    sems = {}
    cnt = {}
    dbg_outs = {}

    def mkdbg(name, shape):
        dbg_outs[name] = nc.dram_tensor("dbg_" + name, shape, F32,
                                        kind="ExternalOutput").ap()
        return dbg_outs[name]

    def dma(e, out, in_, s):
        e.dma_start(out=out, in_=in_).then_inc(sems[s], 16)
        cnt[s] += 16

    def wt(e, s):
        e.wait_ge(sems[s], cnt[s])

    def dbg_dump(e, name, ap):
        if not KDBG:
            return
        if name not in dbg_outs:
            mkdbg(name, list(ap.shape))
        dma(e, dbg_outs[name][tuple(slice(0, s) for s in ap.shape)], ap, "sout")

    def rep3(ap_cols, blk):
        return ap_cols.unsqueeze(2).broadcast_to([128, ap_cols.shape[1], blk])

    def r3(ap2d, blk):
        return ap2d.rearrange("p (n b) -> p n b", b=blk)

    from contextlib import ExitStack
    with ExitStack() as _g:
        _sw = _g.enter_context(nc.semaphore("sw"))
        _sx = _g.enter_context(nc.semaphore("sx"))
        _ssp = _g.enter_context(nc.semaphore("ssp"))
        _sac = _g.enter_context(nc.semaphore("sac"))
        _sout = _g.enter_context(nc.semaphore("sout"))
        _svf = _g.enter_context(nc.semaphore("svf"))
        ident = _g.enter_context(nc.sbuf_tensor("ident_s", [128, 128], F32))
        ones_r = _g.enter_context(nc.sbuf_tensor("ones_r", [1, 128], F32))
        onescol = _g.enter_context(nc.sbuf_tensor("onescol", [128, 1], F32))
        tmv = _g.enter_context(nc.sbuf_tensor("tmv", [128, 32], F32))
        gates = _g.enter_context(nc.sbuf_tensor("gates", [128, 128], F32))
        stat = _g.enter_context(nc.sbuf_tensor("stat", [128, 12], F32))
        epsc = _g.enter_context(nc.sbuf_tensor("epsc", [128, 2], F32))
        ps = _g.enter_context(nc.psum_tensor("ps", [128, 4096], F32))
        sems.update(sw=_sw, sx=_sx, ssp=_ssp, sac=_sac, sout=_sout, vf=_svf)
        cnt.update(sw=0, sx=0, ssp=0, sac=0, sout=0, vf=0)

        def vfence(e, inst):
            inst.then_inc(sems["vf"], 1)
            cnt["vf"] += 1
            e.wait_ge(sems["vf"], cnt["vf"])

        def bank(i):
            return ps[:, i * 512:(i + 1) * 512]

        def bcast_row(row_ap, dst_sbuf_ap, width):
            with nc.Block(no_gpsimd_drain=True) as blk:
                @blk.tensor
                def _(e):
                    for sgi in range(width // 512):
                        e.matmul(ps[:, sgi * 512:sgi * 512 + 512],
                                 ones_r[0:1, :],
                                 row_ap[0:1, sgi * 512:(sgi + 1) * 512],
                                 start=True, stop=True)

            with nc.Block(no_gpsimd_drain=True) as blk:
                @blk.vector
                def _(e):
                    e.tensor_copy(dst_sbuf_ap, ps[:, 0:width])

        def ln_plain(xap, outap, scr, eps_ap=None):
            with nc.Block(no_gpsimd_drain=True) as blk:
                @blk.vector
                def _(e):
                    e.reduce_sum(stat[:, 0:1], xap, axis=AX.X)
                    e.tensor_tensor(scr, xap, xap, ALU.mult)
                    e.reduce_sum(stat[:, 1:2], scr, axis=AX.X)
                    vfence(e, e.tensor_scalar(stat[:, 0:1], stat[:, 0:1],
                                               1.0 / 1024, None, ALU.mult))
                    vfence(e, e.tensor_tensor(stat[:, 2:3], stat[:, 0:1],
                                              stat[:, 0:1], ALU.mult))
                    e.scalar_tensor_tensor(stat[:, 2:3], stat[:, 1:2], 1.0 / 1024,
                                           stat[:, 2:3], ALU.mult, ALU.subtract)

            with nc.Block(no_gpsimd_drain=True) as blk:
                @blk.scalar
                def _(e):
                    e.activation(stat[:, 3:4], stat[:, 2:3], AF.Sqrt,
                                 bias=(eps_ap if eps_ap is not None
                                       else epsc[:, 0:1]))

            with nc.Block(no_gpsimd_drain=True) as blk:
                @blk.vector
                def _(e):
                    vfence(e, e.reciprocal(stat[:, 2:3], stat[:, 3:4]))
                    e.tensor_scalar(outap, xap, stat[:, 0:1], stat[:, 2:3],
                                    ALU.subtract, ALU.mult)

        with ExitStack() as _a:
            wA_hi = _a.enter_context(nc.sbuf_tensor("wA_hi", [128, 8192], BF16))
            wA_lo = _a.enter_context(nc.sbuf_tensor("wA_lo", [128, 8192], BF16))
            wB_hi = _a.enter_context(nc.sbuf_tensor("wB_hi", [128, 8192], BF16))
            wB_lo = _a.enter_context(nc.sbuf_tensor("wB_lo", [128, 8192], BF16))
            wstage = _a.enter_context(nc.sbuf_tensor("wstage", [128, 4096], F32))
            x_tm = _a.enter_context(nc.sbuf_tensor("x_tm", [128, 1024], F32))
            xln_tm = _a.enter_context(nc.sbuf_tensor("xln_tm", [128, 1024], F32))
            xlnf = _a.enter_context(nc.sbuf_tensor("xlnf", [128, 2048], F32))
            mix_hi = _a.enter_context(nc.sbuf_tensor("mix_hi", [128, 1024], BF16))
            mix_lo = _a.enter_context(nc.sbuf_tensor("mix_lo", [128, 1024], BF16))
            dxf = _a.enter_context(nc.sbuf_tensor("dxf", [128, 1024], F32))
            s1 = _a.enter_context(nc.sbuf_tensor("s1", [128, 1024], F32))
            s2 = _a.enter_context(nc.sbuf_tensor("s2", [128, 1024], F32))
            s3 = _a.enter_context(nc.sbuf_tensor("s3", [128, 1024], F32))
            sbig = _a.enter_context(nc.sbuf_tensor("sbig", [128, 1024], F32))
            kwk_hi = _a.enter_context(nc.sbuf_tensor("kwk_hi", [128, 1024], BF16))
            kwk_lo = _a.enter_context(nc.sbuf_tensor("kwk_lo", [128, 1024], BF16))
            kfm_hi = _a.enter_context(nc.sbuf_tensor("kfm_hi", [64, 4096], BF16))
            kfm_lo = _a.enter_context(nc.sbuf_tensor("kfm_lo", [64, 4096], BF16))
            vtm_hi = _a.enter_context(nc.sbuf_tensor("vtm_hi", [128, 2048], BF16))
            vtm_lo = _a.enter_context(nc.sbuf_tensor("vtm_lo", [128, 2048], BF16))
            ssn_hi = _a.enter_context(nc.sbuf_tensor("ssn_hi", [64, 2048], BF16))
            ssn_lo = _a.enter_context(nc.sbuf_tensor("ssn_lo", [64, 2048], BF16))
            rfm_hi = _a.enter_context(nc.sbuf_tensor("rfm_hi", [64, 2048], BF16))
            rfm_lo = _a.enter_context(nc.sbuf_tensor("rfm_lo", [64, 2048], BF16))
            atw_hi = _a.enter_context(nc.sbuf_tensor("atw_hi", [128, 1024], BF16))
            atw_lo = _a.enter_context(nc.sbuf_tensor("atw_lo", [128, 1024], BF16))
            yg_hi = _a.enter_context(nc.sbuf_tensor("yg_hi", [128, 2048], BF16))
            yg_lo = _a.enter_context(nc.sbuf_tensor("yg_lo", [128, 2048], BF16))
            s_f32 = _a.enter_context(nc.sbuf_tensor("s_f32", [64, 1024], F32))
            s_hi = _a.enter_context(nc.sbuf_tensor("s_hi", [64, 1024], BF16))
            s_lo = _a.enter_context(nc.sbuf_tensor("s_lo", [64, 1024], BF16))
            wmT = _a.enter_context(nc.sbuf_tensor("wmT", [128, 2048], F32))
            bcol = _a.enter_context(nc.sbuf_tensor("bcol", [128, 8], F32))
            bhist = _a.enter_context(nc.sbuf_tensor("bhist", [128, 16], F32))
            lndb = _a.enter_context(nc.sbuf_tensor("lndb", [128, 16], F32))
            ub = _a.enter_context(nc.sbuf_tensor("ub", [128, 16], F32))
            wb_t = _a.enter_context(nc.sbuf_tensor("wb_t", [128, 16], F32))
            wk_t = _a.enter_context(nc.sbuf_tensor("wk_t", [128, 16], F32))
            ws_p = _a.enter_context(nc.sbuf_tensor("ws_p", [64, 16], F32))
            lnw_tm = _a.enter_context(nc.sbuf_tensor("lnw_tm", [128, 1024], F32))
            lnb_tm = _a.enter_context(nc.sbuf_tensor("lnb_tm", [128, 1024], F32))
            iotac = _a.enter_context(nc.sbuf_tensor("iotac", [128, 2], F32))
            csc = _a.enter_context(nc.sbuf_tensor("csc", [128, 48], F32))
            dbgst = (_a.enter_context(nc.sbuf_tensor("dbgst", [128, 1024], F32))
                     if KDBG else None)
            with nc.Block() as blk:
                @blk.sync
                def _(e):
                    dma(e, ident[:, :], ident_d[:, :], "sw")
                    dma(e, tmv[:, :], tmix_d[:, :], "sw")
                    dma(e, csc[0:1, 0:32], tdu_d[0:1, :], "sw")
                    dma(e, wstage[0:1, 0:2048], lnrow_d[0:1, :], "sw")

                @blk.gpsimd
                def _(e):
                    e.memset(ones_r[0:1, :], 1.0)
                    e.memset(onescol[:, :], 1.0)
                    e.memset(epsc[:, 0:1], EPS)
                    e.memset(epsc[:, 1:2], GEPS)
                    e.memset(s_f32[:, :], 0.0)
                    e.memset(s_hi[:, :], 0.0)
                    e.memset(s_lo[:, :], 0.0)
                    e.memset(bcol[:, :], 0.0)
                    e.iota(sbig[:, 0:128], [[1, 128]], channel_multiplier=-1,
                           allow_small_or_imprecise_dtypes=True)
                    e.iota(iotac[:, 0:1], [[1, 1]], channel_multiplier=1,
                           allow_small_or_imprecise_dtypes=True)

            with nc.Block(no_gpsimd_drain=True) as blk:
                @blk.scalar
                def _(e):
                    wt(e, "sw")
                    e.activation(csc[0:1, 0:16], csc[0:1, 0:16], AF.Exp)

                @blk.vector
                def _(e):
                    wt(e, "sw")
                    e.tensor_scalar(iotac[:, 1:2], iotac[:, 0:1], -1.0, 127.0,
                                    ALU.mult, ALU.add)
                    e.tensor_scalar(s1[:, 0:128], sbig[:, 0:128], 1.0, 0.0,
                                    ALU.subtract, ALU.max)
                    e.tensor_scalar(s2[:, 0:128], sbig[:, 0:128], 1.0, None,
                                    ALU.is_ge)
                    e.tensor_scalar(s3[:, 0:128], sbig[:, 0:128], 0.0, None,
                                    ALU.is_equal)

            with nc.Block(no_gpsimd_drain=True) as blk:
                @blk.vector
                def _(e):
                    e.tensor_scalar(csc[0:1, 0:16], csc[0:1, 0:16], -1.0, None,
                                    ALU.mult)

            with nc.Block(no_gpsimd_drain=True) as blk:
                @blk.tensor
                def _(e):
                    e.matmul(ps[:, 0:16], ones_r[0:1, :], csc[0:1, 0:16],
                             start=True, stop=False)
                    e.matmul(ps[:, 16:32], ones_r[0:1, :], csc[0:1, 16:32],
                             start=False, stop=True)

            with nc.Block(no_gpsimd_drain=True) as blk:
                @blk.vector
                def _(e):
                    e.tensor_copy(lndb[:, :], ps[:, 0:16])
                    e.tensor_copy(ub[:, :], ps[:, 16:32])

            with nc.Block(no_gpsimd_drain=True) as blk:
                @blk.scalar
                def _(e):
                    for h in range(H):
                        e.activation(wb_t[:, h:h + 1], iotac[:, 0:1], AF.Exp,
                                     scale=lndb[:, h:h + 1])
                        e.activation(wk_t[:, h:h + 1], iotac[:, 1:2], AF.Exp,
                                     scale=lndb[:, h:h + 1])
                    e.activation(csc[:, 0:16], lndb[:, :], AF.Exp, scale=128.0)
                    for h in range(H):
                        e.activation(wmT[:, h * 128:(h + 1) * 128], s1[:, 0:128],
                                     AF.Exp, scale=lndb[:, h:h + 1])

            with nc.Block(no_gpsimd_drain=True) as blk:
                @blk.vector
                def _(e):
                    e.tensor_copy(ws_p[0:64, :], csc[0:64, 0:16])
                    for h in range(H):
                        sl = wmT[:, h * 128:(h + 1) * 128]
                        e.tensor_tensor(sl, sl, s2[:, 0:128], ALU.mult)
                        e.scalar_tensor_tensor(sl, s3[:, 0:128], ub[:, h:h + 1],
                                               sl, ALU.mult, ALU.add)

            bcast_row(wstage[0:1, 0:1024], lnw_tm[:, :], 1024)
            bcast_row(wstage[0:1, 1024:2048], lnb_tm[:, :], 1024)

            def load_wsplit(dsth, dstl, dram):
                def _dma(e, step):
                    srcap = dram[step * 256:(step + 1) * 256, :].rearrange(
                        "(cc p) w -> p cc w", p=128)
                    dma(e, r3(wstage[:, (step % 2) * 2048:
                                     (step % 2) * 2048 + 2048], 1024),
                        srcap, "sw")

                def _split(e, step, when):
                    e.wait_ge(sems["sw"], when)
                    o = step * 2048
                    wsl = wstage[:, (step % 2) * 2048:(step % 2) * 2048 + 2048]
                    e.tensor_copy(dsth[:, o:o + 2048], wsl)
                    e.tensor_tensor(dstl[:, o:o + 2048], wsl,
                                    dsth[:, o:o + 2048], ALU.subtract)

                with nc.Block(no_gpsimd_drain=True) as blk:
                    @blk.sync
                    def _(e):
                        _dma(e, 0)
                for step in range(3):
                    done = cnt["sw"]
                    with nc.Block(no_gpsimd_drain=True) as blk:
                        @blk.sync
                        def _(e, step=step):
                            _dma(e, step + 1)

                        @blk.vector
                        def _(e, step=step, done=done):
                            _split(e, step, done)
                done = cnt["sw"]
                with nc.Block(no_gpsimd_drain=True) as blk:
                    @blk.vector
                    def _(e, done=done):
                        _split(e, 3, done)

            def mix_split(slot, m):
                cur = r3(xlnf[:, slot * 1024:(slot + 1) * 1024], 128)
                bh = bhist[:, slot * 8:slot * 8 + 8].unsqueeze(2)
                tm3 = rep3(tmv[:, m * 8:(m + 1) * 8], 128)
                with nc.Block(no_gpsimd_drain=True) as blk:
                    @blk.vector
                    def _(e):
                        e.tensor_tensor(r3(s2[:, :], 128), r3(dxf[:, :], 128),
                                        tm3, ALU.mult)
                        e.tensor_tensor(r3(s3[:, :], 128)[:, :, 1:128],
                                        cur[:, :, 0:127],
                                        r3(s2[:, :], 128)[:, :, 1:128], ALU.add)
                        e.tensor_tensor(r3(s3[:, :], 128)[:, :, 0:1], bh,
                                        r3(s2[:, :], 128)[:, :, 0:1], ALU.add)
                        e.tensor_copy(mix_hi[:, :], s3[:, :])
                        e.tensor_tensor(mix_lo[:, :], s3[:, :], mix_hi[:, :],
                                        ALU.subtract)

            def proj_tm(whi, wlo, pbank0):
                with nc.Block(no_gpsimd_drain=True) as blk:
                    @blk.tensor
                    def _(e):
                        for half in range(2):
                            out = bank(pbank0 + half)
                            first = True
                            for cc in range(NCC):
                                a_hi = mix_hi[:, cc * 128:(cc + 1) * 128]
                                a_lo = mix_lo[:, cc * 128:(cc + 1) * 128]
                                woff = cc * 1024 + half * 512
                                e.matmul(out, a_hi, whi[:, woff:woff + 512],
                                         start=first, stop=False)
                                first = False
                                e.matmul(out, a_hi, wlo[:, woff:woff + 512],
                                         start=False, stop=False)
                                e.matmul(out, a_lo, whi[:, woff:woff + 512],
                                         start=False, stop=(cc == NCC - 1))

            def transpose8(src, pbank0):
                with nc.Block(no_gpsimd_drain=True) as blk:
                    @blk.tensor
                    def _(e):
                        for oc in range(NCC):
                            e.transpose(ps[:, pbank0 * 512 + oc * 128:
                                           pbank0 * 512 + (oc + 1) * 128],
                                        src[:, oc * 128:(oc + 1) * 128],
                                        ident[:, :])

            def chunk_part1(n, slot, is_ctx):
                tok0 = n * T
                with nc.Block(no_gpsimd_drain=True) as blk:
                    @blk.sync
                    def _(e):
                        dma(e, x_tm[:, :], xin[tok0:tok0 + T, :], "sx")

                with nc.Block(no_gpsimd_drain=True) as blk:
                    @blk.vector
                    def _(e):
                        wt(e, "sx")

                    @blk.sync
                    def _(e):
                        wt(e, "ssp")

                ln_plain(x_tm[:, :], xln_tm[:, :], s1[:, :])
                transpose8(xln_tm, 0)
                with nc.Block(no_gpsimd_drain=True) as blk:
                    @blk.vector
                    def _(e):
                        e.tensor_copy(bhist[:, slot * 8:slot * 8 + 8], bcol[:, :])
                        e.tensor_copy(xlnf[:, slot * 1024:(slot + 1) * 1024],
                                      ps[:, 0:1024])

                with nc.Block(no_gpsimd_drain=True) as blk:
                    @blk.vector
                    def _(e):
                        cur = r3(xlnf[:, slot * 1024:(slot + 1) * 1024], 128)
                        e.tensor_copy(bcol[:, :].unsqueeze(2),
                                      cur[:, :, 127:128])
                        e.tensor_tensor(r3(dxf[:, :], 128)[:, :, 1:128],
                                        cur[:, :, 1:128], cur[:, :, 0:127],
                                        ALU.subtract)
                        e.tensor_tensor(r3(dxf[:, :], 128)[:, :, 0:1],
                                        cur[:, :, 0:1],
                                        bhist[:, slot * 8:slot * 8 + 8]
                                        .unsqueeze(2), ALU.subtract)

                if KDBG and n == 16:
                    with nc.Block(no_gpsimd_drain=True) as blk:
                        @blk.sync
                        def _(e):
                            dbg_dump(e, "xln_tm", xln_tm[:, :])
                            dbg_dump(e, "dxf", dxf[:, :])
                mix_split(slot, M_K)
                if KDBG and n == 16:
                    with nc.Block(no_gpsimd_drain=True) as blk:
                        @blk.sync
                        def _(e):
                            dbg_dump(e, "xmk", s3[:, :])
                proj_tm(wA_hi, wA_lo, 0)
                with nc.Block(no_gpsimd_drain=True) as blk:
                    @blk.vector
                    def _(e):
                        e.tensor_copy(s1[:, :], ps[:, 0:1024])

                if KDBG and n == 16:
                    with nc.Block(no_gpsimd_drain=True) as blk:
                        @blk.sync
                        def _(e):
                            dbg_dump(e, "k_tm", s1[:, :])

                with nc.Block(no_gpsimd_drain=True) as blk:
                    @blk.vector
                    def _(e):
                        e.tensor_tensor(r3(s2[:, :], 64), r3(s1[:, :], 64),
                                        rep3(wk_t[:, :], 64), ALU.mult)
                        e.tensor_copy(kwk_hi[:, :], s2[:, :])
                        e.tensor_tensor(kwk_lo[:, :], s2[:, :], kwk_hi[:, :],
                                        ALU.subtract)

                if not is_ctx:
                    with nc.Block(no_gpsimd_drain=True) as blk:
                        @blk.tensor
                        def _(e):
                            for h in range(H):
                                e.transpose(ps[0:64, 1024 + h * 128:
                                               1024 + (h + 1) * 128],
                                            s1[:, h * 64:(h + 1) * 64],
                                            ident[:, :])

                    with nc.Block(no_gpsimd_drain=True) as blk:
                        @blk.vector
                        def _(e):
                            o = slot * 2048
                            e.tensor_copy(kfm_hi[0:64, o:o + 2048],
                                          ps[0:64, 1024:3072])
                            e.tensor_tensor(kfm_lo[0:64, o:o + 2048],
                                            ps[0:64, 1024:3072],
                                            kfm_hi[0:64, o:o + 2048],
                                            ALU.subtract)

                mix_split(slot, M_V)
                proj_tm(wB_hi, wB_lo, 4)
                with nc.Block(no_gpsimd_drain=True) as blk:
                    @blk.vector
                    def _(e):
                        o = slot * 1024
                        e.tensor_copy(vtm_hi[:, o:o + 1024], ps[:, 2048:3072])
                        e.tensor_tensor(vtm_lo[:, o:o + 1024], ps[:, 2048:3072],
                                        vtm_hi[:, o:o + 1024], ALU.subtract)

                if KDBG and n == 16:
                    with nc.Block(no_gpsimd_drain=True) as blk:
                        @blk.vector
                        def _(e):
                            o = slot * 1024
                            e.tensor_tensor(dbgst[:, 0:1024], vtm_hi[:, o:o + 1024],
                                            vtm_lo[:, o:o + 1024], ALU.add)

                    with nc.Block(no_gpsimd_drain=True) as blk:
                        @blk.sync
                        def _(e):
                            dbg_dump(e, "v_tm", dbgst[:, 0:1024])
                if not is_ctx:
                    with nc.Block(no_gpsimd_drain=True) as blk:
                        @blk.vector
                        def _(e):
                            o = slot * 1024
                            e.tensor_copy(ssn_hi[0:64, o:o + 1024], s_hi[0:64, :])
                            e.tensor_copy(ssn_lo[0:64, o:o + 1024], s_lo[0:64, :])

                with nc.Block(no_gpsimd_drain=True) as blk:
                    @blk.tensor
                    def _(e):
                        for h in range(H):
                            o = slot * 1024 + h * 64
                            outsl = ps[0:64, 6 * 512 + h * 64:
                                       6 * 512 + (h + 1) * 64]
                            e.matmul(outsl, kwk_hi[:, h * 64:h * 64 + 64],
                                     vtm_hi[:, o:o + 64],
                                     start=(h % 8 == 0), stop=False)
                            e.matmul(outsl, kwk_hi[:, h * 64:h * 64 + 64],
                                     vtm_lo[:, o:o + 64], start=False,
                                     stop=False)
                            e.matmul(outsl, kwk_lo[:, h * 64:h * 64 + 64],
                                     vtm_hi[:, o:o + 64], start=False,
                                     stop=(h % 8 == 7))

                with nc.Block(no_gpsimd_drain=True) as blk:
                    @blk.vector
                    def _(e):
                        e.tensor_tensor(
                            s_f32[0:64, :].rearrange("p (n b) -> p n b", b=64),
                            s_f32[0:64, :].rearrange("p (n b) -> p n b", b=64),
                            ws_p[0:64, :].unsqueeze(2)
                            .broadcast_to([64, 16, 64]), ALU.mult)
                        e.tensor_tensor(s_f32[0:64, :], s_f32[0:64, :],
                                        ps[0:64, 6 * 512:8 * 512], ALU.add)
                        e.tensor_copy(s_hi[0:64, :], s_f32[0:64, :])
                        e.tensor_tensor(s_lo[0:64, :], s_f32[0:64, :],
                                        s_hi[0:64, :], ALU.subtract)

                if KDBG and n == 16:
                    with nc.Block(no_gpsimd_drain=True) as blk:
                        @blk.sync
                        def _(e):
                            dbg_dump(e, "s_next", s_f32[0:64, :])

            def chunk_part2(n, slot):
                with nc.Block(no_gpsimd_drain=True) as blk:
                    @blk.vector
                    def _(e):
                        cur = r3(xlnf[:, slot * 1024:(slot + 1) * 1024], 128)
                        e.tensor_tensor(r3(dxf[:, :], 128)[:, :, 1:128],
                                        cur[:, :, 1:128], cur[:, :, 0:127],
                                        ALU.subtract)
                        e.tensor_tensor(r3(dxf[:, :], 128)[:, :, 0:1],
                                        cur[:, :, 0:1],
                                        bhist[:, slot * 8:slot * 8 + 8]
                                        .unsqueeze(2), ALU.subtract)

                mix_split(slot, M_R)
                proj_tm(wA_hi, wA_lo, 0)
                with nc.Block(no_gpsimd_drain=True) as blk:
                    @blk.vector
                    def _(e):
                        e.tensor_copy(s1[:, :], ps[:, 0:1024])

                with nc.Block(no_gpsimd_drain=True) as blk:
                    @blk.tensor
                    def _(e):
                        for h in range(H):
                            e.transpose(ps[0:64, 1024 + h * 128:
                                           1024 + (h + 1) * 128],
                                        s1[:, h * 64:(h + 1) * 64], ident[:, :])

                with nc.Block(no_gpsimd_drain=True) as blk:
                    @blk.vector
                    def _(e):
                        e.tensor_copy(rfm_hi[0:64, :], ps[0:64, 1024:3072])
                        e.tensor_tensor(rfm_lo[0:64, :], ps[0:64, 1024:3072],
                                        rfm_hi[0:64, :], ALU.subtract)

                mix_split(slot, M_G)
                proj_tm(wB_hi, wB_lo, 4)
                with nc.Block(no_gpsimd_drain=True) as blk:
                    @blk.scalar
                    def _(e):
                        e.activation(s3[:, :], ps[:, 2048:3072], AF.Silu)

                if KDBG and n == 16:
                    with nc.Block(no_gpsimd_drain=True) as blk:
                        @blk.sync
                        def _(e):
                            dbg_dump(e, "g_tm", s3[:, :])
                for hh in range(2):
                    with nc.Block(no_gpsimd_drain=True) as blk:
                        @blk.tensor
                        def _(e, hh=hh):
                            for hi_ in range(8):
                                h = hh * 8 + hi_
                                ko = slot * 2048 + h * 128
                                ro = h * 128
                                out = ps[:, hi_ * 128:(hi_ + 1) * 128]
                                e.matmul(out, kfm_hi[0:64, ko:ko + 128],
                                         rfm_hi[0:64, ro:ro + 128],
                                         start=(hi_ % 4 == 0), stop=False)
                                e.matmul(out, kfm_hi[0:64, ko:ko + 128],
                                         rfm_lo[0:64, ro:ro + 128],
                                         start=False, stop=False)
                                e.matmul(out, kfm_lo[0:64, ko:ko + 128],
                                         rfm_hi[0:64, ro:ro + 128],
                                         start=False, stop=(hi_ % 4 == 3))

                    with nc.Block(no_gpsimd_drain=True) as blk:
                        @blk.vector
                        def _(e, hh=hh):
                            e.tensor_tensor(sbig[:, :], ps[:, 0:1024],
                                            wmT[:, hh * 1024:(hh + 1) * 1024],
                                            ALU.mult)
                            e.tensor_copy(atw_hi[:, :], sbig[:, :])
                            e.tensor_tensor(atw_lo[:, :], sbig[:, :],
                                            atw_hi[:, :], ALU.subtract)

                    if KDBG and n == 16 and hh == 0:
                        with nc.Block(no_gpsimd_drain=True) as blk:
                            @blk.sync
                            def _(e):
                                dbg_dump(e, "attw0", sbig[:, :])

                    with nc.Block(no_gpsimd_drain=True) as blk:
                        @blk.tensor
                        def _(e, hh=hh):
                            for hi_ in range(8):
                                h = hh * 8 + hi_
                                vo = slot * 1024 + h * 64
                                ro = h * 128
                                so = slot * 1024 + h * 64
                                o1 = ps[:, 2 * 512 + h * 64:2 * 512 + h * 64 + 64]
                                e.matmul(o1, atw_hi[:, hi_ * 128:(hi_ + 1) * 128],
                                         vtm_hi[:, vo:vo + 64],
                                         start=(hi_ == 0), stop=False)
                                e.matmul(o1, atw_hi[:, hi_ * 128:(hi_ + 1) * 128],
                                         vtm_lo[:, vo:vo + 64], start=False,
                                         stop=False)
                                e.matmul(o1, atw_lo[:, hi_ * 128:(hi_ + 1) * 128],
                                         vtm_hi[:, vo:vo + 64], start=False,
                                         stop=(hi_ == 7))
                                o2 = ps[:, 4 * 512 + h * 64:4 * 512 + h * 64 + 64]
                                e.matmul(o2, rfm_hi[0:64, ro:ro + 128],
                                         ssn_hi[0:64, so:so + 64],
                                         start=(hi_ == 0), stop=False)
                                e.matmul(o2, rfm_hi[0:64, ro:ro + 128],
                                         ssn_lo[0:64, so:so + 64],
                                         start=False, stop=False)
                                e.matmul(o2, rfm_lo[0:64, ro:ro + 128],
                                         ssn_hi[0:64, so:so + 64],
                                         start=False, stop=(hi_ == 7))

                with nc.Block(no_gpsimd_drain=True) as blk:
                    @blk.vector
                    def _(e):
                        e.tensor_tensor(r3(s1[:, :], 64),
                                        r3(ps[:, 4 * 512:6 * 512], 64),
                                        rep3(wb_t[:, :], 64), ALU.mult)
                        e.tensor_tensor(s1[:, :], s1[:, :],
                                        ps[:, 2 * 512:4 * 512], ALU.add)

                if KDBG and n == 16:
                    with nc.Block(no_gpsimd_drain=True) as blk:
                        @blk.sync
                        def _(e):
                            dbg_dump(e, "y_raw", s1[:, :])

                with nc.Block(no_gpsimd_drain=True) as blk:
                    @blk.vector
                    def _(e):
                        e.tensor_tensor(s2[:, :], s1[:, :], s1[:, :], ALU.mult)
                        e.reduce_sum(csc[:, 0:16], r3(s1[:, :], 64), axis=AX.X)
                        vfence(e, e.reduce_sum(csc[:, 16:32], r3(s2[:, :], 64),
                                               axis=AX.X))
                        vfence(e, e.tensor_scalar(csc[:, 0:16], csc[:, 0:16],
                                                   1.0 / 64, None, ALU.mult))
                        vfence(e, e.tensor_tensor(csc[:, 32:48], csc[:, 0:16],
                                                  csc[:, 0:16], ALU.mult))
                        e.scalar_tensor_tensor(csc[:, 16:32], csc[:, 16:32],
                                               1.0 / 64, csc[:, 32:48],
                                               ALU.mult, ALU.subtract)

                with nc.Block(no_gpsimd_drain=True) as blk:
                    @blk.scalar
                    def _(e):
                        e.activation(csc[:, 16:32], csc[:, 16:32], AF.Sqrt,
                                     bias=epsc[:, 1:2])

                with nc.Block(no_gpsimd_drain=True) as blk:
                    @blk.vector
                    def _(e):
                        vfence(e, e.reciprocal(csc[:, 16:32], csc[:, 16:32]))
                        e.tensor_tensor(r3(s1[:, :], 64), r3(s1[:, :], 64),
                                        rep3(csc[:, 0:16], 64), ALU.subtract)
                        e.tensor_tensor(r3(s1[:, :], 64), r3(s1[:, :], 64),
                                        rep3(csc[:, 16:32], 64), ALU.mult)
                        e.tensor_tensor(s1[:, :], s1[:, :], lnw_tm[:, :],
                                        ALU.mult)
                        e.tensor_tensor(s1[:, :], s1[:, :], lnb_tm[:, :],
                                        ALU.add)
                        e.tensor_tensor(s1[:, :], s1[:, :], s3[:, :], ALU.mult)

                if KDBG and n == 16:
                    with nc.Block(no_gpsimd_drain=True) as blk:
                        @blk.sync
                        def _(e):
                            dbg_dump(e, "yg_tm", s1[:, :])

                transpose8(s1, 0)
                with nc.Block(no_gpsimd_drain=True) as blk:
                    @blk.vector
                    def _(e):
                        o = slot * 1024
                        e.tensor_copy(yg_hi[:, o:o + 1024], ps[:, 0:1024])
                        e.tensor_tensor(yg_lo[:, o:o + 1024], ps[:, 0:1024],
                                        yg_hi[:, o:o + 1024], ALU.subtract)

            def chunk_part3(n, slot):
                tok0 = n * T
                own_t0 = (n - OWN0) * T
                with nc.Block(no_gpsimd_drain=True) as blk:
                    @blk.tensor
                    def _(e):
                        for half in range(2):
                            out = bank(half)
                            first = True
                            for cc in range(NCC):
                                o = slot * 1024 + cc * 128
                                woff = cc * 1024 + half * 512
                                e.matmul(out, yg_hi[:, o:o + 128],
                                         wA_hi[:, woff:woff + 512],
                                         start=first, stop=False)
                                first = False
                                e.matmul(out, yg_hi[:, o:o + 128],
                                         wA_lo[:, woff:woff + 512],
                                         start=False, stop=False)
                                e.matmul(out, yg_lo[:, o:o + 128],
                                         wA_hi[:, woff:woff + 512],
                                         start=False, stop=(cc == NCC - 1))

                    @blk.sync
                    def _(e):
                        dma(e, x_tm[:, :], xin[tok0:tok0 + T, :], "sx")

                with nc.Block(no_gpsimd_drain=True) as blk:
                    @blk.vector
                    def _(e):
                        e.tensor_copy(s1[:, :], ps[:, 0:1024])

                if KDBG and n == 16:
                    with nc.Block(no_gpsimd_drain=True) as blk:
                        @blk.sync
                        def _(e):
                            dbg_dump(e, "attnout", s1[:, :])

                ln_plain(s1[:, :], s2[:, :], s3[:, :])
                with nc.Block(no_gpsimd_drain=True) as blk:
                    @blk.vector
                    def _(e):
                        wt(e, "sx")
                        e.tensor_tensor(s1[:, :], x_tm[:, :], s2[:, :], ALU.add)

                with nc.Block(no_gpsimd_drain=True) as blk:
                    @blk.sync
                    def _(e):
                        dma(e, x1sp[own_t0:own_t0 + T, :], s1[:, :], "ssp")

            def partial_out():
                for j in range(16):
                    with nc.Block(no_gpsimd_drain=True) as blk:
                        @blk.sync
                        def _(e, j=j):
                            if KPART in ("attn",):
                                wt(e, "ssp")
                                dma(e, x_tm[:, :], x1sp[j * T:(j + 1) * T, :],
                                    "sx")
                            else:
                                dma(e, x_tm[:, :],
                                    xin[OWNTOK + j * T:OWNTOK + (j + 1) * T, :],
                                    "sx")

                    with nc.Block(no_gpsimd_drain=True) as blk:
                        @blk.sync
                        def _(e, j=j):
                            wt(e, "sx")
                            dma(e, xout[j * T:(j + 1) * T, :], x_tm[:, :],
                                "sout")

                with nc.Block(no_gpsimd_drain=True) as blk:
                    @blk.sync
                    def _(e):
                        wt(e, "sout")

            if KPART == "setup":
                partial_out()
                return nc
            load_wsplit(wA_hi, wA_lo, wT["wk"])
            load_wsplit(wB_hi, wB_lo, wT["wv"])
            for n in range(0, KCTX):
                chunk_part1(n, n % 2, is_ctx=True)
            if KPART == "ctx":
                partial_out()
                return nc
            for g in range(KGRP):
                n0 = OWN0 + 2 * g
                if g > 0:
                    load_wsplit(wA_hi, wA_lo, wT["wk"])
                    load_wsplit(wB_hi, wB_lo, wT["wv"])
                for i in range(2):
                    chunk_part1(n0 + i, i, is_ctx=False)
                if KPART == "ga":
                    partial_out()
                    return nc
                load_wsplit(wA_hi, wA_lo, wT["wr"])
                load_wsplit(wB_hi, wB_lo, wT["wg"])
                for i in range(2):
                    chunk_part2(n0 + i, i)
                if KPART == "gb":
                    partial_out()
                    return nc
                load_wsplit(wA_hi, wA_lo, wT["wo"])
                for i in range(2):
                    chunk_part3(n0 + i, i)
            if KPART == "attn":
                partial_out()
                return nc

        with ExitStack() as _f:
            h_hi = _f.enter_context(nc.sbuf_tensor("h_hi", [128, 8 * 2048], BF16))
            h_lo = _f.enter_context(nc.sbuf_tensor("h_lo", [128, 8 * 2048], BF16))
            rw_hi = _f.enter_context(nc.sbuf_tensor("rw_hi", [128, 64], BF16))
            rw_lo = _f.enter_context(nc.sbuf_tensor("rw_lo", [128, 64], BF16))
            f1 = _f.enter_context(nc.sbuf_tensor("f1", [128, 2048], F32))
            f2 = _f.enter_context(nc.sbuf_tensor("f2", [128, 1024], F32))
            f3 = _f.enter_context(nc.sbuf_tensor("f3", [128, 1024], F32))
            f4 = _f.enter_context(nc.sbuf_tensor("f4", [128, 1024], F32))
            hv_hi = h_hi[:, :].rearrange("p (cc x) -> p cc x", x=2048)
            hv_lo = h_lo[:, :].rearrange("p (cc x) -> p cc x", x=2048)

            with ExitStack() as _h:
                gamb = _h.enter_context(nc.sbuf_tensor("gamb", [128, 1024], F32))
                betb = _h.enter_context(nc.sbuf_tensor("betb", [128, 1024], F32))
                rowf = _h.enter_context(nc.sbuf_tensor("rowf", [1, 2048], F32))
                with nc.Block(no_gpsimd_drain=True) as blk:
                    @blk.sync
                    def _(e):
                        dma(e, rowf[0:1, :], gbrow_d[0:1, :], "sw")
                        dma(e, r3(f2[:, 0:64], 8),
                            rwT_d[:, :].rearrange("(cc p) x -> p cc x", p=128),
                            "sw")

                with nc.Block(no_gpsimd_drain=True) as blk:
                    @blk.vector
                    def _(e):
                        wt(e, "sw")
                        e.tensor_copy(rw_hi[:, :], f2[:, 0:64])
                        e.tensor_tensor(rw_lo[:, :], f2[:, 0:64], rw_hi[:, :],
                                        ALU.subtract)

                bcast_row(rowf[0:1, 0:1024], gamb[:, :], 1024)
                bcast_row(rowf[0:1, 1024:2048], betb[:, :], 1024)

                with nc.Block(no_gpsimd_drain=True) as blk:
                    @blk.sync
                    def _(e):
                        wt(e, "ssp")

                for j in range(16):
                    with nc.Block(no_gpsimd_drain=True) as blk:
                        @blk.sync
                        def _(e, j=j):
                            dma(e, f1[:, 0:1024], x1sp[j * T:(j + 1) * T, :], "sx")

                    with nc.Block(no_gpsimd_drain=True) as blk:
                        @blk.vector
                        def _(e):
                            wt(e, "sx")

                    ln_plain(f1[:, 0:1024], f2[:, :], f3[:, :])
                    ln_plain(f2[:, :], f2[:, :], f3[:, :])
                    with nc.Block(no_gpsimd_drain=True) as blk:
                        @blk.vector
                        def _(e):
                            e.tensor_tensor(f2[:, :], f2[:, :], gamb[:, :],
                                            ALU.mult)

                    with nc.Block(no_gpsimd_drain=True) as blk:
                        @blk.vector
                        def _(e):
                            e.tensor_tensor(f2[:, :], f2[:, :], betb[:, :],
                                            ALU.add)

                    if KDBG and j == 0:
                        with nc.Block(no_gpsimd_drain=True) as blk:
                            @blk.sync
                            def _(e):
                                dbg_dump(e, "h_tm", f2[:, :])

                    transpose8(f2, 0)
                    with nc.Block(no_gpsimd_drain=True) as blk:
                        @blk.vector
                        def _(e, j=j):
                            dsth = hv_hi[:, :, j * 128:(j + 1) * 128]
                            dstl = hv_lo[:, :, j * 128:(j + 1) * 128]
                            e.tensor_copy(dsth, r3(ps[:, 0:1024], 128))
                            e.tensor_tensor(dstl, r3(ps[:, 0:1024], 128), dsth,
                                            ALU.subtract)

                    with nc.Block(no_gpsimd_drain=True) as blk:
                        @blk.tensor
                        def _(e, j=j):
                            out = ps[:, 4 * 512:4 * 512 + 8]
                            for cc in range(NCC):
                                a_hi = hv_hi[:, cc, j * 128:(j + 1) * 128]
                                a_lo = hv_lo[:, cc, j * 128:(j + 1) * 128]
                                e.matmul(out, a_hi, rw_hi[:, cc * 8:cc * 8 + 8],
                                         start=(cc == 0), stop=False)
                                e.matmul(out, a_hi, rw_lo[:, cc * 8:cc * 8 + 8],
                                         start=False, stop=False)
                                e.matmul(out, a_lo, rw_hi[:, cc * 8:cc * 8 + 8],
                                         start=False, stop=(cc == NCC - 1))

                    with nc.Block(no_gpsimd_drain=True) as blk:
                        @blk.vector
                        def _(e):
                            e.reduce_max(stat[:, 0:1],
                                         ps[:, 4 * 512:4 * 512 + 8],
                                         axis=AX.X, negate=True)

                    with nc.Block(no_gpsimd_drain=True) as blk:
                        @blk.scalar
                        def _(e):
                            e.activation(f3[:, 0:8], ps[:, 4 * 512:4 * 512 + 8],
                                         AF.Exp, bias=stat[:, 0:1],
                                         accum_out=stat[:, 1:2])

                    with nc.Block(no_gpsimd_drain=True) as blk:
                        @blk.vector
                        def _(e):
                            e.reduce_max(stat[:, 2:3], f3[:, 0:8], axis=AX.X)
                            e.reciprocal(stat[:, 3:4], stat[:, 1:2])

                    with nc.Block(no_gpsimd_drain=True) as blk:
                        @blk.vector
                        def _(e):
                            e.tensor_tensor(stat[:, 4:5], stat[:, 2:3],
                                            stat[:, 3:4], ALU.mult)
                            e.tensor_scalar(f3[:, 8:16], f3[:, 0:8],
                                            stat[:, 2:3], None, ALU.is_ge)

                    with nc.Block(no_gpsimd_drain=True) as blk:
                        @blk.vector
                        def _(e, j=j):
                            e.tensor_scalar(gates[:, j * 8:(j + 1) * 8],
                                            f3[:, 8:16], stat[:, 4:5], None,
                                            ALU.mult)


            if KDBG:
                with nc.Block(no_gpsimd_drain=True) as blk:
                    @blk.sync
                    def _(e):
                        dbg_dump(e, "gates", gates[:, :])

            with ExitStack() as _e2:
                wstf = _e2.enter_context(nc.sbuf_tensor("wstf", [128, 2048], F32))
                thrb = _e2.enter_context(nc.sbuf_tensor("thrb", [128, 2048], F32))
                nthrb = _e2.enter_context(nc.sbuf_tensor("nthrb", [128, 2048], F32))
                fsc = _e2.enter_context(nc.sbuf_tensor("fsc", [128, 2048], F32))
                q1 = _e2.enter_context(nc.sbuf_tensor("q1", [128, 8 * 2048], FP8))
                q2 = _e2.enter_context(nc.sbuf_tensor("q2", [128, 16 * 1024], FP8))
                a1_hi = _e2.enter_context(nc.sbuf_tensor("a1_hi", [128, 16 * 512], BF16))
                a1_lo = _e2.enter_context(nc.sbuf_tensor("a1_lo", [128, 16 * 512], BF16))
                accb = _e2.enter_context(nc.sbuf_tensor("accb", [128, 8 * 512], F32))
                scl1c = _e2.enter_context(nc.sbuf_tensor("scl1c", [128, 16], F32))
                scl2c = _e2.enter_context(nc.sbuf_tensor("scl2c", [128, 8], F32))
                sclrow = _e2.enter_context(nc.sbuf_tensor("sclrow", [1, 2048], F32))
                grow = sclrow
                def quant_weight(dram2d, nchunks, width, qdst, sclc, scl_n):
                    nseg = width // 512
                    for ch in range(nchunks):
                        with nc.Block(no_gpsimd_drain=True) as blk:
                            @blk.sync
                            def _(e, ch=ch):
                                dma(e, wstf[:, 0:width],
                                    dram2d[ch * 128:(ch + 1) * 128, :], "sw")

                        with nc.Block(no_gpsimd_drain=True) as blk:
                            @blk.scalar
                            def _(e):
                                wt(e, "sw")
                                e.activation(fsc[:, 0:width], wstf[:, 0:width],
                                             AF.Abs)

                        with nc.Block(no_gpsimd_drain=True) as blk:
                            @blk.tensor
                            def _(e, ch=ch):
                                for sgi in range(nseg):
                                    e.matmul(ps[0:1, 2048 + sgi * 512:
                                                2048 + sgi * 512 + 512],
                                             onescol[:, 0:1],
                                             fsc[:, sgi * 512:(sgi + 1) * 512],
                                             start=(ch == 0),
                                             stop=(ch == nchunks - 1))

                    with nc.Block(no_gpsimd_drain=True) as blk:
                        @blk.vector
                        def _(e):
                            e.tensor_scalar(sclrow[0:1, 0:width],
                                            ps[0:1, 2048:2048 + width],
                                            1.0 / (nchunks * 128), 1e-5,
                                            ALU.mult, ALU.max)

                    with nc.Block(no_gpsimd_drain=True) as blk:
                        @blk.tensor
                        def _(e):
                            for q in range(scl_n):
                                e.matmul(ps[:, 2048 + q:2048 + q + 1],
                                         sclrow[0:1, q * 128:(q + 1) * 128],
                                         ones_r[0:1, 0:1], start=True, stop=True)

                    with nc.Block(no_gpsimd_drain=True) as blk:
                        @blk.vector
                        def _(e):
                            e.tensor_copy(sclc[:, 0:scl_n],
                                          ps[:, 2048:2048 + scl_n])

                    with nc.Block(no_gpsimd_drain=True) as blk:
                        @blk.tensor
                        def _(e):
                            for sgi in range(nseg):
                                e.matmul(ps[:, sgi * 512:(sgi + 1) * 512],
                                         ones_r[0:1, :],
                                         sclrow[0:1, sgi * 512:(sgi + 1) * 512],
                                         start=True, stop=True)

                    with nc.Block(no_gpsimd_drain=True) as blk:
                        @blk.vector
                        def _(e):
                            e.tensor_scalar(thrb[:, 0:width], ps[:, 0:width],
                                            0.5, None, ALU.mult)

                    with nc.Block(no_gpsimd_drain=True) as blk:
                        @blk.vector
                        def _(e):
                            e.tensor_scalar(nthrb[:, 0:width], thrb[:, 0:width],
                                            -1.0, None, ALU.mult)

                    for ch in range(nchunks):
                        with nc.Block(no_gpsimd_drain=True) as blk:
                            @blk.sync
                            def _(e, ch=ch):
                                dma(e, wstf[:, 0:width],
                                    dram2d[ch * 128:(ch + 1) * 128, :], "sw")

                        with nc.Block(no_gpsimd_drain=True) as blk:
                            @blk.vector
                            def _(e):
                                wt(e, "sw")
                                e.tensor_tensor(fsc[:, 0:width], wstf[:, 0:width],
                                                thrb[:, 0:width], ALU.is_ge)

                        with nc.Block(no_gpsimd_drain=True) as blk:
                            @blk.vector
                            def _(e):
                                e.tensor_tensor(wstf[:, 0:width], wstf[:, 0:width],
                                                nthrb[:, 0:width], ALU.is_le)

                        with nc.Block(no_gpsimd_drain=True) as blk:
                            @blk.vector
                            def _(e, ch=ch):
                                e.tensor_tensor(
                                    qdst[:, ch * width:(ch + 1) * width],
                                    fsc[:, 0:width], wstf[:, 0:width],
                                    ALU.subtract)

                for eidx in list(range(8))[:max(0, KEXP - 1)] + [8]:
                    shared = (eidx == 8)
                    w1d = sw1T if shared else ew1T[eidx, :, :]
                    w2d = sw2T if shared else ew2T[eidx, :, :]
                    quant_weight(w1d, NCC, 2048, q1, scl1c, 16)
                    quant_weight(w2d, 16, 1024, q2, scl2c, 8)

                    if not shared:
                        with nc.Block(no_gpsimd_drain=True) as blk:
                            @blk.tensor
                            def _(e, eidx=eidx):
                                for j in range(16):
                                    e.transpose(ps[0:1, j * 128:(j + 1) * 128],
                                                gates[:, j * 8 + eidx:
                                                      j * 8 + eidx + 1],
                                                ident[:, :])

                        with nc.Block(no_gpsimd_drain=True) as blk:
                            @blk.vector
                            def _(e):
                                e.tensor_copy(grow[0:1, :], ps[0:1, 0:2048])

                    if KDBG and eidx == 0:
                        with nc.Block(no_gpsimd_drain=True) as blk:
                            @blk.vector
                            def _(e):
                                e.tensor_copy(f1[:, 0:16], scl1c[:, :])
                                e.tensor_copy(f1[:, 16:24], scl2c[:, :])
                                e.tensor_copy(f2[:, 0:1024], q1[:, 0:1024])

                        with nc.Block(no_gpsimd_drain=True) as blk:
                            @blk.sync
                            def _(e):
                                dbg_dump(e, "scls", f1[:, 0:24])
                                dbg_dump(e, "q1c0", f2[:, 0:1024])

                    for tp in range(NTAU):
                        done_sac = cnt["sac"]
                        with nc.Block(no_gpsimd_drain=True) as blk:
                            @blk.sync
                            def _(e, tp=tp, eidx=eidx):
                                e.wait_ge(sems["sac"], done_sac)
                                if eidx > 0:
                                    dma(e, accb[:, :],
                                        accd[:, tp * 4096:(tp + 1) * 4096],
                                        "sx")

                            if not shared:
                                @blk.tensor
                                def _(e, tp=tp):
                                    e.matmul(ps[:, 7 * 512:8 * 512],
                                             ones_r[0:1, :],
                                             grow[0:1, tp * TAU:(tp + 1) * TAU],
                                             start=True, stop=True)
                        acc_ready = cnt["sx"]

                        def silu_pair(e, fcp):
                            b0 = (fcp % 2) * 2
                            for k2 in range(2):
                                fc = fcp * 2 + k2
                                e.activation(
                                    f1[:, (fcp % 2) * 1024 + k2 * 512:
                                       (fcp % 2) * 1024 + (k2 + 1) * 512],
                                    ps[:, (b0 + k2) * 512:(b0 + k2 + 1) * 512],
                                    AF.Silu, scale=scl1c[:, fc:fc + 1])

                        def gate_split_pair(e, fcp, shared=shared):
                            fb = (fcp % 2) * 1024
                            srcap = f1[:, fb:fb + 1024]
                            if not shared:
                                e.tensor_tensor(
                                    r3(f2[:, 0:1024], 512), r3(srcap, 512),
                                    ps[:, 7 * 512:8 * 512].unsqueeze(1)
                                    .broadcast_to([128, 2, 512]), ALU.mult)
                                srcap = f2[:, 0:1024]
                            o = fcp * 1024
                            e.tensor_copy(a1_hi[:, o:o + 1024], srcap)
                            e.tensor_tensor(a1_lo[:, o:o + 1024], srcap,
                                            a1_hi[:, o:o + 1024], ALU.subtract)

                        def mm1_pair(e, fcp, tp):
                            b0 = (fcp % 2) * 2
                            for k2 in range(2):
                                fc = fcp * 2 + k2
                                out = ps[:, (b0 + k2) * 512:(b0 + k2 + 1) * 512]
                                for cc in range(NCC):
                                    qsl = q1[:, cc * 2048 + fc * 128:
                                             cc * 2048 + (fc + 1) * 128]
                                    e.matmul(out, qsl,
                                             hv_hi[:, cc, tp * 512:(tp + 1) * 512],
                                             start=(cc == 0), stop=False)
                                    e.matmul(out, qsl,
                                             hv_lo[:, cc, tp * 512:(tp + 1) * 512],
                                             start=False, stop=(cc == NCC - 1))

                        for fcp in range(10):
                            with nc.Block(no_gpsimd_drain=True) as blk:
                                if fcp < 8:
                                    @blk.tensor
                                    def _(e, fcp=fcp, tp=tp):
                                        mm1_pair(e, fcp, tp)
                                if 0 <= fcp - 1 < 8:
                                    @blk.scalar
                                    def _(e, fcp=fcp):
                                        silu_pair(e, fcp - 1)
                                if 0 <= fcp - 2 < 8:
                                    @blk.vector
                                    def _(e, fcp=fcp):
                                        gate_split_pair(e, fcp - 2)

                        def mm2_pair(e, dcp, tp):
                            b0 = 2 + (dcp % 2) * 2
                            for k2 in range(2):
                                dc = dcp * 2 + k2
                                out = ps[:, (b0 + k2) * 512:(b0 + k2 + 1) * 512]
                                for fc in range(16):
                                    qsl = q2[:, fc * 1024 + dc * 128:
                                             fc * 1024 + (dc + 1) * 128]
                                    e.matmul(out, qsl,
                                             a1_hi[:, fc * 512:(fc + 1) * 512],
                                             start=(fc == 0), stop=False)
                                    e.matmul(out, qsl,
                                             a1_lo[:, fc * 512:(fc + 1) * 512],
                                             start=False, stop=(fc == 15))

                        def acc_pair(e, dcp, eidx=eidx, acc_ready=acc_ready):
                            if eidx > 0:
                                e.wait_ge(sems["sx"], acc_ready)
                            b0 = 2 + (dcp % 2) * 2
                            for k2 in range(2):
                                dc = dcp * 2 + k2
                                psl = ps[:, (b0 + k2) * 512:(b0 + k2 + 1) * 512]
                                asl = accb[:, dc * 512:(dc + 1) * 512]
                                if eidx == 0:
                                    e.tensor_scalar(asl, psl,
                                                    scl2c[:, dc:dc + 1],
                                                    None, ALU.mult)
                                else:
                                    e.scalar_tensor_tensor(
                                        asl, psl, scl2c[:, dc:dc + 1], asl,
                                        ALU.mult, ALU.add)

                        for dcp in range(5):
                            with nc.Block(no_gpsimd_drain=True) as blk:
                                if dcp < 4:
                                    @blk.tensor
                                    def _(e, dcp=dcp, tp=tp):
                                        mm2_pair(e, dcp, tp)
                                if dcp >= 1:
                                    @blk.vector
                                    def _(e, dcp=dcp):
                                        acc_pair(e, dcp - 1)

                        with nc.Block(no_gpsimd_drain=True) as blk:
                            @blk.sync
                            def _(e, tp=tp):
                                dma(e, accd[:, tp * 4096:(tp + 1) * 4096],
                                    accb[:, :], "sac")

            with nc.Block(no_gpsimd_drain=True) as blk:
                @blk.sync
                def _(e):
                    wt(e, "sac")

            for j in range(16):
                with nc.Block(no_gpsimd_drain=True) as blk:
                    @blk.sync
                    def _(e, j=j):
                        tp, sub = j // 4, j % 4
                        dma(e, r3(f1[:, 0:1024], 128),
                            accd[:, tp * 4096:(tp + 1) * 4096]
                            .rearrange("p (dc x) -> p dc x", x=512)
                            [:, :, sub * 128:(sub + 1) * 128], "sx")
                        dma(e, f2[:, :], x1sp[j * T:(j + 1) * T, :], "sx")

                with nc.Block(no_gpsimd_drain=True) as blk:
                    @blk.vector
                    def _(e):
                        wt(e, "sx")

                with nc.Block(no_gpsimd_drain=True) as blk:
                    @blk.tensor
                    def _(e):
                        for dc in range(NCC):
                            e.transpose(ps[:, dc * 128:(dc + 1) * 128],
                                        f1[:, dc * 128:(dc + 1) * 128],
                                        ident[:, :])

                with nc.Block(no_gpsimd_drain=True) as blk:
                    @blk.vector
                    def _(e):
                        e.tensor_copy(f1[:, 0:1024], ps[:, 0:1024])

                if KDBG and j == 0:
                    with nc.Block(no_gpsimd_drain=True) as blk:
                        @blk.sync
                        def _(e):
                            dbg_dump(e, "ffnacc0", f1[:, 0:1024])

                with nc.Block(no_gpsimd_drain=True) as blk:
                    @blk.sync
                    def _(e):
                        wt(e, "sout")

                ln_plain(f1[:, 0:1024], f3[:, :], f4[:, :])
                with nc.Block(no_gpsimd_drain=True) as blk:
                    @blk.vector
                    def _(e):
                        e.tensor_tensor(f3[:, :], f3[:, :], f2[:, :], ALU.add)

                with nc.Block(no_gpsimd_drain=True) as blk:
                    @blk.sync
                    def _(e, j=j):
                        dma(e, xout[j * T:(j + 1) * T, :], f3[:, :], "sout")

            with nc.Block(no_gpsimd_drain=True) as blk:
                @blk.sync
                def _(e):
                    wt(e, "sout")

    return nc


_NC_CACHE = None


def kernel(**inputs):
    global _NC_CACHE
    if _NC_CACHE is None:
        _NC_CACHE = build_nc()
    nc = _NC_CACHE

    x = np.asarray(inputs["L0_attn_xq"], dtype=np.float32)
    tm = {m: np.asarray(inputs[f"L0_attn_time_mix_{m}"], np.float32).reshape(C)
          for m in ("k", "v", "r", "g")}
    tmix = np.zeros((128, 32), np.float32)
    for mi, m in enumerate(("k", "v", "r", "g")):
        tmix[:, mi * 8:(mi + 1) * 8] = tm[m].reshape(8, 128).T
    tdu = np.concatenate([
        np.asarray(inputs["L0_attn_time_decay"], np.float32),
        np.asarray(inputs["L0_attn_time_faaaa"], np.float32)]).reshape(1, 32)
    lnrow = np.concatenate([
        np.asarray(inputs["L0_attn_ln_w"], np.float32),
        np.asarray(inputs["L0_attn_ln_b"], np.float32)]).reshape(1, 2048)
    gbrow = np.concatenate([
        np.asarray(inputs["L0_ffn_ln_gamma"], np.float32),
        np.asarray(inputs["L0_ffn_ln_beta"], np.float32)]).reshape(1, 2048)

    def t2(a):
        return np.ascontiguousarray(np.asarray(a, np.float32).T)

    com = {
        "wrT": t2(inputs["L0_attn_W_r"]), "wkT": t2(inputs["L0_attn_W_k"]),
        "wvT": t2(inputs["L0_attn_W_v"]), "wgT": t2(inputs["L0_attn_W_g"]),
        "woT": t2(inputs["L0_attn_W_o"]),
        "tmix": tmix, "tdu": tdu, "lnrow": lnrow, "gbrow": gbrow,
        "sw1T": t2(inputs["L0_ffn_shared_w1"]),
        "sw2T": t2(inputs["L0_ffn_shared_w2"]),
        "ew1T": np.ascontiguousarray(
            np.asarray(inputs["L0_ffn_expert_w1"], np.float32)
            .transpose(0, 2, 1)),
        "ew2T": np.ascontiguousarray(
            np.asarray(inputs["L0_ffn_expert_w2"], np.float32)
            .transpose(0, 2, 1)),
        "rwT": t2(inputs["L0_ffn_router_w"]),
        "ident": np.eye(128, dtype=np.float32),
    }

    in_maps = []
    for core in range(8):
        b, half = core // 2, core % 2
        own = x[b, half * OWNTOK:(half + 1) * OWNTOK, :]
        ctx = (x[b, 0:OWNTOK, :] if half == 1
               else np.zeros((OWNTOK, C), np.float32))
        m = dict(com)
        m["xin"] = np.ascontiguousarray(np.concatenate([ctx, own], axis=0))
        in_maps.append(m)

    res = run_bass_kernel_spmd(nc, in_maps, core_ids=list(range(8)))
    out = np.empty((B, TT, C), np.float32)
    for core in range(8):
        b, half = core // 2, core % 2
        out[b, half * OWNTOK:(half + 1) * OWNTOK, :] = res.results[core]["xout"]
    return out


# version 07
# speedup vs baseline: 1.1980x; 1.1980x over previous
import os
import numpy as np
import concourse.bass as bass
import concourse.mybir as mybir
from concourse.bass_utils import run_bass_kernel_spmd

F32 = mybir.dt.float32
BF16 = mybir.dt.bfloat16
FP8 = mybir.dt.float8e4
ALU = mybir.AluOpType
AF = mybir.ActivationFunctionType
AX = mybir.AxisListType

B, TT, C = 4, 4096, 1024
H, T, K = 16, 128, 64
NCC = C // 128
OWN0 = 16
OWNTOK = 2048
FDIM = 2048
GEPS = 64e-5
EPS = 1e-5
TAU = 512
NTAU = OWNTOK // TAU

M_K, M_V, M_R, M_G = 0, 1, 2, 3
KPART = os.environ.get("KPART", "full")
KDBG = int(os.environ.get("KDBG", "0"))
KCTX = int(os.environ.get("KCTX", "16"))
KGRP = int(os.environ.get("KGRP", "8"))
KEXP = int(os.environ.get("KEXP", "9"))


def build_nc():
    nc = bass.Bass("TRN2", target_bir_lowering=False)

    xin = nc.dram_tensor("xin", [TT, C], F32, kind="ExternalInput").ap()
    wT = {}
    for nm in ("wr", "wk", "wv", "wg", "wo"):
        wT[nm] = nc.dram_tensor(nm + "T", [C, C], F32, kind="ExternalInput").ap()
    tmix_d = nc.dram_tensor("tmix", [128, 32], F32, kind="ExternalInput").ap()
    tdu_d = nc.dram_tensor("tdu", [1, 32], F32, kind="ExternalInput").ap()
    lnrow_d = nc.dram_tensor("lnrow", [1, 2048], F32, kind="ExternalInput").ap()
    gbrow_d = nc.dram_tensor("gbrow", [1, 2048], F32, kind="ExternalInput").ap()
    sw1T = nc.dram_tensor("sw1T", [C, FDIM], F32, kind="ExternalInput").ap()
    sw2T = nc.dram_tensor("sw2T", [FDIM, C], F32, kind="ExternalInput").ap()
    ew1T = nc.dram_tensor("ew1T", [8, C, FDIM], F32, kind="ExternalInput").ap()
    ew2T = nc.dram_tensor("ew2T", [8, FDIM, C], F32, kind="ExternalInput").ap()
    rwT_d = nc.dram_tensor("rwT", [C, 8], F32, kind="ExternalInput").ap()
    ident_d = nc.dram_tensor("ident", [128, 128], F32, kind="ExternalInput").ap()
    x1sp = nc.dram_tensor("x1sp", [OWNTOK, C], F32,
                          **({"kind": "ExternalOutput"} if KDBG else {})).ap()
    accd = nc.dram_tensor("accd", [128, 8 * OWNTOK], F32,
                          **({"kind": "ExternalOutput"} if KDBG else {})).ap()
    xout = nc.dram_tensor("xout", [OWNTOK, C], F32, kind="ExternalOutput").ap()

    sems = {}
    cnt = {}
    dbg_outs = {}

    def mkdbg(name, shape):
        dbg_outs[name] = nc.dram_tensor("dbg_" + name, shape, F32,
                                        kind="ExternalOutput").ap()
        return dbg_outs[name]

    def dma(e, out, in_, s):
        e.dma_start(out=out, in_=in_).then_inc(sems[s], 16)
        cnt[s] += 16

    def wt(e, s):
        e.wait_ge(sems[s], cnt[s])

    def dbg_dump(e, name, ap):
        if not KDBG:
            return
        if name not in dbg_outs:
            mkdbg(name, list(ap.shape))
        dma(e, dbg_outs[name][tuple(slice(0, s) for s in ap.shape)], ap, "sout")

    def rep3(ap_cols, blk):
        return ap_cols.unsqueeze(2).broadcast_to([128, ap_cols.shape[1], blk])

    def r3(ap2d, blk):
        return ap2d.rearrange("p (n b) -> p n b", b=blk)

    from contextlib import ExitStack
    with ExitStack() as _g:
        _sw = _g.enter_context(nc.semaphore("sw"))
        _sx = _g.enter_context(nc.semaphore("sx"))
        _ssp = _g.enter_context(nc.semaphore("ssp"))
        _sac = _g.enter_context(nc.semaphore("sac"))
        _sout = _g.enter_context(nc.semaphore("sout"))
        _svf = _g.enter_context(nc.semaphore("svf"))
        ident = _g.enter_context(nc.sbuf_tensor("ident_s", [128, 128], F32))
        ones_r = _g.enter_context(nc.sbuf_tensor("ones_r", [1, 128], F32))
        onescol = _g.enter_context(nc.sbuf_tensor("onescol", [128, 1], F32))
        tmv = _g.enter_context(nc.sbuf_tensor("tmv", [128, 32], F32))
        gates = _g.enter_context(nc.sbuf_tensor("gates", [128, 128], F32))
        stat = _g.enter_context(nc.sbuf_tensor("stat", [128, 12], F32))
        epsc = _g.enter_context(nc.sbuf_tensor("epsc", [128, 2], F32))
        ps = _g.enter_context(nc.psum_tensor("ps", [128, 4096], F32))
        sems.update(sw=_sw, sx=_sx, ssp=_ssp, sac=_sac, sout=_sout, vf=_svf)
        cnt.update(sw=0, sx=0, ssp=0, sac=0, sout=0, vf=0)

        def vfence(e, inst):
            inst.then_inc(sems["vf"], 1)
            cnt["vf"] += 1
            e.wait_ge(sems["vf"], cnt["vf"])

        def bank(i):
            return ps[:, i * 512:(i + 1) * 512]

        def bcast_row(row_ap, dst_sbuf_ap, width):
            with nc.Block(no_gpsimd_drain=True) as blk:
                @blk.tensor
                def _(e):
                    for sgi in range(width // 512):
                        e.matmul(ps[:, sgi * 512:sgi * 512 + 512],
                                 ones_r[0:1, :],
                                 row_ap[0:1, sgi * 512:(sgi + 1) * 512],
                                 start=True, stop=True)

            with nc.Block(no_gpsimd_drain=True) as blk:
                @blk.vector
                def _(e):
                    e.tensor_copy(dst_sbuf_ap, ps[:, 0:width])

        def ln_plain(xap, outap, scr, eps_ap=None):
            with nc.Block(no_gpsimd_drain=True) as blk:
                @blk.vector
                def _(e):
                    e.reduce_sum(stat[:, 0:1], xap, axis=AX.X)
                    e.tensor_tensor(scr, xap, xap, ALU.mult)
                    e.reduce_sum(stat[:, 1:2], scr, axis=AX.X)
                    vfence(e, e.tensor_scalar(stat[:, 0:1], stat[:, 0:1],
                                               1.0 / 1024, None, ALU.mult))
                    vfence(e, e.tensor_tensor(stat[:, 2:3], stat[:, 0:1],
                                              stat[:, 0:1], ALU.mult))
                    e.scalar_tensor_tensor(stat[:, 2:3], stat[:, 1:2], 1.0 / 1024,
                                           stat[:, 2:3], ALU.mult, ALU.subtract)

            with nc.Block(no_gpsimd_drain=True) as blk:
                @blk.scalar
                def _(e):
                    e.activation(stat[:, 3:4], stat[:, 2:3], AF.Sqrt,
                                 bias=(eps_ap if eps_ap is not None
                                       else epsc[:, 0:1]))

            with nc.Block(no_gpsimd_drain=True) as blk:
                @blk.vector
                def _(e):
                    vfence(e, e.reciprocal(stat[:, 2:3], stat[:, 3:4]))
                    e.tensor_scalar(outap, xap, stat[:, 0:1], stat[:, 2:3],
                                    ALU.subtract, ALU.mult)

        with ExitStack() as _a:
            wA_hi = _a.enter_context(nc.sbuf_tensor("wA_hi", [128, 8192], BF16))
            wA_lo = _a.enter_context(nc.sbuf_tensor("wA_lo", [128, 8192], BF16))
            wB_hi = _a.enter_context(nc.sbuf_tensor("wB_hi", [128, 8192], BF16))
            wB_lo = _a.enter_context(nc.sbuf_tensor("wB_lo", [128, 8192], BF16))
            wstage = _a.enter_context(nc.sbuf_tensor("wstage", [128, 4096], F32))
            x_tm = _a.enter_context(nc.sbuf_tensor("x_tm", [128, 1024], F32))
            xln_tm = _a.enter_context(nc.sbuf_tensor("xln_tm", [128, 1024], F32))
            xlnf = _a.enter_context(nc.sbuf_tensor("xlnf", [128, 2048], F32))
            mix_hi = _a.enter_context(nc.sbuf_tensor("mix_hi", [128, 1024], BF16))
            mix_lo = _a.enter_context(nc.sbuf_tensor("mix_lo", [128, 1024], BF16))
            dxf = _a.enter_context(nc.sbuf_tensor("dxf", [128, 1024], F32))
            s1 = _a.enter_context(nc.sbuf_tensor("s1", [128, 1024], F32))
            s2 = _a.enter_context(nc.sbuf_tensor("s2", [128, 1024], F32))
            s3 = _a.enter_context(nc.sbuf_tensor("s3", [128, 1024], F32))
            sbig = _a.enter_context(nc.sbuf_tensor("sbig", [128, 1024], F32))
            kwk_hi = _a.enter_context(nc.sbuf_tensor("kwk_hi", [128, 1024], BF16))
            kwk_lo = _a.enter_context(nc.sbuf_tensor("kwk_lo", [128, 1024], BF16))
            kfm_hi = _a.enter_context(nc.sbuf_tensor("kfm_hi", [64, 4096], BF16))
            kfm_lo = _a.enter_context(nc.sbuf_tensor("kfm_lo", [64, 4096], BF16))
            vtm_hi = _a.enter_context(nc.sbuf_tensor("vtm_hi", [128, 2048], BF16))
            vtm_lo = _a.enter_context(nc.sbuf_tensor("vtm_lo", [128, 2048], BF16))
            ssn_hi = _a.enter_context(nc.sbuf_tensor("ssn_hi", [64, 2048], BF16))
            ssn_lo = _a.enter_context(nc.sbuf_tensor("ssn_lo", [64, 2048], BF16))
            rfm_hi = _a.enter_context(nc.sbuf_tensor("rfm_hi", [64, 2048], BF16))
            rfm_lo = _a.enter_context(nc.sbuf_tensor("rfm_lo", [64, 2048], BF16))
            atw_hi = _a.enter_context(nc.sbuf_tensor("atw_hi", [128, 1024], BF16))
            atw_lo = _a.enter_context(nc.sbuf_tensor("atw_lo", [128, 1024], BF16))
            yg_hi = _a.enter_context(nc.sbuf_tensor("yg_hi", [128, 2048], BF16))
            yg_lo = _a.enter_context(nc.sbuf_tensor("yg_lo", [128, 2048], BF16))
            s_f32 = _a.enter_context(nc.sbuf_tensor("s_f32", [64, 1024], F32))
            s_hi = _a.enter_context(nc.sbuf_tensor("s_hi", [64, 1024], BF16))
            s_lo = _a.enter_context(nc.sbuf_tensor("s_lo", [64, 1024], BF16))
            wmT = _a.enter_context(nc.sbuf_tensor("wmT", [128, 2048], F32))
            bcol = _a.enter_context(nc.sbuf_tensor("bcol", [128, 8], F32))
            bhist = _a.enter_context(nc.sbuf_tensor("bhist", [128, 16], F32))
            lndb = _a.enter_context(nc.sbuf_tensor("lndb", [128, 16], F32))
            ub = _a.enter_context(nc.sbuf_tensor("ub", [128, 16], F32))
            wb_t = _a.enter_context(nc.sbuf_tensor("wb_t", [128, 16], F32))
            wk_t = _a.enter_context(nc.sbuf_tensor("wk_t", [128, 16], F32))
            ws_p = _a.enter_context(nc.sbuf_tensor("ws_p", [64, 16], F32))
            lnw_tm = _a.enter_context(nc.sbuf_tensor("lnw_tm", [128, 1024], F32))
            lnb_tm = _a.enter_context(nc.sbuf_tensor("lnb_tm", [128, 1024], F32))
            iotac = _a.enter_context(nc.sbuf_tensor("iotac", [128, 2], F32))
            csc = _a.enter_context(nc.sbuf_tensor("csc", [128, 48], F32))
            dbgst = (_a.enter_context(nc.sbuf_tensor("dbgst", [128, 1024], F32))
                     if KDBG else None)
            with nc.Block() as blk:
                @blk.sync
                def _(e):
                    dma(e, ident[:, :], ident_d[:, :], "sw")
                    dma(e, tmv[:, :], tmix_d[:, :], "sw")
                    dma(e, csc[0:1, 0:32], tdu_d[0:1, :], "sw")
                    dma(e, wstage[0:1, 0:2048], lnrow_d[0:1, :], "sw")

                @blk.gpsimd
                def _(e):
                    e.memset(ones_r[0:1, :], 1.0)
                    e.memset(onescol[:, :], 1.0)
                    e.memset(epsc[:, 0:1], EPS)
                    e.memset(epsc[:, 1:2], GEPS)
                    e.memset(s_f32[:, :], 0.0)
                    e.memset(s_hi[:, :], 0.0)
                    e.memset(s_lo[:, :], 0.0)
                    e.memset(bcol[:, :], 0.0)
                    e.iota(sbig[:, 0:128], [[1, 128]], channel_multiplier=-1,
                           allow_small_or_imprecise_dtypes=True)
                    e.iota(iotac[:, 0:1], [[1, 1]], channel_multiplier=1,
                           allow_small_or_imprecise_dtypes=True)

            with nc.Block(no_gpsimd_drain=True) as blk:
                @blk.scalar
                def _(e):
                    wt(e, "sw")
                    e.activation(csc[0:1, 0:16], csc[0:1, 0:16], AF.Exp)

                @blk.vector
                def _(e):
                    wt(e, "sw")
                    e.tensor_scalar(iotac[:, 1:2], iotac[:, 0:1], -1.0, 127.0,
                                    ALU.mult, ALU.add)
                    e.tensor_scalar(s1[:, 0:128], sbig[:, 0:128], 1.0, 0.0,
                                    ALU.subtract, ALU.max)
                    e.tensor_scalar(s2[:, 0:128], sbig[:, 0:128], 1.0, None,
                                    ALU.is_ge)
                    e.tensor_scalar(s3[:, 0:128], sbig[:, 0:128], 0.0, None,
                                    ALU.is_equal)

            with nc.Block(no_gpsimd_drain=True) as blk:
                @blk.vector
                def _(e):
                    e.tensor_scalar(csc[0:1, 0:16], csc[0:1, 0:16], -1.0, None,
                                    ALU.mult)

            with nc.Block(no_gpsimd_drain=True) as blk:
                @blk.tensor
                def _(e):
                    e.matmul(ps[:, 0:16], ones_r[0:1, :], csc[0:1, 0:16],
                             start=True, stop=False)
                    e.matmul(ps[:, 16:32], ones_r[0:1, :], csc[0:1, 16:32],
                             start=False, stop=True)

            with nc.Block(no_gpsimd_drain=True) as blk:
                @blk.vector
                def _(e):
                    e.tensor_copy(lndb[:, :], ps[:, 0:16])
                    e.tensor_copy(ub[:, :], ps[:, 16:32])

            with nc.Block(no_gpsimd_drain=True) as blk:
                @blk.scalar
                def _(e):
                    for h in range(H):
                        e.activation(wb_t[:, h:h + 1], iotac[:, 0:1], AF.Exp,
                                     scale=lndb[:, h:h + 1])
                        e.activation(wk_t[:, h:h + 1], iotac[:, 1:2], AF.Exp,
                                     scale=lndb[:, h:h + 1])
                    e.activation(csc[:, 0:16], lndb[:, :], AF.Exp, scale=128.0)
                    for h in range(H):
                        e.activation(wmT[:, h * 128:(h + 1) * 128], s1[:, 0:128],
                                     AF.Exp, scale=lndb[:, h:h + 1])

            with nc.Block(no_gpsimd_drain=True) as blk:
                @blk.vector
                def _(e):
                    e.tensor_copy(ws_p[0:64, :], csc[0:64, 0:16])
                    for h in range(H):
                        sl = wmT[:, h * 128:(h + 1) * 128]
                        e.tensor_tensor(sl, sl, s2[:, 0:128], ALU.mult)
                        e.scalar_tensor_tensor(sl, s3[:, 0:128], ub[:, h:h + 1],
                                               sl, ALU.mult, ALU.add)

            bcast_row(wstage[0:1, 0:1024], lnw_tm[:, :], 1024)
            bcast_row(wstage[0:1, 1024:2048], lnb_tm[:, :], 1024)

            def load_wsplit(dsth, dstl, dram):
                def _dma(e, step):
                    srcap = dram[step * 256:(step + 1) * 256, :].rearrange(
                        "(cc p) w -> p cc w", p=128)
                    dma(e, r3(wstage[:, (step % 2) * 2048:
                                     (step % 2) * 2048 + 2048], 1024),
                        srcap, "sw")

                def _split(e, step, when):
                    e.wait_ge(sems["sw"], when)
                    o = step * 2048
                    wsl = wstage[:, (step % 2) * 2048:(step % 2) * 2048 + 2048]
                    e.tensor_copy(dsth[:, o:o + 2048], wsl)
                    e.tensor_tensor(dstl[:, o:o + 2048], wsl,
                                    dsth[:, o:o + 2048], ALU.subtract)

                with nc.Block(no_gpsimd_drain=True) as blk:
                    @blk.sync
                    def _(e):
                        _dma(e, 0)
                for step in range(3):
                    done = cnt["sw"]
                    with nc.Block(no_gpsimd_drain=True) as blk:
                        @blk.sync
                        def _(e, step=step):
                            _dma(e, step + 1)

                        @blk.vector
                        def _(e, step=step, done=done):
                            _split(e, step, done)
                done = cnt["sw"]
                with nc.Block(no_gpsimd_drain=True) as blk:
                    @blk.vector
                    def _(e, done=done):
                        _split(e, 3, done)

            def mix_split(slot, m):
                cur = r3(xlnf[:, slot * 1024:(slot + 1) * 1024], 128)
                bh = bhist[:, slot * 8:slot * 8 + 8].unsqueeze(2)
                tm3 = rep3(tmv[:, m * 8:(m + 1) * 8], 128)
                with nc.Block(no_gpsimd_drain=True) as blk:
                    @blk.vector
                    def _(e):
                        e.tensor_tensor(r3(s2[:, :], 128), r3(dxf[:, :], 128),
                                        tm3, ALU.mult)
                        e.tensor_tensor(r3(s3[:, :], 128)[:, :, 1:128],
                                        cur[:, :, 0:127],
                                        r3(s2[:, :], 128)[:, :, 1:128], ALU.add)
                        e.tensor_tensor(r3(s3[:, :], 128)[:, :, 0:1], bh,
                                        r3(s2[:, :], 128)[:, :, 0:1], ALU.add)
                        e.tensor_copy(mix_hi[:, :], s3[:, :])
                        e.tensor_tensor(mix_lo[:, :], s3[:, :], mix_hi[:, :],
                                        ALU.subtract)

            def proj_tm(whi, wlo, pbank0):
                with nc.Block(no_gpsimd_drain=True) as blk:
                    @blk.tensor
                    def _(e):
                        for half in range(2):
                            out = bank(pbank0 + half)
                            first = True
                            for cc in range(NCC):
                                a_hi = mix_hi[:, cc * 128:(cc + 1) * 128]
                                a_lo = mix_lo[:, cc * 128:(cc + 1) * 128]
                                woff = cc * 1024 + half * 512
                                e.matmul(out, a_hi, whi[:, woff:woff + 512],
                                         start=first, stop=False)
                                first = False
                                e.matmul(out, a_hi, wlo[:, woff:woff + 512],
                                         start=False, stop=False)
                                e.matmul(out, a_lo, whi[:, woff:woff + 512],
                                         start=False, stop=(cc == NCC - 1))

            def transpose8(src, pbank0):
                with nc.Block(no_gpsimd_drain=True) as blk:
                    @blk.tensor
                    def _(e):
                        for oc in range(NCC):
                            e.transpose(ps[:, pbank0 * 512 + oc * 128:
                                           pbank0 * 512 + (oc + 1) * 128],
                                        src[:, oc * 128:(oc + 1) * 128],
                                        ident[:, :])

            def chunk_part1(n, slot, is_ctx):
                tok0 = n * T
                with nc.Block(no_gpsimd_drain=True) as blk:
                    @blk.sync
                    def _(e):
                        dma(e, x_tm[:, :], xin[tok0:tok0 + T, :], "sx")

                with nc.Block(no_gpsimd_drain=True) as blk:
                    @blk.vector
                    def _(e):
                        wt(e, "sx")

                    @blk.sync
                    def _(e):
                        wt(e, "ssp")

                ln_plain(x_tm[:, :], xln_tm[:, :], s1[:, :])
                transpose8(xln_tm, 0)
                with nc.Block(no_gpsimd_drain=True) as blk:
                    @blk.vector
                    def _(e):
                        e.tensor_copy(bhist[:, slot * 8:slot * 8 + 8], bcol[:, :])
                        e.tensor_copy(xlnf[:, slot * 1024:(slot + 1) * 1024],
                                      ps[:, 0:1024])

                with nc.Block(no_gpsimd_drain=True) as blk:
                    @blk.vector
                    def _(e):
                        cur = r3(xlnf[:, slot * 1024:(slot + 1) * 1024], 128)
                        e.tensor_copy(bcol[:, :].unsqueeze(2),
                                      cur[:, :, 127:128])
                        e.tensor_tensor(r3(dxf[:, :], 128)[:, :, 1:128],
                                        cur[:, :, 1:128], cur[:, :, 0:127],
                                        ALU.subtract)
                        e.tensor_tensor(r3(dxf[:, :], 128)[:, :, 0:1],
                                        cur[:, :, 0:1],
                                        bhist[:, slot * 8:slot * 8 + 8]
                                        .unsqueeze(2), ALU.subtract)

                if KDBG and n == 16:
                    with nc.Block(no_gpsimd_drain=True) as blk:
                        @blk.sync
                        def _(e):
                            dbg_dump(e, "xln_tm", xln_tm[:, :])
                            dbg_dump(e, "dxf", dxf[:, :])
                mix_split(slot, M_K)
                if KDBG and n == 16:
                    with nc.Block(no_gpsimd_drain=True) as blk:
                        @blk.sync
                        def _(e):
                            dbg_dump(e, "xmk", s3[:, :])
                proj_tm(wA_hi, wA_lo, 0)
                with nc.Block(no_gpsimd_drain=True) as blk:
                    @blk.vector
                    def _(e):
                        e.tensor_copy(s1[:, :], ps[:, 0:1024])

                if KDBG and n == 16:
                    with nc.Block(no_gpsimd_drain=True) as blk:
                        @blk.sync
                        def _(e):
                            dbg_dump(e, "k_tm", s1[:, :])

                with nc.Block(no_gpsimd_drain=True) as blk:
                    @blk.vector
                    def _(e):
                        e.tensor_tensor(r3(s2[:, :], 64), r3(s1[:, :], 64),
                                        rep3(wk_t[:, :], 64), ALU.mult)
                        e.tensor_copy(kwk_hi[:, :], s2[:, :])
                        e.tensor_tensor(kwk_lo[:, :], s2[:, :], kwk_hi[:, :],
                                        ALU.subtract)

                if not is_ctx:
                    with nc.Block(no_gpsimd_drain=True) as blk:
                        @blk.tensor
                        def _(e):
                            for h in range(H):
                                e.transpose(ps[0:64, 1024 + h * 128:
                                               1024 + (h + 1) * 128],
                                            s1[:, h * 64:(h + 1) * 64],
                                            ident[:, :])

                    with nc.Block(no_gpsimd_drain=True) as blk:
                        @blk.vector
                        def _(e):
                            o = slot * 2048
                            e.tensor_copy(kfm_hi[0:64, o:o + 2048],
                                          ps[0:64, 1024:3072])
                            e.tensor_tensor(kfm_lo[0:64, o:o + 2048],
                                            ps[0:64, 1024:3072],
                                            kfm_hi[0:64, o:o + 2048],
                                            ALU.subtract)

                mix_split(slot, M_V)
                proj_tm(wB_hi, wB_lo, 4)
                with nc.Block(no_gpsimd_drain=True) as blk:
                    @blk.vector
                    def _(e):
                        o = slot * 1024
                        e.tensor_copy(vtm_hi[:, o:o + 1024], ps[:, 2048:3072])
                        e.tensor_tensor(vtm_lo[:, o:o + 1024], ps[:, 2048:3072],
                                        vtm_hi[:, o:o + 1024], ALU.subtract)

                if KDBG and n == 16:
                    with nc.Block(no_gpsimd_drain=True) as blk:
                        @blk.vector
                        def _(e):
                            o = slot * 1024
                            e.tensor_tensor(dbgst[:, 0:1024], vtm_hi[:, o:o + 1024],
                                            vtm_lo[:, o:o + 1024], ALU.add)

                    with nc.Block(no_gpsimd_drain=True) as blk:
                        @blk.sync
                        def _(e):
                            dbg_dump(e, "v_tm", dbgst[:, 0:1024])
                if not is_ctx:
                    with nc.Block(no_gpsimd_drain=True) as blk:
                        @blk.vector
                        def _(e):
                            o = slot * 1024
                            e.tensor_copy(ssn_hi[0:64, o:o + 1024], s_hi[0:64, :])
                            e.tensor_copy(ssn_lo[0:64, o:o + 1024], s_lo[0:64, :])

                with nc.Block(no_gpsimd_drain=True) as blk:
                    @blk.tensor
                    def _(e):
                        for h in range(H):
                            o = slot * 1024 + h * 64
                            outsl = ps[0:64, 6 * 512 + h * 64:
                                       6 * 512 + (h + 1) * 64]
                            e.matmul(outsl, kwk_hi[:, h * 64:h * 64 + 64],
                                     vtm_hi[:, o:o + 64],
                                     start=(h % 8 == 0), stop=False)
                            e.matmul(outsl, kwk_hi[:, h * 64:h * 64 + 64],
                                     vtm_lo[:, o:o + 64], start=False,
                                     stop=False)
                            e.matmul(outsl, kwk_lo[:, h * 64:h * 64 + 64],
                                     vtm_hi[:, o:o + 64], start=False,
                                     stop=(h % 8 == 7))

                with nc.Block(no_gpsimd_drain=True) as blk:
                    @blk.vector
                    def _(e):
                        e.tensor_tensor(
                            s_f32[0:64, :].rearrange("p (n b) -> p n b", b=64),
                            s_f32[0:64, :].rearrange("p (n b) -> p n b", b=64),
                            ws_p[0:64, :].unsqueeze(2)
                            .broadcast_to([64, 16, 64]), ALU.mult)
                        e.tensor_tensor(s_f32[0:64, :], s_f32[0:64, :],
                                        ps[0:64, 6 * 512:8 * 512], ALU.add)
                        e.tensor_copy(s_hi[0:64, :], s_f32[0:64, :])
                        e.tensor_tensor(s_lo[0:64, :], s_f32[0:64, :],
                                        s_hi[0:64, :], ALU.subtract)

                if KDBG and n == 16:
                    with nc.Block(no_gpsimd_drain=True) as blk:
                        @blk.sync
                        def _(e):
                            dbg_dump(e, "s_next", s_f32[0:64, :])

            def chunk_part2(n, slot):
                with nc.Block(no_gpsimd_drain=True) as blk:
                    @blk.vector
                    def _(e):
                        cur = r3(xlnf[:, slot * 1024:(slot + 1) * 1024], 128)
                        e.tensor_tensor(r3(dxf[:, :], 128)[:, :, 1:128],
                                        cur[:, :, 1:128], cur[:, :, 0:127],
                                        ALU.subtract)
                        e.tensor_tensor(r3(dxf[:, :], 128)[:, :, 0:1],
                                        cur[:, :, 0:1],
                                        bhist[:, slot * 8:slot * 8 + 8]
                                        .unsqueeze(2), ALU.subtract)

                mix_split(slot, M_R)
                proj_tm(wA_hi, wA_lo, 0)
                with nc.Block(no_gpsimd_drain=True) as blk:
                    @blk.vector
                    def _(e):
                        e.tensor_copy(s1[:, :], ps[:, 0:1024])

                with nc.Block(no_gpsimd_drain=True) as blk:
                    @blk.tensor
                    def _(e):
                        for h in range(H):
                            e.transpose(ps[0:64, 1024 + h * 128:
                                           1024 + (h + 1) * 128],
                                        s1[:, h * 64:(h + 1) * 64], ident[:, :])

                with nc.Block(no_gpsimd_drain=True) as blk:
                    @blk.vector
                    def _(e):
                        e.tensor_copy(rfm_hi[0:64, :], ps[0:64, 1024:3072])
                        e.tensor_tensor(rfm_lo[0:64, :], ps[0:64, 1024:3072],
                                        rfm_hi[0:64, :], ALU.subtract)

                mix_split(slot, M_G)
                proj_tm(wB_hi, wB_lo, 4)
                with nc.Block(no_gpsimd_drain=True) as blk:
                    @blk.scalar
                    def _(e):
                        e.activation(s3[:, :], ps[:, 2048:3072], AF.Silu)

                if KDBG and n == 16:
                    with nc.Block(no_gpsimd_drain=True) as blk:
                        @blk.sync
                        def _(e):
                            dbg_dump(e, "g_tm", s3[:, :])
                for hh in range(2):
                    with nc.Block(no_gpsimd_drain=True) as blk:
                        @blk.tensor
                        def _(e, hh=hh):
                            for hi_ in range(8):
                                h = hh * 8 + hi_
                                ko = slot * 2048 + h * 128
                                ro = h * 128
                                out = ps[:, hi_ * 128:(hi_ + 1) * 128]
                                e.matmul(out, kfm_hi[0:64, ko:ko + 128],
                                         rfm_hi[0:64, ro:ro + 128],
                                         start=(hi_ % 4 == 0), stop=False)
                                e.matmul(out, kfm_hi[0:64, ko:ko + 128],
                                         rfm_lo[0:64, ro:ro + 128],
                                         start=False, stop=False)
                                e.matmul(out, kfm_lo[0:64, ko:ko + 128],
                                         rfm_hi[0:64, ro:ro + 128],
                                         start=False, stop=(hi_ % 4 == 3))

                    with nc.Block(no_gpsimd_drain=True) as blk:
                        @blk.vector
                        def _(e, hh=hh):
                            e.tensor_tensor(sbig[:, :], ps[:, 0:1024],
                                            wmT[:, hh * 1024:(hh + 1) * 1024],
                                            ALU.mult)
                            e.tensor_copy(atw_hi[:, :], sbig[:, :])
                            e.tensor_tensor(atw_lo[:, :], sbig[:, :],
                                            atw_hi[:, :], ALU.subtract)

                    if KDBG and n == 16 and hh == 0:
                        with nc.Block(no_gpsimd_drain=True) as blk:
                            @blk.sync
                            def _(e):
                                dbg_dump(e, "attw0", sbig[:, :])

                    with nc.Block(no_gpsimd_drain=True) as blk:
                        @blk.tensor
                        def _(e, hh=hh):
                            for hi_ in range(8):
                                h = hh * 8 + hi_
                                vo = slot * 1024 + h * 64
                                ro = h * 128
                                so = slot * 1024 + h * 64
                                o1 = ps[:, 2 * 512 + h * 64:2 * 512 + h * 64 + 64]
                                e.matmul(o1, atw_hi[:, hi_ * 128:(hi_ + 1) * 128],
                                         vtm_hi[:, vo:vo + 64],
                                         start=(hi_ == 0), stop=False)
                                e.matmul(o1, atw_hi[:, hi_ * 128:(hi_ + 1) * 128],
                                         vtm_lo[:, vo:vo + 64], start=False,
                                         stop=False)
                                e.matmul(o1, atw_lo[:, hi_ * 128:(hi_ + 1) * 128],
                                         vtm_hi[:, vo:vo + 64], start=False,
                                         stop=(hi_ == 7))
                                o2 = ps[:, 4 * 512 + h * 64:4 * 512 + h * 64 + 64]
                                e.matmul(o2, rfm_hi[0:64, ro:ro + 128],
                                         ssn_hi[0:64, so:so + 64],
                                         start=(hi_ == 0), stop=False)
                                e.matmul(o2, rfm_hi[0:64, ro:ro + 128],
                                         ssn_lo[0:64, so:so + 64],
                                         start=False, stop=False)
                                e.matmul(o2, rfm_lo[0:64, ro:ro + 128],
                                         ssn_hi[0:64, so:so + 64],
                                         start=False, stop=(hi_ == 7))

                with nc.Block(no_gpsimd_drain=True) as blk:
                    @blk.vector
                    def _(e):
                        e.tensor_tensor(r3(s1[:, :], 64),
                                        r3(ps[:, 4 * 512:6 * 512], 64),
                                        rep3(wb_t[:, :], 64), ALU.mult)
                        e.tensor_tensor(s1[:, :], s1[:, :],
                                        ps[:, 2 * 512:4 * 512], ALU.add)

                if KDBG and n == 16:
                    with nc.Block(no_gpsimd_drain=True) as blk:
                        @blk.sync
                        def _(e):
                            dbg_dump(e, "y_raw", s1[:, :])

                with nc.Block(no_gpsimd_drain=True) as blk:
                    @blk.vector
                    def _(e):
                        e.tensor_tensor(s2[:, :], s1[:, :], s1[:, :], ALU.mult)
                        e.reduce_sum(csc[:, 0:16], r3(s1[:, :], 64), axis=AX.X)
                        vfence(e, e.reduce_sum(csc[:, 16:32], r3(s2[:, :], 64),
                                               axis=AX.X))
                        vfence(e, e.tensor_scalar(csc[:, 0:16], csc[:, 0:16],
                                                   1.0 / 64, None, ALU.mult))
                        vfence(e, e.tensor_tensor(csc[:, 32:48], csc[:, 0:16],
                                                  csc[:, 0:16], ALU.mult))
                        e.scalar_tensor_tensor(csc[:, 16:32], csc[:, 16:32],
                                               1.0 / 64, csc[:, 32:48],
                                               ALU.mult, ALU.subtract)

                with nc.Block(no_gpsimd_drain=True) as blk:
                    @blk.scalar
                    def _(e):
                        e.activation(csc[:, 16:32], csc[:, 16:32], AF.Sqrt,
                                     bias=epsc[:, 1:2])

                with nc.Block(no_gpsimd_drain=True) as blk:
                    @blk.vector
                    def _(e):
                        vfence(e, e.reciprocal(csc[:, 16:32], csc[:, 16:32]))
                        e.tensor_tensor(r3(s1[:, :], 64), r3(s1[:, :], 64),
                                        rep3(csc[:, 0:16], 64), ALU.subtract)
                        e.tensor_tensor(r3(s1[:, :], 64), r3(s1[:, :], 64),
                                        rep3(csc[:, 16:32], 64), ALU.mult)
                        e.tensor_tensor(s1[:, :], s1[:, :], lnw_tm[:, :],
                                        ALU.mult)
                        e.tensor_tensor(s1[:, :], s1[:, :], lnb_tm[:, :],
                                        ALU.add)
                        e.tensor_tensor(s1[:, :], s1[:, :], s3[:, :], ALU.mult)

                if KDBG and n == 16:
                    with nc.Block(no_gpsimd_drain=True) as blk:
                        @blk.sync
                        def _(e):
                            dbg_dump(e, "yg_tm", s1[:, :])

                transpose8(s1, 0)
                with nc.Block(no_gpsimd_drain=True) as blk:
                    @blk.vector
                    def _(e):
                        o = slot * 1024
                        e.tensor_copy(yg_hi[:, o:o + 1024], ps[:, 0:1024])
                        e.tensor_tensor(yg_lo[:, o:o + 1024], ps[:, 0:1024],
                                        yg_hi[:, o:o + 1024], ALU.subtract)

            def chunk_part3(n, slot):
                tok0 = n * T
                own_t0 = (n - OWN0) * T
                with nc.Block(no_gpsimd_drain=True) as blk:
                    @blk.tensor
                    def _(e):
                        for half in range(2):
                            out = bank(half)
                            first = True
                            for cc in range(NCC):
                                o = slot * 1024 + cc * 128
                                woff = cc * 1024 + half * 512
                                e.matmul(out, yg_hi[:, o:o + 128],
                                         wA_hi[:, woff:woff + 512],
                                         start=first, stop=False)
                                first = False
                                e.matmul(out, yg_hi[:, o:o + 128],
                                         wA_lo[:, woff:woff + 512],
                                         start=False, stop=False)
                                e.matmul(out, yg_lo[:, o:o + 128],
                                         wA_hi[:, woff:woff + 512],
                                         start=False, stop=(cc == NCC - 1))

                    @blk.sync
                    def _(e):
                        dma(e, x_tm[:, :], xin[tok0:tok0 + T, :], "sx")

                with nc.Block(no_gpsimd_drain=True) as blk:
                    @blk.vector
                    def _(e):
                        e.tensor_copy(s1[:, :], ps[:, 0:1024])

                if KDBG and n == 16:
                    with nc.Block(no_gpsimd_drain=True) as blk:
                        @blk.sync
                        def _(e):
                            dbg_dump(e, "attnout", s1[:, :])

                ln_plain(s1[:, :], s2[:, :], s3[:, :])
                with nc.Block(no_gpsimd_drain=True) as blk:
                    @blk.vector
                    def _(e):
                        wt(e, "sx")
                        e.tensor_tensor(s1[:, :], x_tm[:, :], s2[:, :], ALU.add)

                with nc.Block(no_gpsimd_drain=True) as blk:
                    @blk.sync
                    def _(e):
                        dma(e, x1sp[own_t0:own_t0 + T, :], s1[:, :], "ssp")

            def partial_out():
                for j in range(16):
                    with nc.Block(no_gpsimd_drain=True) as blk:
                        @blk.sync
                        def _(e, j=j):
                            if KPART in ("attn",):
                                wt(e, "ssp")
                                dma(e, x_tm[:, :], x1sp[j * T:(j + 1) * T, :],
                                    "sx")
                            else:
                                dma(e, x_tm[:, :],
                                    xin[OWNTOK + j * T:OWNTOK + (j + 1) * T, :],
                                    "sx")

                    with nc.Block(no_gpsimd_drain=True) as blk:
                        @blk.sync
                        def _(e, j=j):
                            wt(e, "sx")
                            dma(e, xout[j * T:(j + 1) * T, :], x_tm[:, :],
                                "sout")

                with nc.Block(no_gpsimd_drain=True) as blk:
                    @blk.sync
                    def _(e):
                        wt(e, "sout")

            if KPART == "setup":
                partial_out()
                return nc
            load_wsplit(wA_hi, wA_lo, wT["wk"])
            load_wsplit(wB_hi, wB_lo, wT["wv"])
            for n in range(0, KCTX):
                chunk_part1(n, n % 2, is_ctx=True)
            if KPART == "ctx":
                partial_out()
                return nc
            for g in range(KGRP):
                n0 = OWN0 + 2 * g
                if g > 0:
                    load_wsplit(wA_hi, wA_lo, wT["wk"])
                    load_wsplit(wB_hi, wB_lo, wT["wv"])
                for i in range(2):
                    chunk_part1(n0 + i, i, is_ctx=False)
                if KPART == "ga":
                    partial_out()
                    return nc
                load_wsplit(wA_hi, wA_lo, wT["wr"])
                load_wsplit(wB_hi, wB_lo, wT["wg"])
                for i in range(2):
                    chunk_part2(n0 + i, i)
                if KPART == "gb":
                    partial_out()
                    return nc
                load_wsplit(wA_hi, wA_lo, wT["wo"])
                for i in range(2):
                    chunk_part3(n0 + i, i)
            if KPART == "attn":
                partial_out()
                return nc

        with ExitStack() as _f:
            h_hi = _f.enter_context(nc.sbuf_tensor("h_hi", [128, 8 * 2048], BF16))
            h_lo = _f.enter_context(nc.sbuf_tensor("h_lo", [128, 8 * 2048], BF16))
            rw_hi = _f.enter_context(nc.sbuf_tensor("rw_hi", [128, 64], BF16))
            rw_lo = _f.enter_context(nc.sbuf_tensor("rw_lo", [128, 64], BF16))
            f1 = _f.enter_context(nc.sbuf_tensor("f1", [128, 2048], F32))
            f2 = _f.enter_context(nc.sbuf_tensor("f2", [128, 1024], F32))
            f3 = _f.enter_context(nc.sbuf_tensor("f3", [128, 1024], F32))
            f4 = _f.enter_context(nc.sbuf_tensor("f4", [128, 1024], F32))
            hv_hi = h_hi[:, :].rearrange("p (cc x) -> p cc x", x=2048)
            hv_lo = h_lo[:, :].rearrange("p (cc x) -> p cc x", x=2048)

            with ExitStack() as _h:
                gamb = _h.enter_context(nc.sbuf_tensor("gamb", [128, 1024], F32))
                betb = _h.enter_context(nc.sbuf_tensor("betb", [128, 1024], F32))
                rowf = _h.enter_context(nc.sbuf_tensor("rowf", [1, 2048], F32))
                with nc.Block(no_gpsimd_drain=True) as blk:
                    @blk.sync
                    def _(e):
                        dma(e, rowf[0:1, :], gbrow_d[0:1, :], "sw")
                        dma(e, r3(f2[:, 0:64], 8),
                            rwT_d[:, :].rearrange("(cc p) x -> p cc x", p=128),
                            "sw")

                with nc.Block(no_gpsimd_drain=True) as blk:
                    @blk.vector
                    def _(e):
                        wt(e, "sw")
                        e.tensor_copy(rw_hi[:, :], f2[:, 0:64])
                        e.tensor_tensor(rw_lo[:, :], f2[:, 0:64], rw_hi[:, :],
                                        ALU.subtract)

                bcast_row(rowf[0:1, 0:1024], gamb[:, :], 1024)
                bcast_row(rowf[0:1, 1024:2048], betb[:, :], 1024)

                with nc.Block(no_gpsimd_drain=True) as blk:
                    @blk.sync
                    def _(e):
                        wt(e, "ssp")

                for j in range(16):
                    with nc.Block(no_gpsimd_drain=True) as blk:
                        @blk.sync
                        def _(e, j=j):
                            dma(e, f1[:, 0:1024], x1sp[j * T:(j + 1) * T, :], "sx")

                    with nc.Block(no_gpsimd_drain=True) as blk:
                        @blk.vector
                        def _(e):
                            wt(e, "sx")

                    ln_plain(f1[:, 0:1024], f2[:, :], f3[:, :])
                    ln_plain(f2[:, :], f2[:, :], f3[:, :])
                    with nc.Block(no_gpsimd_drain=True) as blk:
                        @blk.vector
                        def _(e):
                            e.tensor_tensor(f2[:, :], f2[:, :], gamb[:, :],
                                            ALU.mult)

                    with nc.Block(no_gpsimd_drain=True) as blk:
                        @blk.vector
                        def _(e):
                            e.tensor_tensor(f2[:, :], f2[:, :], betb[:, :],
                                            ALU.add)

                    if KDBG and j == 0:
                        with nc.Block(no_gpsimd_drain=True) as blk:
                            @blk.sync
                            def _(e):
                                dbg_dump(e, "h_tm", f2[:, :])

                    transpose8(f2, 0)
                    with nc.Block(no_gpsimd_drain=True) as blk:
                        @blk.vector
                        def _(e, j=j):
                            dsth = hv_hi[:, :, j * 128:(j + 1) * 128]
                            dstl = hv_lo[:, :, j * 128:(j + 1) * 128]
                            e.tensor_copy(dsth, r3(ps[:, 0:1024], 128))
                            e.tensor_tensor(dstl, r3(ps[:, 0:1024], 128), dsth,
                                            ALU.subtract)

                    with nc.Block(no_gpsimd_drain=True) as blk:
                        @blk.tensor
                        def _(e, j=j):
                            out = ps[:, 4 * 512:4 * 512 + 8]
                            for cc in range(NCC):
                                a_hi = hv_hi[:, cc, j * 128:(j + 1) * 128]
                                a_lo = hv_lo[:, cc, j * 128:(j + 1) * 128]
                                e.matmul(out, a_hi, rw_hi[:, cc * 8:cc * 8 + 8],
                                         start=(cc == 0), stop=False)
                                e.matmul(out, a_hi, rw_lo[:, cc * 8:cc * 8 + 8],
                                         start=False, stop=False)
                                e.matmul(out, a_lo, rw_hi[:, cc * 8:cc * 8 + 8],
                                         start=False, stop=(cc == NCC - 1))

                    with nc.Block(no_gpsimd_drain=True) as blk:
                        @blk.vector
                        def _(e):
                            e.reduce_max(stat[:, 0:1],
                                         ps[:, 4 * 512:4 * 512 + 8],
                                         axis=AX.X, negate=True)

                    with nc.Block(no_gpsimd_drain=True) as blk:
                        @blk.scalar
                        def _(e):
                            e.activation(f3[:, 0:8], ps[:, 4 * 512:4 * 512 + 8],
                                         AF.Exp, bias=stat[:, 0:1],
                                         accum_out=stat[:, 1:2])

                    with nc.Block(no_gpsimd_drain=True) as blk:
                        @blk.vector
                        def _(e):
                            e.reduce_max(stat[:, 2:3], f3[:, 0:8], axis=AX.X)
                            e.reciprocal(stat[:, 3:4], stat[:, 1:2])

                    with nc.Block(no_gpsimd_drain=True) as blk:
                        @blk.vector
                        def _(e):
                            e.tensor_tensor(stat[:, 4:5], stat[:, 2:3],
                                            stat[:, 3:4], ALU.mult)
                            e.tensor_scalar(f3[:, 8:16], f3[:, 0:8],
                                            stat[:, 2:3], None, ALU.is_ge)

                    with nc.Block(no_gpsimd_drain=True) as blk:
                        @blk.vector
                        def _(e, j=j):
                            e.tensor_scalar(gates[:, j * 8:(j + 1) * 8],
                                            f3[:, 8:16], stat[:, 4:5], None,
                                            ALU.mult)


            if KDBG:
                with nc.Block(no_gpsimd_drain=True) as blk:
                    @blk.sync
                    def _(e):
                        dbg_dump(e, "gates", gates[:, :])

            with ExitStack() as _e2:
                q1 = _e2.enter_context(nc.sbuf_tensor("q1", [128, 8 * 2048], FP8))
                q2 = _e2.enter_context(nc.sbuf_tensor("q2", [128, 16 * 1024], FP8))
                accb = _e2.enter_context(nc.sbuf_tensor("accb", [128, 8 * 512], F32))
                scl1c = _e2.enter_context(nc.sbuf_tensor("scl1c", [128, 16], F32))
                scl2c = _e2.enter_context(nc.sbuf_tensor("scl2c", [128, 8], F32))
                sclrow = _e2.enter_context(nc.sbuf_tensor("sclrow", [1, 2048], F32))
                grow = sclrow
                def quant_weight(dram2d, nchunks, width, qdst, sclc, scl_n,
                                 wstf, fsc, thrb, nthrb):
                    nseg = width // 512
                    rows_per_grp = 2048 // width
                    ngrp = nchunks // rows_per_grp
                    gview = dram2d.rearrange("(g s p) w -> g p s w", p=128,
                                             s=rows_per_grp)

                    def _dma(e, g):
                        dma(e, r3(wstf[:, (g % 2) * 2048:(g % 2) * 2048 + 2048],
                                  width), gview[g], "sw")

                    def _abs(e, g):
                        e.activation(fsc[:, (g % 2) * 2048:(g % 2) * 2048 + 2048],
                                     wstf[:, (g % 2) * 2048:(g % 2) * 2048 + 2048],
                                     AF.Abs)

                    def _mm(e, g):
                        fb = (g % 2) * 2048
                        for s in range(rows_per_grp):
                            for sgi in range(nseg):
                                e.matmul(ps[0:1, 2048 + sgi * 512:
                                            2048 + sgi * 512 + 512],
                                         onescol[:, 0:1],
                                         fsc[:, fb + s * width + sgi * 512:
                                             fb + s * width + (sgi + 1) * 512],
                                         start=(g == 0 and s == 0),
                                         stop=(g == ngrp - 1 and
                                               s == rows_per_grp - 1))

                    waits = {}
                    for g in range(ngrp + 2):
                        with nc.Block(no_gpsimd_drain=True) as blk:
                            if g < ngrp:
                                @blk.sync
                                def _(e, g=g):
                                    _dma(e, g)
                                waits[g] = cnt["sw"]
                            if 0 <= g - 1 < ngrp:
                                @blk.scalar
                                def _(e, g=g):
                                    e.wait_ge(sems["sw"], waits[g - 1])
                                    _abs(e, g - 1)
                            if 0 <= g - 2 < ngrp:
                                @blk.tensor
                                def _(e, g=g):
                                    _mm(e, g - 2)

                    with nc.Block(no_gpsimd_drain=True) as blk:
                        @blk.vector
                        def _(e):
                            e.tensor_scalar(sclrow[0:1, 0:width],
                                            ps[0:1, 2048:2048 + width],
                                            1.0 / (nchunks * 128), 1e-5,
                                            ALU.mult, ALU.max)

                    with nc.Block(no_gpsimd_drain=True) as blk:
                        @blk.tensor
                        def _(e):
                            for q in range(scl_n):
                                e.matmul(ps[:, 2048 + q:2048 + q + 1],
                                         sclrow[0:1, q * 128:(q + 1) * 128],
                                         ones_r[0:1, 0:1], start=True, stop=True)
                            for sgi in range(nseg):
                                e.matmul(ps[:, sgi * 512:(sgi + 1) * 512],
                                         ones_r[0:1, :],
                                         sclrow[0:1, sgi * 512:(sgi + 1) * 512],
                                         start=True, stop=True)

                    with nc.Block(no_gpsimd_drain=True) as blk:
                        @blk.vector
                        def _(e):
                            e.tensor_copy(sclc[:, 0:scl_n],
                                          ps[:, 2048:2048 + scl_n])
                            e.tensor_scalar(thrb[:, 0:width], ps[:, 0:width],
                                            0.5, None, ALU.mult)
                            e.tensor_scalar(nthrb[:, 0:width], thrb[:, 0:width],
                                            -1.0, None, ALU.mult)

                    def _quant(e, g):
                        fb = (g % 2) * 2048
                        for s in range(rows_per_grp):
                            wsl = wstf[:, fb + s * width:fb + (s + 1) * width]
                            fsl = fsc[:, s * width:(s + 1) * width]
                            ch = g * rows_per_grp + s
                            e.tensor_tensor(fsl, wsl, thrb[:, 0:width], ALU.is_ge)
                            e.tensor_tensor(wsl, wsl, nthrb[:, 0:width], ALU.is_le)
                            e.tensor_tensor(qdst[:, ch * width:(ch + 1) * width],
                                            fsl, wsl, ALU.subtract)

                    for g in range(ngrp + 1):
                        with nc.Block(no_gpsimd_drain=True) as blk:
                            if g < ngrp:
                                @blk.sync
                                def _(e, g=g):
                                    _dma(e, g)
                                waits[g] = cnt["sw"]
                            if 0 <= g - 1 < ngrp:
                                @blk.vector
                                def _(e, g=g):
                                    e.wait_ge(sems["sw"], waits[g - 1])
                                    _quant(e, g - 1)

                for eidx in list(range(8))[:max(0, KEXP - 1)] + [8]:
                    shared = (eidx == 8)
                    w1d = sw1T if shared else ew1T[eidx, :, :]
                    w2d = sw2T if shared else ew2T[eidx, :, :]
                    _qe = ExitStack()
                    wstf = _qe.enter_context(
                        nc.sbuf_tensor(f"wstf{eidx}", [128, 4096], F32))
                    fsc = _qe.enter_context(
                        nc.sbuf_tensor(f"fsc{eidx}", [128, 4096], F32))
                    thrb = _qe.enter_context(
                        nc.sbuf_tensor(f"thrb{eidx}", [128, 2048], F32))
                    nthrb = _qe.enter_context(
                        nc.sbuf_tensor(f"nthrb{eidx}", [128, 2048], F32))
                    quant_weight(w1d, NCC, 2048, q1, scl1c, 16,
                                 wstf, fsc, thrb, nthrb)
                    quant_weight(w2d, 16, 1024, q2, scl2c, 8,
                                 wstf, fsc, thrb, nthrb)
                    _qe.close()

                    if not shared:
                        with nc.Block(no_gpsimd_drain=True) as blk:
                            @blk.tensor
                            def _(e, eidx=eidx):
                                for j in range(16):
                                    e.transpose(ps[0:1, j * 128:(j + 1) * 128],
                                                gates[:, j * 8 + eidx:
                                                      j * 8 + eidx + 1],
                                                ident[:, :])

                        with nc.Block(no_gpsimd_drain=True) as blk:
                            @blk.vector
                            def _(e):
                                e.tensor_copy(grow[0:1, :], ps[0:1, 0:2048])

                    if KDBG and eidx == 0:
                        with nc.Block(no_gpsimd_drain=True) as blk:
                            @blk.vector
                            def _(e):
                                e.tensor_copy(f1[:, 0:16], scl1c[:, :])
                                e.tensor_copy(f1[:, 16:24], scl2c[:, :])
                                e.tensor_copy(f2[:, 0:1024], q1[:, 0:1024])

                        with nc.Block(no_gpsimd_drain=True) as blk:
                            @blk.sync
                            def _(e):
                                dbg_dump(e, "scls", f1[:, 0:24])
                                dbg_dump(e, "q1c0", f2[:, 0:1024])

                    _me = ExitStack()
                    a1_hi = _me.enter_context(
                        nc.sbuf_tensor(f"a1h{eidx}", [128, 16 * 512], BF16))
                    a1_lo = _me.enter_context(
                        nc.sbuf_tensor(f"a1l{eidx}", [128, 16 * 512], BF16))
                    for tp in range(NTAU):
                        done_sac = cnt["sac"]
                        with nc.Block(no_gpsimd_drain=True) as blk:
                            @blk.sync
                            def _(e, tp=tp, eidx=eidx):
                                e.wait_ge(sems["sac"], done_sac)
                                if eidx > 0:
                                    dma(e, accb[:, :],
                                        accd[:, tp * 4096:(tp + 1) * 4096],
                                        "sx")

                            if not shared:
                                @blk.tensor
                                def _(e, tp=tp):
                                    e.matmul(ps[:, 7 * 512:8 * 512],
                                             ones_r[0:1, :],
                                             grow[0:1, tp * TAU:(tp + 1) * TAU],
                                             start=True, stop=True)
                        acc_ready = cnt["sx"]

                        def silu_pair(e, fcp):
                            b0 = (fcp % 2) * 2
                            for k2 in range(2):
                                fc = fcp * 2 + k2
                                e.activation(
                                    f1[:, (fcp % 2) * 1024 + k2 * 512:
                                       (fcp % 2) * 1024 + (k2 + 1) * 512],
                                    ps[:, (b0 + k2) * 512:(b0 + k2 + 1) * 512],
                                    AF.Silu, scale=scl1c[:, fc:fc + 1])

                        def gate_split_pair(e, fcp, shared=shared):
                            fb = (fcp % 2) * 1024
                            srcap = f1[:, fb:fb + 1024]
                            if not shared:
                                e.tensor_tensor(
                                    r3(f2[:, 0:1024], 512), r3(srcap, 512),
                                    ps[:, 7 * 512:8 * 512].unsqueeze(1)
                                    .broadcast_to([128, 2, 512]), ALU.mult)
                                srcap = f2[:, 0:1024]
                            o = fcp * 1024
                            e.tensor_copy(a1_hi[:, o:o + 1024], srcap)
                            e.tensor_tensor(a1_lo[:, o:o + 1024], srcap,
                                            a1_hi[:, o:o + 1024], ALU.subtract)

                        def mm1_pair(e, fcp, tp):
                            b0 = (fcp % 2) * 2
                            for k2 in range(2):
                                fc = fcp * 2 + k2
                                out = ps[:, (b0 + k2) * 512:(b0 + k2 + 1) * 512]
                                for cc in range(NCC):
                                    qsl = q1[:, cc * 2048 + fc * 128:
                                             cc * 2048 + (fc + 1) * 128]
                                    e.matmul(out, qsl,
                                             hv_hi[:, cc, tp * 512:(tp + 1) * 512],
                                             start=(cc == 0), stop=False)
                                    e.matmul(out, qsl,
                                             hv_lo[:, cc, tp * 512:(tp + 1) * 512],
                                             start=False, stop=(cc == NCC - 1))

                        for fcp in range(10):
                            with nc.Block(no_gpsimd_drain=True) as blk:
                                if fcp < 8:
                                    @blk.tensor
                                    def _(e, fcp=fcp, tp=tp):
                                        mm1_pair(e, fcp, tp)
                                if 0 <= fcp - 1 < 8:
                                    @blk.scalar
                                    def _(e, fcp=fcp):
                                        silu_pair(e, fcp - 1)
                                if 0 <= fcp - 2 < 8:
                                    @blk.vector
                                    def _(e, fcp=fcp):
                                        gate_split_pair(e, fcp - 2)

                        def mm2_pair(e, dcp, tp):
                            b0 = 2 + (dcp % 2) * 2
                            for k2 in range(2):
                                dc = dcp * 2 + k2
                                out = ps[:, (b0 + k2) * 512:(b0 + k2 + 1) * 512]
                                for fc in range(16):
                                    qsl = q2[:, fc * 1024 + dc * 128:
                                             fc * 1024 + (dc + 1) * 128]
                                    e.matmul(out, qsl,
                                             a1_hi[:, fc * 512:(fc + 1) * 512],
                                             start=(fc == 0), stop=False)
                                    e.matmul(out, qsl,
                                             a1_lo[:, fc * 512:(fc + 1) * 512],
                                             start=False, stop=(fc == 15))

                        def acc_pair(e, dcp, eidx=eidx, acc_ready=acc_ready):
                            if eidx > 0:
                                e.wait_ge(sems["sx"], acc_ready)
                            b0 = 2 + (dcp % 2) * 2
                            for k2 in range(2):
                                dc = dcp * 2 + k2
                                psl = ps[:, (b0 + k2) * 512:(b0 + k2 + 1) * 512]
                                asl = accb[:, dc * 512:(dc + 1) * 512]
                                if eidx == 0:
                                    e.tensor_scalar(asl, psl,
                                                    scl2c[:, dc:dc + 1],
                                                    None, ALU.mult)
                                else:
                                    e.scalar_tensor_tensor(
                                        asl, psl, scl2c[:, dc:dc + 1], asl,
                                        ALU.mult, ALU.add)

                        for dcp in range(5):
                            with nc.Block(no_gpsimd_drain=True) as blk:
                                if dcp < 4:
                                    @blk.tensor
                                    def _(e, dcp=dcp, tp=tp):
                                        mm2_pair(e, dcp, tp)
                                if dcp >= 1:
                                    @blk.vector
                                    def _(e, dcp=dcp):
                                        acc_pair(e, dcp - 1)

                        with nc.Block(no_gpsimd_drain=True) as blk:
                            @blk.sync
                            def _(e, tp=tp):
                                dma(e, accd[:, tp * 4096:(tp + 1) * 4096],
                                    accb[:, :], "sac")
                    _me.close()

            with nc.Block(no_gpsimd_drain=True) as blk:
                @blk.sync
                def _(e):
                    wt(e, "sac")

            for j in range(16):
                with nc.Block(no_gpsimd_drain=True) as blk:
                    @blk.sync
                    def _(e, j=j):
                        tp, sub = j // 4, j % 4
                        dma(e, r3(f1[:, 0:1024], 128),
                            accd[:, tp * 4096:(tp + 1) * 4096]
                            .rearrange("p (dc x) -> p dc x", x=512)
                            [:, :, sub * 128:(sub + 1) * 128], "sx")
                        dma(e, f2[:, :], x1sp[j * T:(j + 1) * T, :], "sx")

                with nc.Block(no_gpsimd_drain=True) as blk:
                    @blk.vector
                    def _(e):
                        wt(e, "sx")

                with nc.Block(no_gpsimd_drain=True) as blk:
                    @blk.tensor
                    def _(e):
                        for dc in range(NCC):
                            e.transpose(ps[:, dc * 128:(dc + 1) * 128],
                                        f1[:, dc * 128:(dc + 1) * 128],
                                        ident[:, :])

                with nc.Block(no_gpsimd_drain=True) as blk:
                    @blk.vector
                    def _(e):
                        e.tensor_copy(f1[:, 0:1024], ps[:, 0:1024])

                if KDBG and j == 0:
                    with nc.Block(no_gpsimd_drain=True) as blk:
                        @blk.sync
                        def _(e):
                            dbg_dump(e, "ffnacc0", f1[:, 0:1024])

                with nc.Block(no_gpsimd_drain=True) as blk:
                    @blk.sync
                    def _(e):
                        wt(e, "sout")

                ln_plain(f1[:, 0:1024], f3[:, :], f4[:, :])
                with nc.Block(no_gpsimd_drain=True) as blk:
                    @blk.vector
                    def _(e):
                        e.tensor_tensor(f3[:, :], f3[:, :], f2[:, :], ALU.add)

                with nc.Block(no_gpsimd_drain=True) as blk:
                    @blk.sync
                    def _(e, j=j):
                        dma(e, xout[j * T:(j + 1) * T, :], f3[:, :], "sout")

            with nc.Block(no_gpsimd_drain=True) as blk:
                @blk.sync
                def _(e):
                    wt(e, "sout")

    return nc


_NC_CACHE = None


def kernel(**inputs):
    global _NC_CACHE
    if _NC_CACHE is None:
        _NC_CACHE = build_nc()
    nc = _NC_CACHE

    x = np.asarray(inputs["L0_attn_xq"], dtype=np.float32)
    tm = {m: np.asarray(inputs[f"L0_attn_time_mix_{m}"], np.float32).reshape(C)
          for m in ("k", "v", "r", "g")}
    tmix = np.zeros((128, 32), np.float32)
    for mi, m in enumerate(("k", "v", "r", "g")):
        tmix[:, mi * 8:(mi + 1) * 8] = tm[m].reshape(8, 128).T
    tdu = np.concatenate([
        np.asarray(inputs["L0_attn_time_decay"], np.float32),
        np.asarray(inputs["L0_attn_time_faaaa"], np.float32)]).reshape(1, 32)
    lnrow = np.concatenate([
        np.asarray(inputs["L0_attn_ln_w"], np.float32),
        np.asarray(inputs["L0_attn_ln_b"], np.float32)]).reshape(1, 2048)
    gbrow = np.concatenate([
        np.asarray(inputs["L0_ffn_ln_gamma"], np.float32),
        np.asarray(inputs["L0_ffn_ln_beta"], np.float32)]).reshape(1, 2048)

    def t2(a):
        return np.ascontiguousarray(np.asarray(a, np.float32).T)

    com = {
        "wrT": t2(inputs["L0_attn_W_r"]), "wkT": t2(inputs["L0_attn_W_k"]),
        "wvT": t2(inputs["L0_attn_W_v"]), "wgT": t2(inputs["L0_attn_W_g"]),
        "woT": t2(inputs["L0_attn_W_o"]),
        "tmix": tmix, "tdu": tdu, "lnrow": lnrow, "gbrow": gbrow,
        "sw1T": t2(inputs["L0_ffn_shared_w1"]),
        "sw2T": t2(inputs["L0_ffn_shared_w2"]),
        "ew1T": np.ascontiguousarray(
            np.asarray(inputs["L0_ffn_expert_w1"], np.float32)
            .transpose(0, 2, 1)),
        "ew2T": np.ascontiguousarray(
            np.asarray(inputs["L0_ffn_expert_w2"], np.float32)
            .transpose(0, 2, 1)),
        "rwT": t2(inputs["L0_ffn_router_w"]),
        "ident": np.eye(128, dtype=np.float32),
    }

    in_maps = []
    for core in range(8):
        b, half = core // 2, core % 2
        own = x[b, half * OWNTOK:(half + 1) * OWNTOK, :]
        ctx = (x[b, 0:OWNTOK, :] if half == 1
               else np.zeros((OWNTOK, C), np.float32))
        m = dict(com)
        m["xin"] = np.ascontiguousarray(np.concatenate([ctx, own], axis=0))
        in_maps.append(m)

    res = run_bass_kernel_spmd(nc, in_maps, core_ids=list(range(8)))
    out = np.empty((B, TT, C), np.float32)
    for core in range(8):
        b, half = core // 2, core % 2
        out[b, half * OWNTOK:(half + 1) * OWNTOK, :] = res.results[core]["xout"]
    return out


# version 08
# speedup vs baseline: 1.2314x; 1.0279x over previous
import os
import numpy as np
import concourse.bass as bass
import concourse.mybir as mybir
from concourse.bass_utils import run_bass_kernel_spmd

F32 = mybir.dt.float32
BF16 = mybir.dt.bfloat16
FP8 = mybir.dt.float8e4
ALU = mybir.AluOpType
AF = mybir.ActivationFunctionType
AX = mybir.AxisListType

B, TT, C = 4, 4096, 1024
H, T, K = 16, 128, 64
NCC = C // 128
OWN0 = 16
OWNTOK = 2048
FDIM = 2048
GEPS = 64e-5
EPS = 1e-5
TAU = 512
NTAU = OWNTOK // TAU

M_K, M_V, M_R, M_G = 0, 1, 2, 3
KPART = os.environ.get("KPART", "full")
KDBG = int(os.environ.get("KDBG", "0"))
KCTX = int(os.environ.get("KCTX", "16"))
KGRP = int(os.environ.get("KGRP", "8"))
KEXP = int(os.environ.get("KEXP", "9"))


def build_nc():
    nc = bass.Bass("TRN2", target_bir_lowering=False)

    xin = nc.dram_tensor("xin", [TT, C], F32, kind="ExternalInput").ap()
    wT = {}
    for nm in ("wr", "wk", "wv", "wg", "wo"):
        wT[nm] = nc.dram_tensor(nm + "T", [C, C], F32, kind="ExternalInput").ap()
    tmix_d = nc.dram_tensor("tmix", [128, 32], F32, kind="ExternalInput").ap()
    tdu_d = nc.dram_tensor("tdu", [1, 32], F32, kind="ExternalInput").ap()
    lnrow_d = nc.dram_tensor("lnrow", [1, 2048], F32, kind="ExternalInput").ap()
    gbrow_d = nc.dram_tensor("gbrow", [1, 2048], F32, kind="ExternalInput").ap()
    sw1T = nc.dram_tensor("sw1T", [C, FDIM], F32, kind="ExternalInput").ap()
    sw2T = nc.dram_tensor("sw2T", [FDIM, C], F32, kind="ExternalInput").ap()
    ew1T = nc.dram_tensor("ew1T", [8, C, FDIM], F32, kind="ExternalInput").ap()
    ew2T = nc.dram_tensor("ew2T", [8, FDIM, C], F32, kind="ExternalInput").ap()
    rwT_d = nc.dram_tensor("rwT", [C, 8], F32, kind="ExternalInput").ap()
    ident_d = nc.dram_tensor("ident", [128, 128], F32, kind="ExternalInput").ap()
    x1sp = nc.dram_tensor("x1sp", [OWNTOK, C], F32,
                          **({"kind": "ExternalOutput"} if KDBG else {})).ap()
    accd = nc.dram_tensor("accd", [128, 8 * OWNTOK], F32,
                          **({"kind": "ExternalOutput"} if KDBG else {})).ap()
    xout = nc.dram_tensor("xout", [OWNTOK, C], F32, kind="ExternalOutput").ap()

    sems = {}
    cnt = {}
    dbg_outs = {}

    def mkdbg(name, shape):
        dbg_outs[name] = nc.dram_tensor("dbg_" + name, shape, F32,
                                        kind="ExternalOutput").ap()
        return dbg_outs[name]

    def dma(e, out, in_, s):
        e.dma_start(out=out, in_=in_).then_inc(sems[s], 16)
        cnt[s] += 16

    def wt(e, s):
        e.wait_ge(sems[s], cnt[s])

    def dbg_dump(e, name, ap):
        if not KDBG:
            return
        if name not in dbg_outs:
            mkdbg(name, list(ap.shape))
        dma(e, dbg_outs[name][tuple(slice(0, s) for s in ap.shape)], ap, "sout")

    def rep3(ap_cols, blk):
        return ap_cols.unsqueeze(2).broadcast_to([128, ap_cols.shape[1], blk])

    def r3(ap2d, blk):
        return ap2d.rearrange("p (n b) -> p n b", b=blk)

    from contextlib import ExitStack
    with ExitStack() as _g:
        _sw = _g.enter_context(nc.semaphore("sw"))
        _sx = _g.enter_context(nc.semaphore("sx"))
        _ssp = _g.enter_context(nc.semaphore("ssp"))
        _sac = _g.enter_context(nc.semaphore("sac"))
        _sout = _g.enter_context(nc.semaphore("sout"))
        _svf = _g.enter_context(nc.semaphore("svf"))
        ident = _g.enter_context(nc.sbuf_tensor("ident_s", [128, 128], F32))
        ones_r = _g.enter_context(nc.sbuf_tensor("ones_r", [1, 128], F32))
        onescol = _g.enter_context(nc.sbuf_tensor("onescol", [128, 1], F32))
        tmv = _g.enter_context(nc.sbuf_tensor("tmv", [128, 32], F32))
        gates = _g.enter_context(nc.sbuf_tensor("gates", [128, 128], F32))
        stat = _g.enter_context(nc.sbuf_tensor("stat", [128, 12], F32))
        epsc = _g.enter_context(nc.sbuf_tensor("epsc", [128, 2], F32))
        ps = _g.enter_context(nc.psum_tensor("ps", [128, 4096], F32))
        sems.update(sw=_sw, sx=_sx, ssp=_ssp, sac=_sac, sout=_sout, vf=_svf)
        cnt.update(sw=0, sx=0, ssp=0, sac=0, sout=0, vf=0)

        def vfence(e, inst):
            inst.then_inc(sems["vf"], 1)
            cnt["vf"] += 1
            e.wait_ge(sems["vf"], cnt["vf"])

        def bank(i):
            return ps[:, i * 512:(i + 1) * 512]

        def bcast_row(row_ap, dst_sbuf_ap, width):
            with nc.Block(no_gpsimd_drain=True) as blk:
                @blk.tensor
                def _(e):
                    for sgi in range(width // 512):
                        e.matmul(ps[:, sgi * 512:sgi * 512 + 512],
                                 ones_r[0:1, :],
                                 row_ap[0:1, sgi * 512:(sgi + 1) * 512],
                                 start=True, stop=True)

            with nc.Block(no_gpsimd_drain=True) as blk:
                @blk.vector
                def _(e):
                    e.tensor_copy(dst_sbuf_ap, ps[:, 0:width])

        def ln_plain(xap, outap, scr, eps_ap=None):
            with nc.Block(no_gpsimd_drain=True) as blk:
                @blk.vector
                def _(e):
                    e.reduce_sum(stat[:, 0:1], xap, axis=AX.X)
                    e.tensor_tensor(scr, xap, xap, ALU.mult)
                    e.reduce_sum(stat[:, 1:2], scr, axis=AX.X)
                    vfence(e, e.tensor_scalar(stat[:, 0:1], stat[:, 0:1],
                                               1.0 / 1024, None, ALU.mult))
                    vfence(e, e.tensor_tensor(stat[:, 2:3], stat[:, 0:1],
                                              stat[:, 0:1], ALU.mult))
                    e.scalar_tensor_tensor(stat[:, 2:3], stat[:, 1:2], 1.0 / 1024,
                                           stat[:, 2:3], ALU.mult, ALU.subtract)

            with nc.Block(no_gpsimd_drain=True) as blk:
                @blk.scalar
                def _(e):
                    e.activation(stat[:, 3:4], stat[:, 2:3], AF.Sqrt,
                                 bias=(eps_ap if eps_ap is not None
                                       else epsc[:, 0:1]))

            with nc.Block(no_gpsimd_drain=True) as blk:
                @blk.vector
                def _(e):
                    vfence(e, e.reciprocal(stat[:, 2:3], stat[:, 3:4]))
                    e.tensor_scalar(outap, xap, stat[:, 0:1], stat[:, 2:3],
                                    ALU.subtract, ALU.mult)

        with ExitStack() as _a:
            wA_hi = _a.enter_context(nc.sbuf_tensor("wA_hi", [128, 8192], BF16))
            wA_lo = _a.enter_context(nc.sbuf_tensor("wA_lo", [128, 8192], BF16))
            wB_hi = _a.enter_context(nc.sbuf_tensor("wB_hi", [128, 8192], BF16))
            wB_lo = _a.enter_context(nc.sbuf_tensor("wB_lo", [128, 8192], BF16))
            wstage = _a.enter_context(nc.sbuf_tensor("wstage", [128, 4096], F32))
            x_tm = _a.enter_context(nc.sbuf_tensor("x_tm", [128, 1024], F32))
            xln_tm = _a.enter_context(nc.sbuf_tensor("xln_tm", [128, 1024], F32))
            xlnf = _a.enter_context(nc.sbuf_tensor("xlnf", [128, 2048], F32))
            mix_hi = _a.enter_context(nc.sbuf_tensor("mix_hi", [128, 1024], BF16))
            mix_lo = _a.enter_context(nc.sbuf_tensor("mix_lo", [128, 1024], BF16))
            dxf = _a.enter_context(nc.sbuf_tensor("dxf", [128, 1024], F32))
            s1 = _a.enter_context(nc.sbuf_tensor("s1", [128, 1024], F32))
            s2 = _a.enter_context(nc.sbuf_tensor("s2", [128, 1024], F32))
            s3 = _a.enter_context(nc.sbuf_tensor("s3", [128, 1024], F32))
            sbig = _a.enter_context(nc.sbuf_tensor("sbig", [128, 1024], F32))
            kwk_hi = _a.enter_context(nc.sbuf_tensor("kwk_hi", [128, 1024], BF16))
            kwk_lo = _a.enter_context(nc.sbuf_tensor("kwk_lo", [128, 1024], BF16))
            kfm_hi = _a.enter_context(nc.sbuf_tensor("kfm_hi", [64, 4096], BF16))
            kfm_lo = _a.enter_context(nc.sbuf_tensor("kfm_lo", [64, 4096], BF16))
            vtm_hi = _a.enter_context(nc.sbuf_tensor("vtm_hi", [128, 2048], BF16))
            vtm_lo = _a.enter_context(nc.sbuf_tensor("vtm_lo", [128, 2048], BF16))
            ssn_hi = _a.enter_context(nc.sbuf_tensor("ssn_hi", [64, 2048], BF16))
            ssn_lo = _a.enter_context(nc.sbuf_tensor("ssn_lo", [64, 2048], BF16))
            rfm_hi = _a.enter_context(nc.sbuf_tensor("rfm_hi", [64, 2048], BF16))
            rfm_lo = _a.enter_context(nc.sbuf_tensor("rfm_lo", [64, 2048], BF16))
            atw_hi = _a.enter_context(nc.sbuf_tensor("atw_hi", [128, 1024], BF16))
            atw_lo = _a.enter_context(nc.sbuf_tensor("atw_lo", [128, 1024], BF16))
            yg_hi = _a.enter_context(nc.sbuf_tensor("yg_hi", [128, 2048], BF16))
            yg_lo = _a.enter_context(nc.sbuf_tensor("yg_lo", [128, 2048], BF16))
            s_f32 = _a.enter_context(nc.sbuf_tensor("s_f32", [64, 1024], F32))
            s_hi = _a.enter_context(nc.sbuf_tensor("s_hi", [64, 1024], BF16))
            s_lo = _a.enter_context(nc.sbuf_tensor("s_lo", [64, 1024], BF16))
            wmT = _a.enter_context(nc.sbuf_tensor("wmT", [128, 2048], F32))
            bcol = _a.enter_context(nc.sbuf_tensor("bcol", [128, 8], F32))
            bhist = _a.enter_context(nc.sbuf_tensor("bhist", [128, 16], F32))
            lndb = _a.enter_context(nc.sbuf_tensor("lndb", [128, 16], F32))
            ub = _a.enter_context(nc.sbuf_tensor("ub", [128, 16], F32))
            wb_t = _a.enter_context(nc.sbuf_tensor("wb_t", [128, 16], F32))
            wk_t = _a.enter_context(nc.sbuf_tensor("wk_t", [128, 16], F32))
            ws_p = _a.enter_context(nc.sbuf_tensor("ws_p", [64, 16], F32))
            lnw_tm = _a.enter_context(nc.sbuf_tensor("lnw_tm", [128, 1024], F32))
            lnb_tm = _a.enter_context(nc.sbuf_tensor("lnb_tm", [128, 1024], F32))
            iotac = _a.enter_context(nc.sbuf_tensor("iotac", [128, 2], F32))
            csc = _a.enter_context(nc.sbuf_tensor("csc", [128, 48], F32))
            dbgst = (_a.enter_context(nc.sbuf_tensor("dbgst", [128, 1024], F32))
                     if KDBG else None)
            with nc.Block() as blk:
                @blk.sync
                def _(e):
                    dma(e, ident[:, :], ident_d[:, :], "sw")
                    dma(e, tmv[:, :], tmix_d[:, :], "sw")
                    dma(e, csc[0:1, 0:32], tdu_d[0:1, :], "sw")
                    dma(e, wstage[0:1, 0:2048], lnrow_d[0:1, :], "sw")

                @blk.gpsimd
                def _(e):
                    e.memset(ones_r[0:1, :], 1.0)
                    e.memset(onescol[:, :], 1.0)
                    e.memset(epsc[:, 0:1], EPS)
                    e.memset(epsc[:, 1:2], GEPS)
                    e.memset(s_f32[:, :], 0.0)
                    e.memset(s_hi[:, :], 0.0)
                    e.memset(s_lo[:, :], 0.0)
                    e.memset(bcol[:, :], 0.0)
                    e.iota(sbig[:, 0:128], [[1, 128]], channel_multiplier=-1,
                           allow_small_or_imprecise_dtypes=True)
                    e.iota(iotac[:, 0:1], [[1, 1]], channel_multiplier=1,
                           allow_small_or_imprecise_dtypes=True)

            with nc.Block(no_gpsimd_drain=True) as blk:
                @blk.scalar
                def _(e):
                    wt(e, "sw")
                    e.activation(csc[0:1, 0:16], csc[0:1, 0:16], AF.Exp)

                @blk.vector
                def _(e):
                    wt(e, "sw")
                    e.tensor_scalar(iotac[:, 1:2], iotac[:, 0:1], -1.0, 127.0,
                                    ALU.mult, ALU.add)
                    e.tensor_scalar(s1[:, 0:128], sbig[:, 0:128], 1.0, 0.0,
                                    ALU.subtract, ALU.max)
                    e.tensor_scalar(s2[:, 0:128], sbig[:, 0:128], 1.0, None,
                                    ALU.is_ge)
                    e.tensor_scalar(s3[:, 0:128], sbig[:, 0:128], 0.0, None,
                                    ALU.is_equal)

            with nc.Block(no_gpsimd_drain=True) as blk:
                @blk.vector
                def _(e):
                    e.tensor_scalar(csc[0:1, 0:16], csc[0:1, 0:16], -1.0, None,
                                    ALU.mult)

            with nc.Block(no_gpsimd_drain=True) as blk:
                @blk.tensor
                def _(e):
                    e.matmul(ps[:, 0:16], ones_r[0:1, :], csc[0:1, 0:16],
                             start=True, stop=False)
                    e.matmul(ps[:, 16:32], ones_r[0:1, :], csc[0:1, 16:32],
                             start=False, stop=True)

            with nc.Block(no_gpsimd_drain=True) as blk:
                @blk.vector
                def _(e):
                    e.tensor_copy(lndb[:, :], ps[:, 0:16])
                    e.tensor_copy(ub[:, :], ps[:, 16:32])

            with nc.Block(no_gpsimd_drain=True) as blk:
                @blk.scalar
                def _(e):
                    for h in range(H):
                        e.activation(wb_t[:, h:h + 1], iotac[:, 0:1], AF.Exp,
                                     scale=lndb[:, h:h + 1])
                        e.activation(wk_t[:, h:h + 1], iotac[:, 1:2], AF.Exp,
                                     scale=lndb[:, h:h + 1])
                    e.activation(csc[:, 0:16], lndb[:, :], AF.Exp, scale=128.0)
                    for h in range(H):
                        e.activation(wmT[:, h * 128:(h + 1) * 128], s1[:, 0:128],
                                     AF.Exp, scale=lndb[:, h:h + 1])

            with nc.Block(no_gpsimd_drain=True) as blk:
                @blk.vector
                def _(e):
                    e.tensor_copy(ws_p[0:64, :], csc[0:64, 0:16])
                    for h in range(H):
                        sl = wmT[:, h * 128:(h + 1) * 128]
                        e.tensor_tensor(sl, sl, s2[:, 0:128], ALU.mult)
                        e.scalar_tensor_tensor(sl, s3[:, 0:128], ub[:, h:h + 1],
                                               sl, ALU.mult, ALU.add)

            bcast_row(wstage[0:1, 0:1024], lnw_tm[:, :], 1024)
            bcast_row(wstage[0:1, 1024:2048], lnb_tm[:, :], 1024)

            def load_wsplit(dsth, dstl, dram):
                def _dma(e, step):
                    srcap = dram[step * 256:(step + 1) * 256, :].rearrange(
                        "(cc p) w -> p cc w", p=128)
                    dma(e, r3(wstage[:, (step % 2) * 2048:
                                     (step % 2) * 2048 + 2048], 1024),
                        srcap, "sw")

                def _split(e, step, when):
                    e.wait_ge(sems["sw"], when)
                    o = step * 2048
                    wsl = wstage[:, (step % 2) * 2048:(step % 2) * 2048 + 2048]
                    e.tensor_copy(dsth[:, o:o + 2048], wsl)
                    e.tensor_tensor(dstl[:, o:o + 2048], wsl,
                                    dsth[:, o:o + 2048], ALU.subtract)

                with nc.Block(no_gpsimd_drain=True) as blk:
                    @blk.sync
                    def _(e):
                        _dma(e, 0)
                for step in range(3):
                    done = cnt["sw"]
                    with nc.Block(no_gpsimd_drain=True) as blk:
                        @blk.sync
                        def _(e, step=step):
                            _dma(e, step + 1)

                        @blk.vector
                        def _(e, step=step, done=done):
                            _split(e, step, done)
                done = cnt["sw"]
                with nc.Block(no_gpsimd_drain=True) as blk:
                    @blk.vector
                    def _(e, done=done):
                        _split(e, 3, done)

            def mix_split(slot, m):
                cur = r3(xlnf[:, slot * 1024:(slot + 1) * 1024], 128)
                bh = bhist[:, slot * 8:slot * 8 + 8].unsqueeze(2)
                tm3 = rep3(tmv[:, m * 8:(m + 1) * 8], 128)
                with nc.Block(no_gpsimd_drain=True) as blk:
                    @blk.vector
                    def _(e):
                        e.tensor_tensor(r3(s2[:, :], 128), r3(dxf[:, :], 128),
                                        tm3, ALU.mult)
                        e.tensor_tensor(r3(s3[:, :], 128)[:, :, 1:128],
                                        cur[:, :, 0:127],
                                        r3(s2[:, :], 128)[:, :, 1:128], ALU.add)
                        e.tensor_tensor(r3(s3[:, :], 128)[:, :, 0:1], bh,
                                        r3(s2[:, :], 128)[:, :, 0:1], ALU.add)
                        e.tensor_copy(mix_hi[:, :], s3[:, :])
                        e.tensor_tensor(mix_lo[:, :], s3[:, :], mix_hi[:, :],
                                        ALU.subtract)

            def proj_tm(whi, wlo, pbank0):
                with nc.Block(no_gpsimd_drain=True) as blk:
                    @blk.tensor
                    def _(e):
                        for half in range(2):
                            out = bank(pbank0 + half)
                            first = True
                            for cc in range(NCC):
                                a_hi = mix_hi[:, cc * 128:(cc + 1) * 128]
                                a_lo = mix_lo[:, cc * 128:(cc + 1) * 128]
                                woff = cc * 1024 + half * 512
                                e.matmul(out, a_hi, whi[:, woff:woff + 512],
                                         start=first, stop=False)
                                first = False
                                e.matmul(out, a_hi, wlo[:, woff:woff + 512],
                                         start=False, stop=False)
                                e.matmul(out, a_lo, whi[:, woff:woff + 512],
                                         start=False, stop=(cc == NCC - 1))

            def transpose8(src, pbank0):
                with nc.Block(no_gpsimd_drain=True) as blk:
                    @blk.tensor
                    def _(e):
                        for oc in range(NCC):
                            e.transpose(ps[:, pbank0 * 512 + oc * 128:
                                           pbank0 * 512 + (oc + 1) * 128],
                                        src[:, oc * 128:(oc + 1) * 128],
                                        ident[:, :])

            def chunk_part1(n, slot, is_ctx):
                tok0 = n * T
                with nc.Block(no_gpsimd_drain=True) as blk:
                    @blk.sync
                    def _(e):
                        dma(e, x_tm[:, :], xin[tok0:tok0 + T, :], "sx")

                with nc.Block(no_gpsimd_drain=True) as blk:
                    @blk.vector
                    def _(e):
                        wt(e, "sx")

                    @blk.sync
                    def _(e):
                        wt(e, "ssp")

                ln_plain(x_tm[:, :], xln_tm[:, :], s1[:, :])
                transpose8(xln_tm, 0)
                with nc.Block(no_gpsimd_drain=True) as blk:
                    @blk.vector
                    def _(e):
                        e.tensor_copy(bhist[:, slot * 8:slot * 8 + 8], bcol[:, :])
                        e.tensor_copy(xlnf[:, slot * 1024:(slot + 1) * 1024],
                                      ps[:, 0:1024])

                with nc.Block(no_gpsimd_drain=True) as blk:
                    @blk.vector
                    def _(e):
                        cur = r3(xlnf[:, slot * 1024:(slot + 1) * 1024], 128)
                        e.tensor_copy(bcol[:, :].unsqueeze(2),
                                      cur[:, :, 127:128])
                        e.tensor_tensor(r3(dxf[:, :], 128)[:, :, 1:128],
                                        cur[:, :, 1:128], cur[:, :, 0:127],
                                        ALU.subtract)
                        e.tensor_tensor(r3(dxf[:, :], 128)[:, :, 0:1],
                                        cur[:, :, 0:1],
                                        bhist[:, slot * 8:slot * 8 + 8]
                                        .unsqueeze(2), ALU.subtract)

                if KDBG and n == 16:
                    with nc.Block(no_gpsimd_drain=True) as blk:
                        @blk.sync
                        def _(e):
                            dbg_dump(e, "xln_tm", xln_tm[:, :])
                            dbg_dump(e, "dxf", dxf[:, :])
                mix_split(slot, M_K)
                if KDBG and n == 16:
                    with nc.Block(no_gpsimd_drain=True) as blk:
                        @blk.sync
                        def _(e):
                            dbg_dump(e, "xmk", s3[:, :])
                proj_tm(wA_hi, wA_lo, 0)
                with nc.Block(no_gpsimd_drain=True) as blk:
                    @blk.vector
                    def _(e):
                        e.tensor_copy(s1[:, :], ps[:, 0:1024])

                if KDBG and n == 16:
                    with nc.Block(no_gpsimd_drain=True) as blk:
                        @blk.sync
                        def _(e):
                            dbg_dump(e, "k_tm", s1[:, :])

                with nc.Block(no_gpsimd_drain=True) as blk:
                    @blk.vector
                    def _(e):
                        e.tensor_tensor(r3(s2[:, :], 64), r3(s1[:, :], 64),
                                        rep3(wk_t[:, :], 64), ALU.mult)
                        e.tensor_copy(kwk_hi[:, :], s2[:, :])
                        e.tensor_tensor(kwk_lo[:, :], s2[:, :], kwk_hi[:, :],
                                        ALU.subtract)

                if not is_ctx:
                    with nc.Block(no_gpsimd_drain=True) as blk:
                        @blk.tensor
                        def _(e):
                            for h in range(H):
                                e.transpose(ps[0:64, 1024 + h * 128:
                                               1024 + (h + 1) * 128],
                                            s1[:, h * 64:(h + 1) * 64],
                                            ident[:, :])

                    with nc.Block(no_gpsimd_drain=True) as blk:
                        @blk.vector
                        def _(e):
                            o = slot * 2048
                            e.tensor_copy(kfm_hi[0:64, o:o + 2048],
                                          ps[0:64, 1024:3072])
                            e.tensor_tensor(kfm_lo[0:64, o:o + 2048],
                                            ps[0:64, 1024:3072],
                                            kfm_hi[0:64, o:o + 2048],
                                            ALU.subtract)

                mix_split(slot, M_V)
                proj_tm(wB_hi, wB_lo, 4)
                with nc.Block(no_gpsimd_drain=True) as blk:
                    @blk.vector
                    def _(e):
                        o = slot * 1024
                        e.tensor_copy(vtm_hi[:, o:o + 1024], ps[:, 2048:3072])
                        e.tensor_tensor(vtm_lo[:, o:o + 1024], ps[:, 2048:3072],
                                        vtm_hi[:, o:o + 1024], ALU.subtract)

                if KDBG and n == 16:
                    with nc.Block(no_gpsimd_drain=True) as blk:
                        @blk.vector
                        def _(e):
                            o = slot * 1024
                            e.tensor_tensor(dbgst[:, 0:1024], vtm_hi[:, o:o + 1024],
                                            vtm_lo[:, o:o + 1024], ALU.add)

                    with nc.Block(no_gpsimd_drain=True) as blk:
                        @blk.sync
                        def _(e):
                            dbg_dump(e, "v_tm", dbgst[:, 0:1024])
                if not is_ctx:
                    with nc.Block(no_gpsimd_drain=True) as blk:
                        @blk.vector
                        def _(e):
                            o = slot * 1024
                            e.tensor_copy(ssn_hi[0:64, o:o + 1024], s_hi[0:64, :])
                            e.tensor_copy(ssn_lo[0:64, o:o + 1024], s_lo[0:64, :])

                with nc.Block(no_gpsimd_drain=True) as blk:
                    @blk.tensor
                    def _(e):
                        for h in range(H):
                            o = slot * 1024 + h * 64
                            outsl = ps[0:64, 6 * 512 + h * 64:
                                       6 * 512 + (h + 1) * 64]
                            e.matmul(outsl, kwk_hi[:, h * 64:h * 64 + 64],
                                     vtm_hi[:, o:o + 64],
                                     start=(h % 8 == 0), stop=False)
                            e.matmul(outsl, kwk_hi[:, h * 64:h * 64 + 64],
                                     vtm_lo[:, o:o + 64], start=False,
                                     stop=False)
                            e.matmul(outsl, kwk_lo[:, h * 64:h * 64 + 64],
                                     vtm_hi[:, o:o + 64], start=False,
                                     stop=(h % 8 == 7))

                with nc.Block(no_gpsimd_drain=True) as blk:
                    @blk.vector
                    def _(e):
                        e.tensor_tensor(
                            s_f32[0:64, :].rearrange("p (n b) -> p n b", b=64),
                            s_f32[0:64, :].rearrange("p (n b) -> p n b", b=64),
                            ws_p[0:64, :].unsqueeze(2)
                            .broadcast_to([64, 16, 64]), ALU.mult)
                        e.tensor_tensor(s_f32[0:64, :], s_f32[0:64, :],
                                        ps[0:64, 6 * 512:8 * 512], ALU.add)
                        e.tensor_copy(s_hi[0:64, :], s_f32[0:64, :])
                        e.tensor_tensor(s_lo[0:64, :], s_f32[0:64, :],
                                        s_hi[0:64, :], ALU.subtract)

                if KDBG and n == 16:
                    with nc.Block(no_gpsimd_drain=True) as blk:
                        @blk.sync
                        def _(e):
                            dbg_dump(e, "s_next", s_f32[0:64, :])

            def chunk_part2(n, slot):
                with nc.Block(no_gpsimd_drain=True) as blk:
                    @blk.vector
                    def _(e):
                        cur = r3(xlnf[:, slot * 1024:(slot + 1) * 1024], 128)
                        e.tensor_tensor(r3(dxf[:, :], 128)[:, :, 1:128],
                                        cur[:, :, 1:128], cur[:, :, 0:127],
                                        ALU.subtract)
                        e.tensor_tensor(r3(dxf[:, :], 128)[:, :, 0:1],
                                        cur[:, :, 0:1],
                                        bhist[:, slot * 8:slot * 8 + 8]
                                        .unsqueeze(2), ALU.subtract)

                mix_split(slot, M_R)
                proj_tm(wA_hi, wA_lo, 0)
                with nc.Block(no_gpsimd_drain=True) as blk:
                    @blk.vector
                    def _(e):
                        e.tensor_copy(s1[:, :], ps[:, 0:1024])

                with nc.Block(no_gpsimd_drain=True) as blk:
                    @blk.tensor
                    def _(e):
                        for h in range(H):
                            e.transpose(ps[0:64, 1024 + h * 128:
                                           1024 + (h + 1) * 128],
                                        s1[:, h * 64:(h + 1) * 64], ident[:, :])

                with nc.Block(no_gpsimd_drain=True) as blk:
                    @blk.vector
                    def _(e):
                        e.tensor_copy(rfm_hi[0:64, :], ps[0:64, 1024:3072])
                        e.tensor_tensor(rfm_lo[0:64, :], ps[0:64, 1024:3072],
                                        rfm_hi[0:64, :], ALU.subtract)

                mix_split(slot, M_G)
                proj_tm(wB_hi, wB_lo, 4)
                with nc.Block(no_gpsimd_drain=True) as blk:
                    @blk.scalar
                    def _(e):
                        e.activation(s3[:, :], ps[:, 2048:3072], AF.Silu)

                if KDBG and n == 16:
                    with nc.Block(no_gpsimd_drain=True) as blk:
                        @blk.sync
                        def _(e):
                            dbg_dump(e, "g_tm", s3[:, :])
                for hh in range(2):
                    with nc.Block(no_gpsimd_drain=True) as blk:
                        @blk.tensor
                        def _(e, hh=hh):
                            for hi_ in range(8):
                                h = hh * 8 + hi_
                                ko = slot * 2048 + h * 128
                                ro = h * 128
                                out = ps[:, hi_ * 128:(hi_ + 1) * 128]
                                e.matmul(out, kfm_hi[0:64, ko:ko + 128],
                                         rfm_hi[0:64, ro:ro + 128],
                                         start=(hi_ % 4 == 0), stop=False)
                                e.matmul(out, kfm_hi[0:64, ko:ko + 128],
                                         rfm_lo[0:64, ro:ro + 128],
                                         start=False, stop=False)
                                e.matmul(out, kfm_lo[0:64, ko:ko + 128],
                                         rfm_hi[0:64, ro:ro + 128],
                                         start=False, stop=(hi_ % 4 == 3))

                    with nc.Block(no_gpsimd_drain=True) as blk:
                        @blk.vector
                        def _(e, hh=hh):
                            e.tensor_tensor(sbig[:, :], ps[:, 0:1024],
                                            wmT[:, hh * 1024:(hh + 1) * 1024],
                                            ALU.mult)
                            e.tensor_copy(atw_hi[:, :], sbig[:, :])
                            e.tensor_tensor(atw_lo[:, :], sbig[:, :],
                                            atw_hi[:, :], ALU.subtract)

                    if KDBG and n == 16 and hh == 0:
                        with nc.Block(no_gpsimd_drain=True) as blk:
                            @blk.sync
                            def _(e):
                                dbg_dump(e, "attw0", sbig[:, :])

                    with nc.Block(no_gpsimd_drain=True) as blk:
                        @blk.tensor
                        def _(e, hh=hh):
                            for hi_ in range(8):
                                h = hh * 8 + hi_
                                vo = slot * 1024 + h * 64
                                ro = h * 128
                                so = slot * 1024 + h * 64
                                o1 = ps[:, 2 * 512 + h * 64:2 * 512 + h * 64 + 64]
                                e.matmul(o1, atw_hi[:, hi_ * 128:(hi_ + 1) * 128],
                                         vtm_hi[:, vo:vo + 64],
                                         start=(hi_ == 0), stop=False)
                                e.matmul(o1, atw_hi[:, hi_ * 128:(hi_ + 1) * 128],
                                         vtm_lo[:, vo:vo + 64], start=False,
                                         stop=False)
                                e.matmul(o1, atw_lo[:, hi_ * 128:(hi_ + 1) * 128],
                                         vtm_hi[:, vo:vo + 64], start=False,
                                         stop=(hi_ == 7))
                                o2 = ps[:, 4 * 512 + h * 64:4 * 512 + h * 64 + 64]
                                e.matmul(o2, rfm_hi[0:64, ro:ro + 128],
                                         ssn_hi[0:64, so:so + 64],
                                         start=(hi_ == 0), stop=False)
                                e.matmul(o2, rfm_hi[0:64, ro:ro + 128],
                                         ssn_lo[0:64, so:so + 64],
                                         start=False, stop=False)
                                e.matmul(o2, rfm_lo[0:64, ro:ro + 128],
                                         ssn_hi[0:64, so:so + 64],
                                         start=False, stop=(hi_ == 7))

                with nc.Block(no_gpsimd_drain=True) as blk:
                    @blk.vector
                    def _(e):
                        e.tensor_tensor(r3(s1[:, :], 64),
                                        r3(ps[:, 4 * 512:6 * 512], 64),
                                        rep3(wb_t[:, :], 64), ALU.mult)
                        e.tensor_tensor(s1[:, :], s1[:, :],
                                        ps[:, 2 * 512:4 * 512], ALU.add)

                if KDBG and n == 16:
                    with nc.Block(no_gpsimd_drain=True) as blk:
                        @blk.sync
                        def _(e):
                            dbg_dump(e, "y_raw", s1[:, :])

                with nc.Block(no_gpsimd_drain=True) as blk:
                    @blk.vector
                    def _(e):
                        e.tensor_tensor(s2[:, :], s1[:, :], s1[:, :], ALU.mult)
                        e.reduce_sum(csc[:, 0:16], r3(s1[:, :], 64), axis=AX.X)
                        vfence(e, e.reduce_sum(csc[:, 16:32], r3(s2[:, :], 64),
                                               axis=AX.X))
                        vfence(e, e.tensor_scalar(csc[:, 0:16], csc[:, 0:16],
                                                   1.0 / 64, None, ALU.mult))
                        vfence(e, e.tensor_tensor(csc[:, 32:48], csc[:, 0:16],
                                                  csc[:, 0:16], ALU.mult))
                        e.scalar_tensor_tensor(csc[:, 16:32], csc[:, 16:32],
                                               1.0 / 64, csc[:, 32:48],
                                               ALU.mult, ALU.subtract)

                with nc.Block(no_gpsimd_drain=True) as blk:
                    @blk.scalar
                    def _(e):
                        e.activation(csc[:, 16:32], csc[:, 16:32], AF.Sqrt,
                                     bias=epsc[:, 1:2])

                with nc.Block(no_gpsimd_drain=True) as blk:
                    @blk.vector
                    def _(e):
                        vfence(e, e.reciprocal(csc[:, 16:32], csc[:, 16:32]))
                        e.tensor_tensor(r3(s1[:, :], 64), r3(s1[:, :], 64),
                                        rep3(csc[:, 0:16], 64), ALU.subtract)
                        e.tensor_tensor(r3(s1[:, :], 64), r3(s1[:, :], 64),
                                        rep3(csc[:, 16:32], 64), ALU.mult)
                        e.tensor_tensor(s1[:, :], s1[:, :], lnw_tm[:, :],
                                        ALU.mult)
                        e.tensor_tensor(s1[:, :], s1[:, :], lnb_tm[:, :],
                                        ALU.add)
                        e.tensor_tensor(s1[:, :], s1[:, :], s3[:, :], ALU.mult)

                if KDBG and n == 16:
                    with nc.Block(no_gpsimd_drain=True) as blk:
                        @blk.sync
                        def _(e):
                            dbg_dump(e, "yg_tm", s1[:, :])

                transpose8(s1, 0)
                with nc.Block(no_gpsimd_drain=True) as blk:
                    @blk.vector
                    def _(e):
                        o = slot * 1024
                        e.tensor_copy(yg_hi[:, o:o + 1024], ps[:, 0:1024])
                        e.tensor_tensor(yg_lo[:, o:o + 1024], ps[:, 0:1024],
                                        yg_hi[:, o:o + 1024], ALU.subtract)

            def chunk_part3(n, slot):
                tok0 = n * T
                own_t0 = (n - OWN0) * T
                with nc.Block(no_gpsimd_drain=True) as blk:
                    @blk.tensor
                    def _(e):
                        for half in range(2):
                            out = bank(half)
                            first = True
                            for cc in range(NCC):
                                o = slot * 1024 + cc * 128
                                woff = cc * 1024 + half * 512
                                e.matmul(out, yg_hi[:, o:o + 128],
                                         wA_hi[:, woff:woff + 512],
                                         start=first, stop=False)
                                first = False
                                e.matmul(out, yg_hi[:, o:o + 128],
                                         wA_lo[:, woff:woff + 512],
                                         start=False, stop=False)
                                e.matmul(out, yg_lo[:, o:o + 128],
                                         wA_hi[:, woff:woff + 512],
                                         start=False, stop=(cc == NCC - 1))

                    @blk.sync
                    def _(e):
                        dma(e, x_tm[:, :], xin[tok0:tok0 + T, :], "sx")

                with nc.Block(no_gpsimd_drain=True) as blk:
                    @blk.vector
                    def _(e):
                        e.tensor_copy(s1[:, :], ps[:, 0:1024])

                if KDBG and n == 16:
                    with nc.Block(no_gpsimd_drain=True) as blk:
                        @blk.sync
                        def _(e):
                            dbg_dump(e, "attnout", s1[:, :])

                ln_plain(s1[:, :], s2[:, :], s3[:, :])
                with nc.Block(no_gpsimd_drain=True) as blk:
                    @blk.vector
                    def _(e):
                        wt(e, "sx")
                        e.tensor_tensor(s1[:, :], x_tm[:, :], s2[:, :], ALU.add)

                with nc.Block(no_gpsimd_drain=True) as blk:
                    @blk.sync
                    def _(e):
                        dma(e, x1sp[own_t0:own_t0 + T, :], s1[:, :], "ssp")

            def partial_out():
                for j in range(16):
                    with nc.Block(no_gpsimd_drain=True) as blk:
                        @blk.sync
                        def _(e, j=j):
                            if KPART in ("attn",):
                                wt(e, "ssp")
                                dma(e, x_tm[:, :], x1sp[j * T:(j + 1) * T, :],
                                    "sx")
                            else:
                                dma(e, x_tm[:, :],
                                    xin[OWNTOK + j * T:OWNTOK + (j + 1) * T, :],
                                    "sx")

                    with nc.Block(no_gpsimd_drain=True) as blk:
                        @blk.sync
                        def _(e, j=j):
                            wt(e, "sx")
                            dma(e, xout[j * T:(j + 1) * T, :], x_tm[:, :],
                                "sout")

                with nc.Block(no_gpsimd_drain=True) as blk:
                    @blk.sync
                    def _(e):
                        wt(e, "sout")

            if KPART == "setup":
                partial_out()
                return nc
            load_wsplit(wA_hi, wA_lo, wT["wk"])
            load_wsplit(wB_hi, wB_lo, wT["wv"])
            for n in range(0, KCTX):
                chunk_part1(n, n % 2, is_ctx=True)
            if KPART == "ctx":
                partial_out()
                return nc
            for g in range(KGRP):
                n0 = OWN0 + 2 * g
                if g > 0:
                    load_wsplit(wA_hi, wA_lo, wT["wk"])
                    load_wsplit(wB_hi, wB_lo, wT["wv"])
                for i in range(2):
                    chunk_part1(n0 + i, i, is_ctx=False)
                if KPART == "ga":
                    partial_out()
                    return nc
                load_wsplit(wA_hi, wA_lo, wT["wr"])
                load_wsplit(wB_hi, wB_lo, wT["wg"])
                for i in range(2):
                    chunk_part2(n0 + i, i)
                if KPART == "gb":
                    partial_out()
                    return nc
                load_wsplit(wA_hi, wA_lo, wT["wo"])
                for i in range(2):
                    chunk_part3(n0 + i, i)
            if KPART == "attn":
                partial_out()
                return nc

        with ExitStack() as _f:
            h_hi = _f.enter_context(nc.sbuf_tensor("h_hi", [128, 8 * 2048], BF16))
            h_lo = _f.enter_context(nc.sbuf_tensor("h_lo", [128, 8 * 2048], BF16))
            rw_hi = _f.enter_context(nc.sbuf_tensor("rw_hi", [128, 64], BF16))
            rw_lo = _f.enter_context(nc.sbuf_tensor("rw_lo", [128, 64], BF16))
            f1 = _f.enter_context(nc.sbuf_tensor("f1", [128, 2048], F32))
            f2 = _f.enter_context(nc.sbuf_tensor("f2", [128, 1024], F32))
            f3 = _f.enter_context(nc.sbuf_tensor("f3", [128, 1024], F32))
            f4 = _f.enter_context(nc.sbuf_tensor("f4", [128, 1024], F32))
            hv_hi = h_hi[:, :].rearrange("p (cc x) -> p cc x", x=2048)
            hv_lo = h_lo[:, :].rearrange("p (cc x) -> p cc x", x=2048)

            with ExitStack() as _h:
                gamb = _h.enter_context(nc.sbuf_tensor("gamb", [128, 1024], F32))
                betb = _h.enter_context(nc.sbuf_tensor("betb", [128, 1024], F32))
                rowf = _h.enter_context(nc.sbuf_tensor("rowf", [1, 2048], F32))
                x1b = _h.enter_context(nc.sbuf_tensor("x1b", [128, 16384], F32))
                sq8 = _h.enter_context(nc.sbuf_tensor("sq8", [128, 8192], F32))
                mus = _h.enter_context(nc.sbuf_tensor("mus", [128, 64], F32))
                x1v = r3(x1b[:, :], 1024)

                with nc.Block(no_gpsimd_drain=True) as blk:
                    @blk.sync
                    def _(e):
                        dma(e, rowf[0:1, :], gbrow_d[0:1, :], "sw")
                        dma(e, r3(f2[:, 0:64], 8),
                            rwT_d[:, :].rearrange("(cc p) x -> p cc x", p=128),
                            "sw")
                        wt(e, "ssp")
                        dma(e, x1v,
                            x1sp.rearrange("(j p) c -> p j c", p=128), "sx")

                with nc.Block(no_gpsimd_drain=True) as blk:
                    @blk.vector
                    def _(e):
                        wt(e, "sw")
                        e.tensor_copy(rw_hi[:, :], f2[:, 0:64])
                        e.tensor_tensor(rw_lo[:, :], f2[:, 0:64], rw_hi[:, :],
                                        ALU.subtract)

                bcast_row(rowf[0:1, 0:1024], gamb[:, :], 1024)
                bcast_row(rowf[0:1, 1024:2048], betb[:, :], 1024)

                def ln_batch(xv, n, gb=None, bb=None):
                    h_ = n // 2
                    sqv = r3(sq8[:, 0:h_ * 1024], 1024)
                    with nc.Block(no_gpsimd_drain=True) as blk:
                        @blk.vector
                        def _(e):
                            e.reduce_sum(mus[:, 0:n], xv, axis=AX.X)
                            e.tensor_tensor(sqv, xv[:, 0:h_], xv[:, 0:h_],
                                            ALU.mult)
                            e.reduce_sum(mus[:, 16:16 + h_], sqv, axis=AX.X)
                            e.tensor_tensor(sqv, xv[:, h_:n], xv[:, h_:n],
                                            ALU.mult)
                            vfence(e, e.reduce_sum(mus[:, 16 + h_:16 + n], sqv,
                                                   axis=AX.X))
                            vfence(e, e.tensor_scalar(mus[:, 0:n], mus[:, 0:n],
                                                      1.0 / 1024, None, ALU.mult))
                            vfence(e, e.tensor_tensor(mus[:, 32:32 + n],
                                                      mus[:, 0:n], mus[:, 0:n],
                                                      ALU.mult))
                            e.scalar_tensor_tensor(mus[:, 16:16 + n],
                                                   mus[:, 16:16 + n], 1.0 / 1024,
                                                   mus[:, 32:32 + n],
                                                   ALU.mult, ALU.subtract)

                    with nc.Block(no_gpsimd_drain=True) as blk:
                        @blk.scalar
                        def _(e):
                            e.activation(mus[:, 16:16 + n], mus[:, 16:16 + n],
                                         AF.Sqrt, bias=epsc[:, 0:1])

                    with nc.Block(no_gpsimd_drain=True) as blk:
                        @blk.vector
                        def _(e):
                            vfence(e, e.reciprocal(mus[:, 16:16 + n],
                                                   mus[:, 16:16 + n]))
                            e.tensor_tensor(xv, xv,
                                            mus[:, 0:n].unsqueeze(2)
                                            .broadcast_to([128, n, 1024]),
                                            ALU.subtract)
                            e.tensor_tensor(xv, xv,
                                            mus[:, 16:16 + n].unsqueeze(2)
                                            .broadcast_to([128, n, 1024]),
                                            ALU.mult)
                            if gb is not None:
                                e.tensor_tensor(xv, xv,
                                                gb[:, :].unsqueeze(1)
                                                .broadcast_to([128, n, 1024]),
                                                ALU.mult)
                                e.tensor_tensor(xv, xv,
                                                bb[:, :].unsqueeze(1)
                                                .broadcast_to([128, n, 1024]),
                                                ALU.add)

                with nc.Block(no_gpsimd_drain=True) as blk:
                    @blk.vector
                    def _(e):
                        wt(e, "sx")

                ln_batch(x1v, 16)
                ln_batch(x1v, 16, gamb, betb)

                def _trans(e, j):
                    for oc in range(NCC):
                        e.transpose(ps[:, (j % 2) * 1024 + oc * 128:
                                       (j % 2) * 1024 + (oc + 1) * 128],
                                    x1b[:, j * 1024 + oc * 128:
                                        j * 1024 + (oc + 1) * 128], ident[:, :])

                def _hevac(e, j):
                    dsth = hv_hi[:, :, j * 128:(j + 1) * 128]
                    dstl = hv_lo[:, :, j * 128:(j + 1) * 128]
                    srcv = r3(ps[:, (j % 2) * 1024:(j % 2) * 1024 + 1024], 128)
                    e.tensor_copy(dsth, srcv)
                    e.tensor_tensor(dstl, srcv, dsth, ALU.subtract)

                def _router(e, j):
                    out = ps[:, 4 * 512 + j * 8:4 * 512 + (j + 1) * 8]
                    for cc in range(NCC):
                        a_hi = hv_hi[:, cc, j * 128:(j + 1) * 128]
                        a_lo = hv_lo[:, cc, j * 128:(j + 1) * 128]
                        e.matmul(out, a_hi, rw_hi[:, cc * 8:cc * 8 + 8],
                                 start=(cc == 0), stop=False)
                        e.matmul(out, a_hi, rw_lo[:, cc * 8:cc * 8 + 8],
                                 start=False, stop=False)
                        e.matmul(out, a_lo, rw_hi[:, cc * 8:cc * 8 + 8],
                                 start=False, stop=(cc == NCC - 1))

                for j in range(18):
                    with nc.Block(no_gpsimd_drain=True) as blk:
                        @blk.tensor
                        def _(e, j=j):
                            if j < 16:
                                _trans(e, j)
                            if 0 <= j - 2 < 16:
                                _router(e, j - 2)
                        if 0 <= j - 1 < 16:
                            @blk.vector
                            def _(e, j=j):
                                _hevac(e, j - 1)

                lg = ps[:, 4 * 512:4 * 512 + 128]
                lgv = r3(lg, 8)
                with nc.Block(no_gpsimd_drain=True) as blk:
                    @blk.vector
                    def _(e):
                        vfence(e, e.reduce_max(mus[:, 0:16], lgv, axis=AX.X))
                        e.tensor_tensor(r3(sq8[:, 0:128], 8), lgv,
                                        mus[:, 0:16].unsqueeze(2)
                                        .broadcast_to([128, 16, 8]),
                                        ALU.subtract)

                with nc.Block(no_gpsimd_drain=True) as blk:
                    @blk.scalar
                    def _(e):
                        e.activation(sq8[:, 128:256], sq8[:, 0:128], AF.Exp)

                with nc.Block(no_gpsimd_drain=True) as blk:
                    @blk.vector
                    def _(e):
                        ev = r3(sq8[:, 128:256], 8)
                        e.reduce_sum(mus[:, 0:16], ev, axis=AX.X)
                        vfence(e, e.reduce_max(mus[:, 16:32], ev, axis=AX.X))
                        vfence(e, e.reciprocal(mus[:, 0:16], mus[:, 0:16]))
                        vfence(e, e.tensor_tensor(mus[:, 32:48], mus[:, 16:32],
                                                  mus[:, 0:16], ALU.mult))
                        e.tensor_tensor(r3(gates[:, :], 8), ev,
                                        mus[:, 16:32].unsqueeze(2)
                                        .broadcast_to([128, 16, 8]), ALU.is_ge)
                        e.tensor_tensor(r3(gates[:, :], 8), r3(gates[:, :], 8),
                                        mus[:, 32:48].unsqueeze(2)
                                        .broadcast_to([128, 16, 8]), ALU.mult)

            if KDBG:
                with nc.Block(no_gpsimd_drain=True) as blk:
                    @blk.sync
                    def _(e):
                        dbg_dump(e, "gates", gates[:, :])

            with ExitStack() as _e2:
                q1 = _e2.enter_context(nc.sbuf_tensor("q1", [128, 8 * 2048], FP8))
                q2 = _e2.enter_context(nc.sbuf_tensor("q2", [128, 16 * 1024], FP8))
                accb = _e2.enter_context(nc.sbuf_tensor("accb", [128, 8 * 512], F32))
                scl1c = _e2.enter_context(nc.sbuf_tensor("scl1c", [128, 16], F32))
                scl2c = _e2.enter_context(nc.sbuf_tensor("scl2c", [128, 8], F32))
                sclrow = _e2.enter_context(nc.sbuf_tensor("sclrow", [1, 2048], F32))
                grow = sclrow
                def quant_weight(dram2d, nchunks, width, qdst, sclc, scl_n,
                                 wstf, fsc, thrb, nthrb):
                    nseg = width // 512
                    rows_per_grp = 2048 // width
                    ngrp = nchunks // rows_per_grp
                    gview = dram2d.rearrange("(g s p) w -> g p s w", p=128,
                                             s=rows_per_grp)

                    def _dma(e, g):
                        dma(e, r3(wstf[:, (g % 2) * 2048:(g % 2) * 2048 + 2048],
                                  width), gview[g], "sw")

                    def _abs(e, g):
                        e.activation(fsc[:, (g % 2) * 2048:(g % 2) * 2048 + 2048],
                                     wstf[:, (g % 2) * 2048:(g % 2) * 2048 + 2048],
                                     AF.Abs)

                    def _mm(e, g):
                        fb = (g % 2) * 2048
                        for s in range(rows_per_grp):
                            for sgi in range(nseg):
                                e.matmul(ps[0:1, 2048 + sgi * 512:
                                            2048 + sgi * 512 + 512],
                                         onescol[:, 0:1],
                                         fsc[:, fb + s * width + sgi * 512:
                                             fb + s * width + (sgi + 1) * 512],
                                         start=(g == 0 and s == 0),
                                         stop=(g == ngrp - 1 and
                                               s == rows_per_grp - 1))

                    waits = {}
                    for g in range(ngrp + 2):
                        with nc.Block(no_gpsimd_drain=True) as blk:
                            if g < ngrp:
                                @blk.sync
                                def _(e, g=g):
                                    _dma(e, g)
                                waits[g] = cnt["sw"]
                            if 0 <= g - 1 < ngrp:
                                @blk.scalar
                                def _(e, g=g):
                                    e.wait_ge(sems["sw"], waits[g - 1])
                                    _abs(e, g - 1)
                            if 0 <= g - 2 < ngrp:
                                @blk.tensor
                                def _(e, g=g):
                                    _mm(e, g - 2)

                    with nc.Block(no_gpsimd_drain=True) as blk:
                        @blk.vector
                        def _(e):
                            e.tensor_scalar(sclrow[0:1, 0:width],
                                            ps[0:1, 2048:2048 + width],
                                            1.0 / (nchunks * 128), 1e-5,
                                            ALU.mult, ALU.max)

                    with nc.Block(no_gpsimd_drain=True) as blk:
                        @blk.tensor
                        def _(e):
                            for q in range(scl_n):
                                e.matmul(ps[:, 2048 + q:2048 + q + 1],
                                         sclrow[0:1, q * 128:(q + 1) * 128],
                                         ones_r[0:1, 0:1], start=True, stop=True)
                            for sgi in range(nseg):
                                e.matmul(ps[:, sgi * 512:(sgi + 1) * 512],
                                         ones_r[0:1, :],
                                         sclrow[0:1, sgi * 512:(sgi + 1) * 512],
                                         start=True, stop=True)

                    with nc.Block(no_gpsimd_drain=True) as blk:
                        @blk.vector
                        def _(e):
                            e.tensor_copy(sclc[:, 0:scl_n],
                                          ps[:, 2048:2048 + scl_n])
                            e.tensor_scalar(thrb[:, 0:width], ps[:, 0:width],
                                            0.5, None, ALU.mult)
                            e.tensor_scalar(nthrb[:, 0:width], thrb[:, 0:width],
                                            -1.0, None, ALU.mult)

                    def _quant(e, g):
                        fb = (g % 2) * 2048
                        for s in range(rows_per_grp):
                            wsl = wstf[:, fb + s * width:fb + (s + 1) * width]
                            fsl = fsc[:, s * width:(s + 1) * width]
                            ch = g * rows_per_grp + s
                            e.tensor_tensor(fsl, wsl, thrb[:, 0:width], ALU.is_ge)
                            e.tensor_tensor(wsl, wsl, nthrb[:, 0:width], ALU.is_le)
                            e.tensor_tensor(qdst[:, ch * width:(ch + 1) * width],
                                            fsl, wsl, ALU.subtract)

                    for g in range(ngrp + 1):
                        with nc.Block(no_gpsimd_drain=True) as blk:
                            if g < ngrp:
                                @blk.sync
                                def _(e, g=g):
                                    _dma(e, g)
                                waits[g] = cnt["sw"]
                            if 0 <= g - 1 < ngrp:
                                @blk.vector
                                def _(e, g=g):
                                    e.wait_ge(sems["sw"], waits[g - 1])
                                    _quant(e, g - 1)

                for eidx in list(range(8))[:max(0, KEXP - 1)] + [8]:
                    shared = (eidx == 8)
                    w1d = sw1T if shared else ew1T[eidx, :, :]
                    w2d = sw2T if shared else ew2T[eidx, :, :]
                    _qe = ExitStack()
                    wstf = _qe.enter_context(
                        nc.sbuf_tensor(f"wstf{eidx}", [128, 4096], F32))
                    fsc = _qe.enter_context(
                        nc.sbuf_tensor(f"fsc{eidx}", [128, 4096], F32))
                    thrb = _qe.enter_context(
                        nc.sbuf_tensor(f"thrb{eidx}", [128, 2048], F32))
                    nthrb = _qe.enter_context(
                        nc.sbuf_tensor(f"nthrb{eidx}", [128, 2048], F32))
                    quant_weight(w1d, NCC, 2048, q1, scl1c, 16,
                                 wstf, fsc, thrb, nthrb)
                    quant_weight(w2d, 16, 1024, q2, scl2c, 8,
                                 wstf, fsc, thrb, nthrb)
                    _qe.close()

                    if not shared:
                        with nc.Block(no_gpsimd_drain=True) as blk:
                            @blk.tensor
                            def _(e, eidx=eidx):
                                for j in range(16):
                                    e.transpose(ps[0:1, j * 128:(j + 1) * 128],
                                                gates[:, j * 8 + eidx:
                                                      j * 8 + eidx + 1],
                                                ident[:, :])

                        with nc.Block(no_gpsimd_drain=True) as blk:
                            @blk.vector
                            def _(e):
                                e.tensor_copy(grow[0:1, :], ps[0:1, 0:2048])

                    if KDBG and eidx == 0:
                        with nc.Block(no_gpsimd_drain=True) as blk:
                            @blk.vector
                            def _(e):
                                e.tensor_copy(f1[:, 0:16], scl1c[:, :])
                                e.tensor_copy(f1[:, 16:24], scl2c[:, :])
                                e.tensor_copy(f2[:, 0:1024], q1[:, 0:1024])

                        with nc.Block(no_gpsimd_drain=True) as blk:
                            @blk.sync
                            def _(e):
                                dbg_dump(e, "scls", f1[:, 0:24])
                                dbg_dump(e, "q1c0", f2[:, 0:1024])

                    _me = ExitStack()
                    a1_hi = _me.enter_context(
                        nc.sbuf_tensor(f"a1h{eidx}", [128, 16 * 512], BF16))
                    a1_lo = _me.enter_context(
                        nc.sbuf_tensor(f"a1l{eidx}", [128, 16 * 512], BF16))
                    for tp in range(NTAU):
                        done_sac = cnt["sac"]
                        with nc.Block(no_gpsimd_drain=True) as blk:
                            @blk.sync
                            def _(e, tp=tp, eidx=eidx):
                                e.wait_ge(sems["sac"], done_sac)
                                if eidx > 0:
                                    dma(e, accb[:, :],
                                        accd[:, tp * 4096:(tp + 1) * 4096],
                                        "sx")

                            if not shared:
                                @blk.tensor
                                def _(e, tp=tp):
                                    e.matmul(ps[:, 7 * 512:8 * 512],
                                             ones_r[0:1, :],
                                             grow[0:1, tp * TAU:(tp + 1) * TAU],
                                             start=True, stop=True)
                        acc_ready = cnt["sx"]

                        def silu_pair(e, fcp):
                            b0 = (fcp % 2) * 2
                            for k2 in range(2):
                                fc = fcp * 2 + k2
                                e.activation(
                                    f1[:, (fcp % 2) * 1024 + k2 * 512:
                                       (fcp % 2) * 1024 + (k2 + 1) * 512],
                                    ps[:, (b0 + k2) * 512:(b0 + k2 + 1) * 512],
                                    AF.Silu, scale=scl1c[:, fc:fc + 1])

                        def gate_split_pair(e, fcp, shared=shared):
                            fb = (fcp % 2) * 1024
                            srcap = f1[:, fb:fb + 1024]
                            if not shared:
                                e.tensor_tensor(
                                    r3(f2[:, 0:1024], 512), r3(srcap, 512),
                                    ps[:, 7 * 512:8 * 512].unsqueeze(1)
                                    .broadcast_to([128, 2, 512]), ALU.mult)
                                srcap = f2[:, 0:1024]
                            o = fcp * 1024
                            e.tensor_copy(a1_hi[:, o:o + 1024], srcap)
                            e.tensor_tensor(a1_lo[:, o:o + 1024], srcap,
                                            a1_hi[:, o:o + 1024], ALU.subtract)

                        def mm1_pair(e, fcp, tp):
                            b0 = (fcp % 2) * 2
                            for k2 in range(2):
                                fc = fcp * 2 + k2
                                out = ps[:, (b0 + k2) * 512:(b0 + k2 + 1) * 512]
                                for cc in range(NCC):
                                    qsl = q1[:, cc * 2048 + fc * 128:
                                             cc * 2048 + (fc + 1) * 128]
                                    e.matmul(out, qsl,
                                             hv_hi[:, cc, tp * 512:(tp + 1) * 512],
                                             start=(cc == 0), stop=False)
                                    e.matmul(out, qsl,
                                             hv_lo[:, cc, tp * 512:(tp + 1) * 512],
                                             start=False, stop=(cc == NCC - 1))

                        for fcp in range(10):
                            with nc.Block(no_gpsimd_drain=True) as blk:
                                if fcp < 8:
                                    @blk.tensor
                                    def _(e, fcp=fcp, tp=tp):
                                        mm1_pair(e, fcp, tp)
                                if 0 <= fcp - 1 < 8:
                                    @blk.scalar
                                    def _(e, fcp=fcp):
                                        silu_pair(e, fcp - 1)
                                if 0 <= fcp - 2 < 8:
                                    @blk.vector
                                    def _(e, fcp=fcp):
                                        gate_split_pair(e, fcp - 2)

                        def mm2_pair(e, dcp, tp):
                            b0 = 2 + (dcp % 2) * 2
                            for k2 in range(2):
                                dc = dcp * 2 + k2
                                out = ps[:, (b0 + k2) * 512:(b0 + k2 + 1) * 512]
                                for fc in range(16):
                                    qsl = q2[:, fc * 1024 + dc * 128:
                                             fc * 1024 + (dc + 1) * 128]
                                    e.matmul(out, qsl,
                                             a1_hi[:, fc * 512:(fc + 1) * 512],
                                             start=(fc == 0), stop=False)
                                    e.matmul(out, qsl,
                                             a1_lo[:, fc * 512:(fc + 1) * 512],
                                             start=False, stop=(fc == 15))

                        def acc_pair(e, dcp, eidx=eidx, acc_ready=acc_ready):
                            if eidx > 0:
                                e.wait_ge(sems["sx"], acc_ready)
                            b0 = 2 + (dcp % 2) * 2
                            for k2 in range(2):
                                dc = dcp * 2 + k2
                                psl = ps[:, (b0 + k2) * 512:(b0 + k2 + 1) * 512]
                                asl = accb[:, dc * 512:(dc + 1) * 512]
                                if eidx == 0:
                                    e.tensor_scalar(asl, psl,
                                                    scl2c[:, dc:dc + 1],
                                                    None, ALU.mult)
                                else:
                                    e.scalar_tensor_tensor(
                                        asl, psl, scl2c[:, dc:dc + 1], asl,
                                        ALU.mult, ALU.add)

                        for dcp in range(5):
                            with nc.Block(no_gpsimd_drain=True) as blk:
                                if dcp < 4:
                                    @blk.tensor
                                    def _(e, dcp=dcp, tp=tp):
                                        mm2_pair(e, dcp, tp)
                                if dcp >= 1:
                                    @blk.vector
                                    def _(e, dcp=dcp):
                                        acc_pair(e, dcp - 1)

                        with nc.Block(no_gpsimd_drain=True) as blk:
                            @blk.sync
                            def _(e, tp=tp):
                                dma(e, accd[:, tp * 4096:(tp + 1) * 4096],
                                    accb[:, :], "sac")
                    _me.close()

            with ExitStack() as _z:
                accf = _z.enter_context(nc.sbuf_tensor("accf", [128, 8192], F32))
                x1f = _z.enter_context(nc.sbuf_tensor("x1f", [128, 8192], F32))
                zf = _z.enter_context(nc.sbuf_tensor("zf", [128, 8192], F32))
                sqf = _z.enter_context(nc.sbuf_tensor("sqf", [128, 4096], F32))
                musf = _z.enter_context(nc.sbuf_tensor("musf", [128, 48], F32))

                with nc.Block(no_gpsimd_drain=True) as blk:
                    @blk.sync
                    def _(e):
                        wt(e, "sac")

                for hf in range(2):
                    with nc.Block(no_gpsimd_drain=True) as blk:
                        @blk.sync
                        def _(e, hf=hf):
                            wt(e, "sout")
                            dma(e, accf[:, :],
                                accd[:, hf * 8192:(hf + 1) * 8192], "sx")
                            dma(e, r3(x1f[:, :], 1024),
                                x1sp[hf * 1024:(hf + 1) * 1024, :]
                                .rearrange("(j p) c -> p j c", p=128), "sx")

                    with nc.Block(no_gpsimd_drain=True) as blk:
                        @blk.vector
                        def _(e):
                            wt(e, "sx")

                    def _ftr(e, j8):
                        tp_l, sub = j8 // 4, j8 % 4
                        for dc in range(NCC):
                            e.transpose(
                                ps[:, (j8 % 2) * 1024 + dc * 128:
                                   (j8 % 2) * 1024 + (dc + 1) * 128],
                                accf[:, tp_l * 4096 + dc * 512 + sub * 128:
                                     tp_l * 4096 + dc * 512 + sub * 128 + 128],
                                ident[:, :])

                    def _fev(e, j8):
                        e.tensor_copy(
                            zf[:, j8 * 1024:(j8 + 1) * 1024],
                            ps[:, (j8 % 2) * 1024:(j8 % 2) * 1024 + 1024])

                    for j8 in range(9):
                        with nc.Block(no_gpsimd_drain=True) as blk:
                            if j8 < 8:
                                @blk.tensor
                                def _(e, j8=j8):
                                    _ftr(e, j8)
                            if j8 >= 1:
                                @blk.vector
                                def _(e, j8=j8):
                                    _fev(e, j8 - 1)

                    zv8 = r3(zf[:, :], 1024)
                    sq4 = r3(sqf[:, :], 1024)
                    with nc.Block(no_gpsimd_drain=True) as blk:
                        @blk.vector
                        def _(e):
                            e.reduce_sum(musf[:, 0:8], zv8, axis=AX.X)
                            e.tensor_tensor(sq4, zv8[:, 0:4], zv8[:, 0:4],
                                            ALU.mult)
                            e.reduce_sum(musf[:, 16:20], sq4, axis=AX.X)
                            e.tensor_tensor(sq4, zv8[:, 4:8], zv8[:, 4:8],
                                            ALU.mult)
                            vfence(e, e.reduce_sum(musf[:, 20:24], sq4,
                                                   axis=AX.X))
                            vfence(e, e.tensor_scalar(musf[:, 0:8], musf[:, 0:8],
                                                      1.0 / 1024, None, ALU.mult))
                            vfence(e, e.tensor_tensor(musf[:, 32:40],
                                                      musf[:, 0:8], musf[:, 0:8],
                                                      ALU.mult))
                            e.scalar_tensor_tensor(musf[:, 16:24], musf[:, 16:24],
                                                   1.0 / 1024, musf[:, 32:40],
                                                   ALU.mult, ALU.subtract)

                    with nc.Block(no_gpsimd_drain=True) as blk:
                        @blk.scalar
                        def _(e):
                            e.activation(musf[:, 16:24], musf[:, 16:24], AF.Sqrt,
                                         bias=epsc[:, 0:1])

                    with nc.Block(no_gpsimd_drain=True) as blk:
                        @blk.vector
                        def _(e):
                            vfence(e, e.reciprocal(musf[:, 16:24],
                                                   musf[:, 16:24]))
                            e.tensor_tensor(zv8, zv8,
                                            musf[:, 0:8].unsqueeze(2)
                                            .broadcast_to([128, 8, 1024]),
                                            ALU.subtract)
                            e.tensor_tensor(zv8, zv8,
                                            musf[:, 16:24].unsqueeze(2)
                                            .broadcast_to([128, 8, 1024]),
                                            ALU.mult)
                            e.tensor_tensor(zf[:, :], zf[:, :], x1f[:, :],
                                            ALU.add)

                    with nc.Block(no_gpsimd_drain=True) as blk:
                        @blk.sync
                        def _(e, hf=hf):
                            dma(e, xout[hf * 1024:(hf + 1) * 1024, :]
                                .rearrange("(j p) c -> p j c", p=128),
                                r3(zf[:, :], 1024), "sout")

            with nc.Block(no_gpsimd_drain=True) as blk:
                @blk.sync
                def _(e):
                    wt(e, "sout")

    return nc


_NC_CACHE = None


def kernel(**inputs):
    global _NC_CACHE
    if _NC_CACHE is None:
        _NC_CACHE = build_nc()
    nc = _NC_CACHE

    x = np.asarray(inputs["L0_attn_xq"], dtype=np.float32)
    tm = {m: np.asarray(inputs[f"L0_attn_time_mix_{m}"], np.float32).reshape(C)
          for m in ("k", "v", "r", "g")}
    tmix = np.zeros((128, 32), np.float32)
    for mi, m in enumerate(("k", "v", "r", "g")):
        tmix[:, mi * 8:(mi + 1) * 8] = tm[m].reshape(8, 128).T
    tdu = np.concatenate([
        np.asarray(inputs["L0_attn_time_decay"], np.float32),
        np.asarray(inputs["L0_attn_time_faaaa"], np.float32)]).reshape(1, 32)
    lnrow = np.concatenate([
        np.asarray(inputs["L0_attn_ln_w"], np.float32),
        np.asarray(inputs["L0_attn_ln_b"], np.float32)]).reshape(1, 2048)
    gbrow = np.concatenate([
        np.asarray(inputs["L0_ffn_ln_gamma"], np.float32),
        np.asarray(inputs["L0_ffn_ln_beta"], np.float32)]).reshape(1, 2048)

    def t2(a):
        return np.ascontiguousarray(np.asarray(a, np.float32).T)

    com = {
        "wrT": t2(inputs["L0_attn_W_r"]), "wkT": t2(inputs["L0_attn_W_k"]),
        "wvT": t2(inputs["L0_attn_W_v"]), "wgT": t2(inputs["L0_attn_W_g"]),
        "woT": t2(inputs["L0_attn_W_o"]),
        "tmix": tmix, "tdu": tdu, "lnrow": lnrow, "gbrow": gbrow,
        "sw1T": t2(inputs["L0_ffn_shared_w1"]),
        "sw2T": t2(inputs["L0_ffn_shared_w2"]),
        "ew1T": np.ascontiguousarray(
            np.asarray(inputs["L0_ffn_expert_w1"], np.float32)
            .transpose(0, 2, 1)),
        "ew2T": np.ascontiguousarray(
            np.asarray(inputs["L0_ffn_expert_w2"], np.float32)
            .transpose(0, 2, 1)),
        "rwT": t2(inputs["L0_ffn_router_w"]),
        "ident": np.eye(128, dtype=np.float32),
    }

    in_maps = []
    for core in range(8):
        b, half = core // 2, core % 2
        own = x[b, half * OWNTOK:(half + 1) * OWNTOK, :]
        ctx = (x[b, 0:OWNTOK, :] if half == 1
               else np.zeros((OWNTOK, C), np.float32))
        m = dict(com)
        m["xin"] = np.ascontiguousarray(np.concatenate([ctx, own], axis=0))
        in_maps.append(m)

    res = run_bass_kernel_spmd(nc, in_maps, core_ids=list(range(8)))
    out = np.empty((B, TT, C), np.float32)
    for core in range(8):
        b, half = core // 2, core % 2
        out[b, half * OWNTOK:(half + 1) * OWNTOK, :] = res.results[core]["xout"]
    return out


# version 10
# speedup vs baseline: 1.2790x; 1.0387x over previous
import os
import numpy as np
import concourse.bass as bass
import concourse.mybir as mybir
from concourse.bass_utils import run_bass_kernel_spmd

F32 = mybir.dt.float32
BF16 = mybir.dt.bfloat16
FP8 = mybir.dt.float8e4
ALU = mybir.AluOpType
AF = mybir.ActivationFunctionType
AX = mybir.AxisListType

B, TT, C = 4, 4096, 1024
H, T, K = 16, 128, 64
NCC = C // 128
OWN0 = 16
OWNTOK = 2048
FDIM = 2048
GEPS = 64e-5
EPS = 1e-5
TAU = 512
NTAU = OWNTOK // TAU

M_K, M_V, M_R, M_G = 0, 1, 2, 3
KPART = os.environ.get("KPART", "full")
KDBG = int(os.environ.get("KDBG", "0"))
KCTX = int(os.environ.get("KCTX", "16"))
KGRP = int(os.environ.get("KGRP", "8"))
KEXP = int(os.environ.get("KEXP", "9"))


def build_nc():
    nc = bass.Bass("TRN2", target_bir_lowering=False)

    xin = nc.dram_tensor("xin", [TT, C], F32, kind="ExternalInput").ap()
    wT = {}
    for nm in ("wr", "wk", "wv", "wg", "wo"):
        wT[nm] = nc.dram_tensor(nm + "T", [C, C], F32, kind="ExternalInput").ap()
    tmix_d = nc.dram_tensor("tmix", [128, 32], F32, kind="ExternalInput").ap()
    tdu_d = nc.dram_tensor("tdu", [1, 32], F32, kind="ExternalInput").ap()
    lnrow_d = nc.dram_tensor("lnrow", [1, 2048], F32, kind="ExternalInput").ap()
    gbrow_d = nc.dram_tensor("gbrow", [1, 2048], F32, kind="ExternalInput").ap()
    sw1T = nc.dram_tensor("sw1T", [C, FDIM], F32, kind="ExternalInput").ap()
    sw2T = nc.dram_tensor("sw2T", [FDIM, C], F32, kind="ExternalInput").ap()
    ew1T = nc.dram_tensor("ew1T", [8, C, FDIM], F32, kind="ExternalInput").ap()
    ew2T = nc.dram_tensor("ew2T", [8, FDIM, C], F32, kind="ExternalInput").ap()
    rwT_d = nc.dram_tensor("rwT", [C, 8], F32, kind="ExternalInput").ap()
    ident_d = nc.dram_tensor("ident", [128, 128], F32, kind="ExternalInput").ap()
    x1sp = nc.dram_tensor("x1sp", [OWNTOK, C], F32,
                          **({"kind": "ExternalOutput"} if KDBG else {})).ap()
    accd = nc.dram_tensor("accd", [128, 8 * OWNTOK], F32,
                          **({"kind": "ExternalOutput"} if KDBG else {})).ap()
    xout = nc.dram_tensor("xout", [OWNTOK, C], F32, kind="ExternalOutput").ap()

    sems = {}
    cnt = {}
    dbg_outs = {}

    def mkdbg(name, shape):
        dbg_outs[name] = nc.dram_tensor("dbg_" + name, shape, F32,
                                        kind="ExternalOutput").ap()
        return dbg_outs[name]

    def dma(e, out, in_, s):
        e.dma_start(out=out, in_=in_).then_inc(sems[s], 16)
        cnt[s] += 16

    def wt(e, s):
        e.wait_ge(sems[s], cnt[s])

    def dbg_dump(e, name, ap):
        if not KDBG:
            return
        if name not in dbg_outs:
            mkdbg(name, list(ap.shape))
        dma(e, dbg_outs[name][tuple(slice(0, s) for s in ap.shape)], ap, "sout")

    def rep3(ap_cols, blk):
        return ap_cols.unsqueeze(2).broadcast_to([128, ap_cols.shape[1], blk])

    def r3(ap2d, blk):
        return ap2d.rearrange("p (n b) -> p n b", b=blk)

    from contextlib import ExitStack
    with ExitStack() as _g:
        _sw = _g.enter_context(nc.semaphore("sw"))
        _sx = _g.enter_context(nc.semaphore("sx"))
        _ssp = _g.enter_context(nc.semaphore("ssp"))
        _sac = _g.enter_context(nc.semaphore("sac"))
        _sout = _g.enter_context(nc.semaphore("sout"))
        _svf = _g.enter_context(nc.semaphore("svf"))
        ident = _g.enter_context(nc.sbuf_tensor("ident_s", [128, 128], F32))
        ones_r = _g.enter_context(nc.sbuf_tensor("ones_r", [1, 128], F32))
        onescol = _g.enter_context(nc.sbuf_tensor("onescol", [128, 1], F32))
        tmv = _g.enter_context(nc.sbuf_tensor("tmv", [128, 32], F32))
        gates = _g.enter_context(nc.sbuf_tensor("gates", [128, 128], F32))
        stat = _g.enter_context(nc.sbuf_tensor("stat", [128, 12], F32))
        epsc = _g.enter_context(nc.sbuf_tensor("epsc", [128, 2], F32))
        ps = _g.enter_context(nc.psum_tensor("ps", [128, 4096], F32))
        sems.update(sw=_sw, sx=_sx, ssp=_ssp, sac=_sac, sout=_sout, vf=_svf)
        cnt.update(sw=0, sx=0, ssp=0, sac=0, sout=0, vf=0)

        def vfence(e, inst):
            inst.then_inc(sems["vf"], 1)
            cnt["vf"] += 1
            e.wait_ge(sems["vf"], cnt["vf"])

        def bank(i):
            return ps[:, i * 512:(i + 1) * 512]

        def bcast_row(row_ap, dst_sbuf_ap, width):
            with nc.Block(no_gpsimd_drain=True) as blk:
                @blk.tensor
                def _(e):
                    for sgi in range(width // 512):
                        e.matmul(ps[:, sgi * 512:sgi * 512 + 512],
                                 ones_r[0:1, :],
                                 row_ap[0:1, sgi * 512:(sgi + 1) * 512],
                                 start=True, stop=True)

            with nc.Block(no_gpsimd_drain=True) as blk:
                @blk.vector
                def _(e):
                    e.tensor_copy(dst_sbuf_ap, ps[:, 0:width])

        def ln_plain(xap, outap, scr, eps_ap=None):
            with nc.Block(no_gpsimd_drain=True) as blk:
                @blk.vector
                def _(e):
                    e.reduce_sum(stat[:, 0:1], xap, axis=AX.X)
                    e.tensor_tensor(scr, xap, xap, ALU.mult)
                    e.reduce_sum(stat[:, 1:2], scr, axis=AX.X)
                    vfence(e, e.tensor_scalar(stat[:, 0:1], stat[:, 0:1],
                                               1.0 / 1024, None, ALU.mult))
                    vfence(e, e.tensor_tensor(stat[:, 2:3], stat[:, 0:1],
                                              stat[:, 0:1], ALU.mult))
                    e.scalar_tensor_tensor(stat[:, 2:3], stat[:, 1:2], 1.0 / 1024,
                                           stat[:, 2:3], ALU.mult, ALU.subtract)

            with nc.Block(no_gpsimd_drain=True) as blk:
                @blk.scalar
                def _(e):
                    e.activation(stat[:, 3:4], stat[:, 2:3], AF.Sqrt,
                                 bias=(eps_ap if eps_ap is not None
                                       else epsc[:, 0:1]))

            with nc.Block(no_gpsimd_drain=True) as blk:
                @blk.vector
                def _(e):
                    vfence(e, e.reciprocal(stat[:, 2:3], stat[:, 3:4]))
                    e.tensor_scalar(outap, xap, stat[:, 0:1], stat[:, 2:3],
                                    ALU.subtract, ALU.mult)

        with ExitStack() as _a:
            wA_hi = _a.enter_context(nc.sbuf_tensor("wA_hi", [128, 8192], BF16))
            wA_lo = _a.enter_context(nc.sbuf_tensor("wA_lo", [128, 8192], BF16))
            wB_hi = _a.enter_context(nc.sbuf_tensor("wB_hi", [128, 8192], BF16))
            wB_lo = _a.enter_context(nc.sbuf_tensor("wB_lo", [128, 8192], BF16))
            wstage = _a.enter_context(nc.sbuf_tensor("wstage", [128, 2048], F32))
            x_tm = _a.enter_context(nc.sbuf_tensor("x_tm", [128, 1024], F32))
            xln_tm = _a.enter_context(nc.sbuf_tensor("xln_tm", [128, 1024], F32))
            xlnf = _a.enter_context(nc.sbuf_tensor("xlnf", [128, 2048], F32))
            mix_hi = _a.enter_context(nc.sbuf_tensor("mix_hi", [128, 1024], BF16))
            mix_lo = _a.enter_context(nc.sbuf_tensor("mix_lo", [128, 1024], BF16))
            mix2_hi = _a.enter_context(nc.sbuf_tensor("mix2_hi", [128, 1024], BF16))
            mix2_lo = _a.enter_context(nc.sbuf_tensor("mix2_lo", [128, 1024], BF16))
            atw2_hi = _a.enter_context(nc.sbuf_tensor("atw2_hi", [128, 1024], BF16))
            atw2_lo = _a.enter_context(nc.sbuf_tensor("atw2_lo", [128, 1024], BF16))
            dxf = _a.enter_context(nc.sbuf_tensor("dxf", [128, 1024], F32))
            s1 = _a.enter_context(nc.sbuf_tensor("s1", [128, 1024], F32))
            s2 = _a.enter_context(nc.sbuf_tensor("s2", [128, 1024], F32))
            s3 = _a.enter_context(nc.sbuf_tensor("s3", [128, 1024], F32))
            sbig = _a.enter_context(nc.sbuf_tensor("sbig", [128, 1024], F32))
            kwk_hi = _a.enter_context(nc.sbuf_tensor("kwk_hi", [128, 1024], BF16))
            kwk_lo = _a.enter_context(nc.sbuf_tensor("kwk_lo", [128, 1024], BF16))
            kfm_hi = _a.enter_context(nc.sbuf_tensor("kfm_hi", [64, 4096], BF16))
            kfm_lo = _a.enter_context(nc.sbuf_tensor("kfm_lo", [64, 4096], BF16))
            vtm_hi = _a.enter_context(nc.sbuf_tensor("vtm_hi", [128, 2048], BF16))
            vtm_lo = _a.enter_context(nc.sbuf_tensor("vtm_lo", [128, 2048], BF16))
            ssn_hi = _a.enter_context(nc.sbuf_tensor("ssn_hi", [64, 2048], BF16))
            ssn_lo = _a.enter_context(nc.sbuf_tensor("ssn_lo", [64, 2048], BF16))
            rfm_hi = _a.enter_context(nc.sbuf_tensor("rfm_hi", [64, 2048], BF16))
            rfm_lo = _a.enter_context(nc.sbuf_tensor("rfm_lo", [64, 2048], BF16))
            atw_hi = _a.enter_context(nc.sbuf_tensor("atw_hi", [128, 1024], BF16))
            atw_lo = _a.enter_context(nc.sbuf_tensor("atw_lo", [128, 1024], BF16))
            yg_hi = _a.enter_context(nc.sbuf_tensor("yg_hi", [128, 2048], BF16))
            yg_lo = _a.enter_context(nc.sbuf_tensor("yg_lo", [128, 2048], BF16))
            s_f32 = _a.enter_context(nc.sbuf_tensor("s_f32", [64, 1024], F32))
            s_hi = _a.enter_context(nc.sbuf_tensor("s_hi", [64, 1024], BF16))
            s_lo = _a.enter_context(nc.sbuf_tensor("s_lo", [64, 1024], BF16))
            wmT = _a.enter_context(nc.sbuf_tensor("wmT", [128, 2048], F32))
            bcol = _a.enter_context(nc.sbuf_tensor("bcol", [128, 8], F32))
            bhist = _a.enter_context(nc.sbuf_tensor("bhist", [128, 16], F32))
            lndb = _a.enter_context(nc.sbuf_tensor("lndb", [128, 16], F32))
            ub = _a.enter_context(nc.sbuf_tensor("ub", [128, 16], F32))
            wb_t = _a.enter_context(nc.sbuf_tensor("wb_t", [128, 16], F32))
            wk_t = _a.enter_context(nc.sbuf_tensor("wk_t", [128, 16], F32))
            ws_p = _a.enter_context(nc.sbuf_tensor("ws_p", [64, 16], F32))
            lnw_tm = _a.enter_context(nc.sbuf_tensor("lnw_tm", [128, 1024], F32))
            lnb_tm = _a.enter_context(nc.sbuf_tensor("lnb_tm", [128, 1024], F32))
            iotac = _a.enter_context(nc.sbuf_tensor("iotac", [128, 2], F32))
            csc = _a.enter_context(nc.sbuf_tensor("csc", [128, 48], F32))
            dbgst = (_a.enter_context(nc.sbuf_tensor("dbgst", [128, 1024], F32))
                     if KDBG else None)
            with nc.Block() as blk:
                @blk.sync
                def _(e):
                    dma(e, ident[:, :], ident_d[:, :], "sw")
                    dma(e, tmv[:, :], tmix_d[:, :], "sw")
                    dma(e, csc[0:1, 0:32], tdu_d[0:1, :], "sw")
                    dma(e, wstage[0:1, 0:2048], lnrow_d[0:1, :], "sw")

                @blk.gpsimd
                def _(e):
                    e.memset(ones_r[0:1, :], 1.0)
                    e.memset(onescol[:, :], 1.0)
                    e.memset(epsc[:, 0:1], EPS)
                    e.memset(epsc[:, 1:2], GEPS)
                    e.memset(s_f32[:, :], 0.0)
                    e.memset(s_hi[:, :], 0.0)
                    e.memset(s_lo[:, :], 0.0)
                    e.memset(bcol[:, :], 0.0)
                    e.iota(sbig[:, 0:128], [[1, 128]], channel_multiplier=-1,
                           allow_small_or_imprecise_dtypes=True)
                    e.iota(iotac[:, 0:1], [[1, 1]], channel_multiplier=1,
                           allow_small_or_imprecise_dtypes=True)

            with nc.Block(no_gpsimd_drain=True) as blk:
                @blk.scalar
                def _(e):
                    wt(e, "sw")
                    e.activation(csc[0:1, 0:16], csc[0:1, 0:16], AF.Exp)

                @blk.vector
                def _(e):
                    wt(e, "sw")
                    e.tensor_scalar(iotac[:, 1:2], iotac[:, 0:1], -1.0, 127.0,
                                    ALU.mult, ALU.add)
                    e.tensor_scalar(s1[:, 0:128], sbig[:, 0:128], 1.0, 0.0,
                                    ALU.subtract, ALU.max)
                    e.tensor_scalar(s2[:, 0:128], sbig[:, 0:128], 1.0, None,
                                    ALU.is_ge)
                    e.tensor_scalar(s3[:, 0:128], sbig[:, 0:128], 0.0, None,
                                    ALU.is_equal)

            with nc.Block(no_gpsimd_drain=True) as blk:
                @blk.vector
                def _(e):
                    e.tensor_scalar(csc[0:1, 0:16], csc[0:1, 0:16], -1.0, None,
                                    ALU.mult)

            with nc.Block(no_gpsimd_drain=True) as blk:
                @blk.tensor
                def _(e):
                    e.matmul(ps[:, 0:16], ones_r[0:1, :], csc[0:1, 0:16],
                             start=True, stop=False)
                    e.matmul(ps[:, 16:32], ones_r[0:1, :], csc[0:1, 16:32],
                             start=False, stop=True)

            with nc.Block(no_gpsimd_drain=True) as blk:
                @blk.vector
                def _(e):
                    e.tensor_copy(lndb[:, :], ps[:, 0:16])
                    e.tensor_copy(ub[:, :], ps[:, 16:32])

            with nc.Block(no_gpsimd_drain=True) as blk:
                @blk.scalar
                def _(e):
                    for h in range(H):
                        e.activation(wb_t[:, h:h + 1], iotac[:, 0:1], AF.Exp,
                                     scale=lndb[:, h:h + 1])
                        e.activation(wk_t[:, h:h + 1], iotac[:, 1:2], AF.Exp,
                                     scale=lndb[:, h:h + 1])
                    e.activation(csc[:, 0:16], lndb[:, :], AF.Exp, scale=128.0)
                    for h in range(H):
                        e.activation(wmT[:, h * 128:(h + 1) * 128], s1[:, 0:128],
                                     AF.Exp, scale=lndb[:, h:h + 1])

            with nc.Block(no_gpsimd_drain=True) as blk:
                @blk.vector
                def _(e):
                    e.tensor_copy(ws_p[0:64, :], csc[0:64, 0:16])
                    for h in range(H):
                        sl = wmT[:, h * 128:(h + 1) * 128]
                        e.tensor_tensor(sl, sl, s2[:, 0:128], ALU.mult)
                        e.scalar_tensor_tensor(sl, s3[:, 0:128], ub[:, h:h + 1],
                                               sl, ALU.mult, ALU.add)

            bcast_row(wstage[0:1, 0:1024], lnw_tm[:, :], 1024)
            bcast_row(wstage[0:1, 1024:2048], lnb_tm[:, :], 1024)

            def load_wsplit(dsth, dstl, dram):
                def _dma(e, step):
                    dma(e, wstage[:, (step % 2) * 1024:(step % 2) * 1024 + 1024],
                        dram[step * 128:(step + 1) * 128, :], "sw")

                def _split(e, step, when):
                    e.wait_ge(sems["sw"], when)
                    o = step * 1024
                    wsl = wstage[:, (step % 2) * 1024:(step % 2) * 1024 + 1024]
                    e.tensor_copy(dsth[:, o:o + 1024], wsl)
                    e.tensor_tensor(dstl[:, o:o + 1024], wsl,
                                    dsth[:, o:o + 1024], ALU.subtract)

                with nc.Block(no_gpsimd_drain=True) as blk:
                    @blk.sync
                    def _(e):
                        _dma(e, 0)
                for step in range(7):
                    done = cnt["sw"]
                    with nc.Block(no_gpsimd_drain=True) as blk:
                        @blk.sync
                        def _(e, step=step):
                            _dma(e, step + 1)

                        @blk.vector
                        def _(e, step=step, done=done):
                            _split(e, step, done)
                done = cnt["sw"]
                with nc.Block(no_gpsimd_drain=True) as blk:
                    @blk.vector
                    def _(e, done=done):
                        _split(e, 7, done)

            def mix_ops(e, slot, m, mh, ml, scr_a, scr_b):
                cur = r3(xlnf[:, slot * 1024:(slot + 1) * 1024], 128)
                bh = bhist[:, slot * 8:slot * 8 + 8].unsqueeze(2)
                tm3 = rep3(tmv[:, m * 8:(m + 1) * 8], 128)
                e.tensor_tensor(r3(scr_a, 128), r3(dxf[:, :], 128), tm3,
                                ALU.mult)
                e.tensor_tensor(r3(scr_b, 128)[:, :, 1:128], cur[:, :, 0:127],
                                r3(scr_a, 128)[:, :, 1:128], ALU.add)
                vfence(e, e.tensor_tensor(r3(scr_b, 128)[:, :, 0:1], bh,
                                           r3(scr_a, 128)[:, :, 0:1], ALU.add))
                e.tensor_copy(mh[:, :], scr_b)
                e.tensor_tensor(ml[:, :], scr_b, mh[:, :], ALU.subtract)

            def mix_split(slot, m):
                with nc.Block(no_gpsimd_drain=True) as blk:
                    @blk.vector
                    def _(e):
                        mix_ops(e, slot, m, mix_hi, mix_lo, s2[:, :], s3[:, :])

            def proj_ops(e, whi, wlo, pbank0, mh, ml):
                for half in range(2):
                    out = bank(pbank0 + half)
                    first = True
                    for cc in range(NCC):
                        a_hi = mh[:, cc * 128:(cc + 1) * 128]
                        a_lo = ml[:, cc * 128:(cc + 1) * 128]
                        woff = cc * 1024 + half * 512
                        e.matmul(out, a_hi, whi[:, woff:woff + 512],
                                 start=first, stop=False)
                        first = False
                        e.matmul(out, a_hi, wlo[:, woff:woff + 512],
                                 start=False, stop=False)
                        e.matmul(out, a_lo, whi[:, woff:woff + 512],
                                 start=False, stop=(cc == NCC - 1))

            def proj_tm(whi, wlo, pbank0):
                with nc.Block(no_gpsimd_drain=True) as blk:
                    @blk.tensor
                    def _(e):
                        proj_ops(e, whi, wlo, pbank0, mix_hi, mix_lo)

            def transpose8(src, pbank0):
                with nc.Block(no_gpsimd_drain=True) as blk:
                    @blk.tensor
                    def _(e):
                        for oc in range(NCC):
                            e.transpose(ps[:, pbank0 * 512 + oc * 128:
                                           pbank0 * 512 + (oc + 1) * 128],
                                        src[:, oc * 128:(oc + 1) * 128],
                                        ident[:, :])

            def chunk_part1(n, slot, is_ctx):
                tok0 = n * T
                with nc.Block(no_gpsimd_drain=True) as blk:
                    @blk.sync
                    def _(e):
                        dma(e, x_tm[:, :], xin[tok0:tok0 + T, :], "sx")

                with nc.Block(no_gpsimd_drain=True) as blk:
                    @blk.vector
                    def _(e):
                        wt(e, "sx")

                    @blk.sync
                    def _(e):
                        wt(e, "ssp")

                ln_plain(x_tm[:, :], xln_tm[:, :], s1[:, :])
                if KDBG and n == 16:
                    with nc.Block(no_gpsimd_drain=True) as blk:
                        @blk.sync
                        def _(e):
                            dbg_dump(e, "xln_tm", xln_tm[:, :])
                transpose8(xln_tm, 0)
                with nc.Block(no_gpsimd_drain=True) as blk:
                    @blk.vector
                    def _(e):
                        vfence(e, e.tensor_copy(bhist[:, slot * 8:slot * 8 + 8],
                                                bcol[:, :]))
                        e.tensor_copy(xlnf[:, slot * 1024:(slot + 1) * 1024],
                                      ps[:, 0:1024])
                        cur = r3(xlnf[:, slot * 1024:(slot + 1) * 1024], 128)
                        vfence(e, e.tensor_copy(bcol[:, :].unsqueeze(2),
                                                cur[:, :, 127:128]))
                        e.tensor_tensor(r3(dxf[:, :], 128)[:, :, 1:128],
                                        cur[:, :, 1:128], cur[:, :, 0:127],
                                        ALU.subtract)
                        vfence(e, e.tensor_tensor(r3(dxf[:, :], 128)[:, :, 0:1],
                                                  cur[:, :, 0:1],
                                                  bhist[:, slot * 8:slot * 8 + 8]
                                                  .unsqueeze(2), ALU.subtract))
                        mix_ops(e, slot, M_K, mix_hi, mix_lo, s2[:, :], s3[:, :])

                with nc.Block(no_gpsimd_drain=True) as blk:
                    @blk.tensor
                    def _(e):
                        proj_ops(e, wA_hi, wA_lo, 0, mix_hi, mix_lo)

                    @blk.vector
                    def _(e):
                        mix_ops(e, slot, M_V, mix2_hi, mix2_lo, s2[:, :],
                                s3[:, :])

                with nc.Block(no_gpsimd_drain=True) as blk:
                    @blk.tensor
                    def _(e):
                        proj_ops(e, wB_hi, wB_lo, 4, mix2_hi, mix2_lo)

                    @blk.vector
                    def _(e):
                        e.tensor_copy(s1[:, :], ps[:, 0:1024])
                        e.tensor_tensor(r3(s2[:, :], 64), r3(s1[:, :], 64),
                                        rep3(wk_t[:, :], 64), ALU.mult)
                        e.tensor_copy(kwk_hi[:, :], s2[:, :])
                        e.tensor_tensor(kwk_lo[:, :], s2[:, :], kwk_hi[:, :],
                                        ALU.subtract)

                if KDBG and n == 16:
                    with nc.Block(no_gpsimd_drain=True) as blk:
                        @blk.sync
                        def _(e):
                            dbg_dump(e, "k_tm", s1[:, :])
                            dbg_dump(e, "mixv_chk", s3[:, :])
                with nc.Block(no_gpsimd_drain=True) as blk:
                    if not is_ctx:
                        @blk.tensor
                        def _(e):
                            for h in range(H):
                                e.transpose(ps[0:64, h * 128:(h + 1) * 128],
                                            s1[:, h * 64:(h + 1) * 64],
                                            ident[:, :])

                    @blk.vector
                    def _(e, is_ctx=is_ctx):
                        o = slot * 1024
                        e.tensor_copy(vtm_hi[:, o:o + 1024], ps[:, 2048:3072])
                        e.tensor_tensor(vtm_lo[:, o:o + 1024], ps[:, 2048:3072],
                                        vtm_hi[:, o:o + 1024], ALU.subtract)
                        if not is_ctx:
                            so = slot * 1024
                            e.tensor_copy(ssn_hi[0:64, so:so + 1024],
                                          s_hi[0:64, :])
                            e.tensor_copy(ssn_lo[0:64, so:so + 1024],
                                          s_lo[0:64, :])

                with nc.Block(no_gpsimd_drain=True) as blk:
                    @blk.tensor
                    def _(e, slot=slot):
                        for h in range(H):
                            o = slot * 1024 + h * 64
                            outsl = ps[0:64, 6 * 512 + h * 64:
                                       6 * 512 + (h + 1) * 64]
                            e.matmul(outsl, kwk_hi[:, h * 64:h * 64 + 64],
                                     vtm_hi[:, o:o + 64],
                                     start=(h % 8 == 0), stop=False)
                            e.matmul(outsl, kwk_hi[:, h * 64:h * 64 + 64],
                                     vtm_lo[:, o:o + 64], start=False,
                                     stop=False)
                            e.matmul(outsl, kwk_lo[:, h * 64:h * 64 + 64],
                                     vtm_hi[:, o:o + 64], start=False,
                                     stop=(h % 8 == 7))

                    if not is_ctx:
                        @blk.vector
                        def _(e):
                            o = slot * 2048
                            e.tensor_copy(kfm_hi[0:64, o:o + 2048],
                                          ps[0:64, 0:2048])
                            e.tensor_tensor(kfm_lo[0:64, o:o + 2048],
                                            ps[0:64, 0:2048],
                                            kfm_hi[0:64, o:o + 2048],
                                            ALU.subtract)

                if KDBG and n == 16:
                    with nc.Block(no_gpsimd_drain=True) as blk:
                        @blk.vector
                        def _(e, slot=slot):
                            o = slot * 1024
                            e.tensor_tensor(dbgst[:, 0:1024],
                                            vtm_hi[:, o:o + 1024],
                                            vtm_lo[:, o:o + 1024], ALU.add)

                    with nc.Block(no_gpsimd_drain=True) as blk:
                        @blk.sync
                        def _(e):
                            dbg_dump(e, "v_tm", dbgst[:, 0:1024])
                with nc.Block(no_gpsimd_drain=True) as blk:
                    @blk.vector
                    def _(e):
                        e.tensor_tensor(
                            s_f32[0:64, :].rearrange("p (n b) -> p n b", b=64),
                            s_f32[0:64, :].rearrange("p (n b) -> p n b", b=64),
                            ws_p[0:64, :].unsqueeze(2)
                            .broadcast_to([64, 16, 64]), ALU.mult)
                        e.tensor_tensor(s_f32[0:64, :], s_f32[0:64, :],
                                        ps[0:64, 6 * 512:8 * 512], ALU.add)
                        e.tensor_copy(s_hi[0:64, :], s_f32[0:64, :])
                        e.tensor_tensor(s_lo[0:64, :], s_f32[0:64, :],
                                        s_hi[0:64, :], ALU.subtract)

                if KDBG and n == 16:
                    with nc.Block(no_gpsimd_drain=True) as blk:
                        @blk.sync
                        def _(e):
                            dbg_dump(e, "s_next", s_f32[0:64, :])

            def _dump_s(n):
                if KDBG and n == 16:
                    with nc.Block(no_gpsimd_drain=True) as blk:
                        @blk.sync
                        def _(e):
                            dbg_dump(e, "s_next", s_f32[0:64, :])

            def chunk_part2(n, slot):
                with nc.Block(no_gpsimd_drain=True) as blk:
                    @blk.vector
                    def _(e):
                        cur = r3(xlnf[:, slot * 1024:(slot + 1) * 1024], 128)
                        e.tensor_tensor(r3(dxf[:, :], 128)[:, :, 1:128],
                                        cur[:, :, 1:128], cur[:, :, 0:127],
                                        ALU.subtract)
                        vfence(e, e.tensor_tensor(r3(dxf[:, :], 128)[:, :, 0:1],
                                                  cur[:, :, 0:1],
                                                  bhist[:, slot * 8:slot * 8 + 8]
                                                  .unsqueeze(2), ALU.subtract))
                        mix_ops(e, slot, M_R, mix_hi, mix_lo, s2[:, :], s3[:, :])

                with nc.Block(no_gpsimd_drain=True) as blk:
                    @blk.tensor
                    def _(e):
                        proj_ops(e, wA_hi, wA_lo, 0, mix_hi, mix_lo)

                    @blk.vector
                    def _(e):
                        mix_ops(e, slot, M_G, mix2_hi, mix2_lo, s2[:, :],
                                s3[:, :])

                with nc.Block(no_gpsimd_drain=True) as blk:
                    @blk.tensor
                    def _(e):
                        proj_ops(e, wB_hi, wB_lo, 4, mix2_hi, mix2_lo)

                    @blk.vector
                    def _(e):
                        e.tensor_copy(s1[:, :], ps[:, 0:1024])

                with nc.Block(no_gpsimd_drain=True) as blk:
                    @blk.tensor
                    def _(e):
                        for h in range(H):
                            e.transpose(ps[0:64, h * 128:(h + 1) * 128],
                                        s1[:, h * 64:(h + 1) * 64], ident[:, :])

                    @blk.scalar
                    def _(e):
                        e.activation(s3[:, :], ps[:, 2048:3072], AF.Silu)

                with nc.Block(no_gpsimd_drain=True) as blk:
                    @blk.vector
                    def _(e):
                        e.tensor_copy(rfm_hi[0:64, :], ps[0:64, 0:2048])
                        e.tensor_tensor(rfm_lo[0:64, :], ps[0:64, 0:2048],
                                        rfm_hi[0:64, :], ALU.subtract)

                def att_mms(e, hh, b0):
                    for hi_ in range(8):
                        h = hh * 8 + hi_
                        ko = slot * 2048 + h * 128
                        ro = h * 128
                        out = ps[:, b0 * 512 + hi_ * 128:
                                 b0 * 512 + (hi_ + 1) * 128]
                        e.matmul(out, kfm_hi[0:64, ko:ko + 128],
                                 rfm_hi[0:64, ro:ro + 128],
                                 start=(hi_ % 4 == 0), stop=False)
                        e.matmul(out, kfm_hi[0:64, ko:ko + 128],
                                 rfm_lo[0:64, ro:ro + 128],
                                 start=False, stop=False)
                        e.matmul(out, kfm_lo[0:64, ko:ko + 128],
                                 rfm_hi[0:64, ro:ro + 128],
                                 start=False, stop=(hi_ % 4 == 3))

                def mask_ops(e, b0, hh, ahi, alo):
                    e.tensor_tensor(sbig[:, :], ps[:, b0 * 512:b0 * 512 + 1024],
                                    wmT[:, hh * 1024:(hh + 1) * 1024], ALU.mult)
                    e.tensor_copy(ahi[:, :], sbig[:, :])
                    e.tensor_tensor(alo[:, :], sbig[:, :], ahi[:, :],
                                    ALU.subtract)

                def out_mms(e, hh, ahi, alo):
                    for hi_ in range(8):
                        h = hh * 8 + hi_
                        vo = slot * 1024 + h * 64
                        ro = h * 128
                        so = slot * 1024 + h * 64
                        o1 = ps[:, 4 * 512 + h * 64:4 * 512 + h * 64 + 64]
                        e.matmul(o1, ahi[:, hi_ * 128:(hi_ + 1) * 128],
                                 vtm_hi[:, vo:vo + 64],
                                 start=(hi_ == 0), stop=False)
                        e.matmul(o1, ahi[:, hi_ * 128:(hi_ + 1) * 128],
                                 vtm_lo[:, vo:vo + 64], start=False, stop=False)
                        e.matmul(o1, alo[:, hi_ * 128:(hi_ + 1) * 128],
                                 vtm_hi[:, vo:vo + 64], start=False,
                                 stop=(hi_ == 7))
                        o2 = ps[:, 6 * 512 + h * 64:6 * 512 + h * 64 + 64]
                        e.matmul(o2, rfm_hi[0:64, ro:ro + 128],
                                 ssn_hi[0:64, so:so + 64],
                                 start=(hi_ == 0), stop=False)
                        e.matmul(o2, rfm_hi[0:64, ro:ro + 128],
                                 ssn_lo[0:64, so:so + 64],
                                 start=False, stop=False)
                        e.matmul(o2, rfm_lo[0:64, ro:ro + 128],
                                 ssn_hi[0:64, so:so + 64],
                                 start=False, stop=(hi_ == 7))

                if KDBG and n == 16:
                    with nc.Block(no_gpsimd_drain=True) as blk:
                        @blk.sync
                        def _(e):
                            dbg_dump(e, "g_tm", s3[:, :])

                with nc.Block(no_gpsimd_drain=True) as blk:
                    @blk.tensor
                    def _(e):
                        att_mms(e, 0, 0)

                with nc.Block(no_gpsimd_drain=True) as blk:
                    @blk.tensor
                    def _(e):
                        att_mms(e, 1, 2)

                    @blk.vector
                    def _(e):
                        mask_ops(e, 0, 0, atw_hi, atw_lo)

                with nc.Block(no_gpsimd_drain=True) as blk:
                    @blk.tensor
                    def _(e):
                        out_mms(e, 0, atw_hi, atw_lo)

                    @blk.vector
                    def _(e):
                        mask_ops(e, 2, 1, atw2_hi, atw2_lo)

                with nc.Block(no_gpsimd_drain=True) as blk:
                    @blk.tensor
                    def _(e):
                        out_mms(e, 1, atw2_hi, atw2_lo)

                if KDBG and n == 16:
                    with nc.Block(no_gpsimd_drain=True) as blk:
                        @blk.sync
                        def _(e):
                            dbg_dump(e, "attw0", sbig[:, :])
                with nc.Block(no_gpsimd_drain=True) as blk:
                    @blk.vector
                    def _(e):
                        e.tensor_tensor(r3(s1[:, :], 64),
                                        r3(ps[:, 6 * 512:8 * 512], 64),
                                        rep3(wb_t[:, :], 64), ALU.mult)
                        e.tensor_tensor(s1[:, :], s1[:, :],
                                        ps[:, 4 * 512:6 * 512], ALU.add)
                        e.tensor_tensor(s2[:, :], s1[:, :], s1[:, :], ALU.mult)
                        e.reduce_sum(csc[:, 0:16], r3(s1[:, :], 64), axis=AX.X)
                        vfence(e, e.reduce_sum(csc[:, 16:32], r3(s2[:, :], 64),
                                               axis=AX.X))
                        vfence(e, e.tensor_scalar(csc[:, 0:16], csc[:, 0:16],
                                                  1.0 / 64, None, ALU.mult))
                        vfence(e, e.tensor_tensor(csc[:, 32:48], csc[:, 0:16],
                                                  csc[:, 0:16], ALU.mult))
                        e.scalar_tensor_tensor(csc[:, 16:32], csc[:, 16:32],
                                               1.0 / 64, csc[:, 32:48],
                                               ALU.mult, ALU.subtract)

                with nc.Block(no_gpsimd_drain=True) as blk:
                    @blk.scalar
                    def _(e):
                        e.activation(csc[:, 16:32], csc[:, 16:32], AF.Sqrt,
                                     bias=epsc[:, 1:2])

                with nc.Block(no_gpsimd_drain=True) as blk:
                    @blk.vector
                    def _(e):
                        vfence(e, e.reciprocal(csc[:, 16:32], csc[:, 16:32]))
                        e.tensor_tensor(r3(s1[:, :], 64), r3(s1[:, :], 64),
                                        rep3(csc[:, 0:16], 64), ALU.subtract)
                        e.tensor_tensor(r3(s1[:, :], 64), r3(s1[:, :], 64),
                                        rep3(csc[:, 16:32], 64), ALU.mult)
                        e.tensor_tensor(s1[:, :], s1[:, :], lnw_tm[:, :],
                                        ALU.mult)
                        e.tensor_tensor(s1[:, :], s1[:, :], lnb_tm[:, :],
                                        ALU.add)
                        e.tensor_tensor(s1[:, :], s1[:, :], s3[:, :], ALU.mult)

                if KDBG and n == 16:
                    with nc.Block(no_gpsimd_drain=True) as blk:
                        @blk.sync
                        def _(e):
                            dbg_dump(e, "yg_tm", s1[:, :])
                transpose8(s1, 0)
                with nc.Block(no_gpsimd_drain=True) as blk:
                    @blk.vector
                    def _(e):
                        o = slot * 1024
                        e.tensor_copy(yg_hi[:, o:o + 1024], ps[:, 0:1024])
                        e.tensor_tensor(yg_lo[:, o:o + 1024], ps[:, 0:1024],
                                        yg_hi[:, o:o + 1024], ALU.subtract)

            def chunk_part3(n, slot):
                tok0 = n * T
                own_t0 = (n - OWN0) * T
                with nc.Block(no_gpsimd_drain=True) as blk:
                    @blk.tensor
                    def _(e):
                        for half in range(2):
                            out = bank(half)
                            first = True
                            for cc in range(NCC):
                                o = slot * 1024 + cc * 128
                                woff = cc * 1024 + half * 512
                                e.matmul(out, yg_hi[:, o:o + 128],
                                         wA_hi[:, woff:woff + 512],
                                         start=first, stop=False)
                                first = False
                                e.matmul(out, yg_hi[:, o:o + 128],
                                         wA_lo[:, woff:woff + 512],
                                         start=False, stop=False)
                                e.matmul(out, yg_lo[:, o:o + 128],
                                         wA_hi[:, woff:woff + 512],
                                         start=False, stop=(cc == NCC - 1))

                    @blk.sync
                    def _(e):
                        dma(e, x_tm[:, :], xin[tok0:tok0 + T, :], "sx")

                with nc.Block(no_gpsimd_drain=True) as blk:
                    @blk.vector
                    def _(e):
                        e.tensor_copy(s1[:, :], ps[:, 0:1024])

                if KDBG and n == 16:
                    with nc.Block(no_gpsimd_drain=True) as blk:
                        @blk.sync
                        def _(e):
                            dbg_dump(e, "attnout", s1[:, :])

                ln_plain(s1[:, :], s2[:, :], s3[:, :])
                with nc.Block(no_gpsimd_drain=True) as blk:
                    @blk.vector
                    def _(e):
                        wt(e, "sx")
                        e.tensor_tensor(s1[:, :], x_tm[:, :], s2[:, :], ALU.add)

                with nc.Block(no_gpsimd_drain=True) as blk:
                    @blk.sync
                    def _(e):
                        dma(e, x1sp[own_t0:own_t0 + T, :], s1[:, :], "ssp")

            def partial_out():
                for j in range(16):
                    with nc.Block(no_gpsimd_drain=True) as blk:
                        @blk.sync
                        def _(e, j=j):
                            if KPART in ("attn",):
                                wt(e, "ssp")
                                dma(e, x_tm[:, :], x1sp[j * T:(j + 1) * T, :],
                                    "sx")
                            else:
                                dma(e, x_tm[:, :],
                                    xin[OWNTOK + j * T:OWNTOK + (j + 1) * T, :],
                                    "sx")

                    with nc.Block(no_gpsimd_drain=True) as blk:
                        @blk.sync
                        def _(e, j=j):
                            wt(e, "sx")
                            dma(e, xout[j * T:(j + 1) * T, :], x_tm[:, :],
                                "sout")

                with nc.Block(no_gpsimd_drain=True) as blk:
                    @blk.sync
                    def _(e):
                        wt(e, "sout")

            if KPART == "setup":
                partial_out()
                return nc
            load_wsplit(wA_hi, wA_lo, wT["wk"])
            load_wsplit(wB_hi, wB_lo, wT["wv"])
            for n in range(0, KCTX):
                chunk_part1(n, n % 2, is_ctx=True)
            if KPART == "ctx":
                partial_out()
                return nc
            for g in range(KGRP):
                n0 = OWN0 + 2 * g
                if g > 0:
                    load_wsplit(wA_hi, wA_lo, wT["wk"])
                    load_wsplit(wB_hi, wB_lo, wT["wv"])
                for i in range(2):
                    chunk_part1(n0 + i, i, is_ctx=False)
                if KPART == "ga":
                    partial_out()
                    return nc
                load_wsplit(wA_hi, wA_lo, wT["wr"])
                load_wsplit(wB_hi, wB_lo, wT["wg"])
                for i in range(2):
                    chunk_part2(n0 + i, i)
                if KPART == "gb":
                    partial_out()
                    return nc
                load_wsplit(wA_hi, wA_lo, wT["wo"])
                for i in range(2):
                    chunk_part3(n0 + i, i)
            if KPART == "attn":
                partial_out()
                return nc

        with ExitStack() as _f:
            h_hi = _f.enter_context(nc.sbuf_tensor("h_hi", [128, 8 * 2048], BF16))
            h_lo = _f.enter_context(nc.sbuf_tensor("h_lo", [128, 8 * 2048], BF16))
            rw_hi = _f.enter_context(nc.sbuf_tensor("rw_hi", [128, 64], BF16))
            rw_lo = _f.enter_context(nc.sbuf_tensor("rw_lo", [128, 64], BF16))
            f1 = _f.enter_context(nc.sbuf_tensor("f1", [128, 2048], F32))
            f2 = _f.enter_context(nc.sbuf_tensor("f2", [128, 1024], F32))
            f3 = _f.enter_context(nc.sbuf_tensor("f3", [128, 1024], F32))
            f4 = _f.enter_context(nc.sbuf_tensor("f4", [128, 1024], F32))
            hv_hi = h_hi[:, :].rearrange("p (cc x) -> p cc x", x=2048)
            hv_lo = h_lo[:, :].rearrange("p (cc x) -> p cc x", x=2048)

            with ExitStack() as _h:
                gamb = _h.enter_context(nc.sbuf_tensor("gamb", [128, 1024], F32))
                betb = _h.enter_context(nc.sbuf_tensor("betb", [128, 1024], F32))
                rowf = _h.enter_context(nc.sbuf_tensor("rowf", [1, 2048], F32))
                x1b = _h.enter_context(nc.sbuf_tensor("x1b", [128, 16384], F32))
                sq8 = _h.enter_context(nc.sbuf_tensor("sq8", [128, 8192], F32))
                mus = _h.enter_context(nc.sbuf_tensor("mus", [128, 64], F32))
                x1v = r3(x1b[:, :], 1024)

                with nc.Block(no_gpsimd_drain=True) as blk:
                    @blk.sync
                    def _(e):
                        dma(e, rowf[0:1, :], gbrow_d[0:1, :], "sw")
                        dma(e, r3(f2[:, 0:64], 8),
                            rwT_d[:, :].rearrange("(cc p) x -> p cc x", p=128),
                            "sw")
                        wt(e, "ssp")
                        dma(e, x1v,
                            x1sp.rearrange("(j p) c -> p j c", p=128), "sx")

                with nc.Block(no_gpsimd_drain=True) as blk:
                    @blk.vector
                    def _(e):
                        wt(e, "sw")
                        e.tensor_copy(rw_hi[:, :], f2[:, 0:64])
                        e.tensor_tensor(rw_lo[:, :], f2[:, 0:64], rw_hi[:, :],
                                        ALU.subtract)

                bcast_row(rowf[0:1, 0:1024], gamb[:, :], 1024)
                bcast_row(rowf[0:1, 1024:2048], betb[:, :], 1024)

                def ln_batch(xv, n, gb=None, bb=None):
                    h_ = n // 2
                    sqv = r3(sq8[:, 0:h_ * 1024], 1024)
                    with nc.Block(no_gpsimd_drain=True) as blk:
                        @blk.vector
                        def _(e):
                            e.reduce_sum(mus[:, 0:n], xv, axis=AX.X)
                            e.tensor_tensor(sqv, xv[:, 0:h_], xv[:, 0:h_],
                                            ALU.mult)
                            e.reduce_sum(mus[:, 16:16 + h_], sqv, axis=AX.X)
                            e.tensor_tensor(sqv, xv[:, h_:n], xv[:, h_:n],
                                            ALU.mult)
                            vfence(e, e.reduce_sum(mus[:, 16 + h_:16 + n], sqv,
                                                   axis=AX.X))
                            vfence(e, e.tensor_scalar(mus[:, 0:n], mus[:, 0:n],
                                                      1.0 / 1024, None, ALU.mult))
                            vfence(e, e.tensor_tensor(mus[:, 32:32 + n],
                                                      mus[:, 0:n], mus[:, 0:n],
                                                      ALU.mult))
                            e.scalar_tensor_tensor(mus[:, 16:16 + n],
                                                   mus[:, 16:16 + n], 1.0 / 1024,
                                                   mus[:, 32:32 + n],
                                                   ALU.mult, ALU.subtract)

                    with nc.Block(no_gpsimd_drain=True) as blk:
                        @blk.scalar
                        def _(e):
                            e.activation(mus[:, 16:16 + n], mus[:, 16:16 + n],
                                         AF.Sqrt, bias=epsc[:, 0:1])

                    with nc.Block(no_gpsimd_drain=True) as blk:
                        @blk.vector
                        def _(e):
                            vfence(e, e.reciprocal(mus[:, 16:16 + n],
                                                   mus[:, 16:16 + n]))
                            e.tensor_tensor(xv, xv,
                                            mus[:, 0:n].unsqueeze(2)
                                            .broadcast_to([128, n, 1024]),
                                            ALU.subtract)
                            e.tensor_tensor(xv, xv,
                                            mus[:, 16:16 + n].unsqueeze(2)
                                            .broadcast_to([128, n, 1024]),
                                            ALU.mult)
                            if gb is not None:
                                e.tensor_tensor(xv, xv,
                                                gb[:, :].unsqueeze(1)
                                                .broadcast_to([128, n, 1024]),
                                                ALU.mult)
                                e.tensor_tensor(xv, xv,
                                                bb[:, :].unsqueeze(1)
                                                .broadcast_to([128, n, 1024]),
                                                ALU.add)

                with nc.Block(no_gpsimd_drain=True) as blk:
                    @blk.vector
                    def _(e):
                        wt(e, "sx")

                ln_batch(x1v, 16)
                ln_batch(x1v, 16, gamb, betb)

                def _trans(e, j):
                    for oc in range(NCC):
                        e.transpose(ps[:, (j % 2) * 1024 + oc * 128:
                                       (j % 2) * 1024 + (oc + 1) * 128],
                                    x1b[:, j * 1024 + oc * 128:
                                        j * 1024 + (oc + 1) * 128], ident[:, :])

                def _hevac(e, j):
                    dsth = hv_hi[:, :, j * 128:(j + 1) * 128]
                    dstl = hv_lo[:, :, j * 128:(j + 1) * 128]
                    srcv = r3(ps[:, (j % 2) * 1024:(j % 2) * 1024 + 1024], 128)
                    e.tensor_copy(dsth, srcv)
                    e.tensor_tensor(dstl, srcv, dsth, ALU.subtract)

                def _router(e, j):
                    out = ps[:, 4 * 512 + j * 8:4 * 512 + (j + 1) * 8]
                    for cc in range(NCC):
                        a_hi = hv_hi[:, cc, j * 128:(j + 1) * 128]
                        a_lo = hv_lo[:, cc, j * 128:(j + 1) * 128]
                        e.matmul(out, a_hi, rw_hi[:, cc * 8:cc * 8 + 8],
                                 start=(cc == 0), stop=False)
                        e.matmul(out, a_hi, rw_lo[:, cc * 8:cc * 8 + 8],
                                 start=False, stop=False)
                        e.matmul(out, a_lo, rw_hi[:, cc * 8:cc * 8 + 8],
                                 start=False, stop=(cc == NCC - 1))

                for j in range(18):
                    with nc.Block(no_gpsimd_drain=True) as blk:
                        @blk.tensor
                        def _(e, j=j):
                            if j < 16:
                                _trans(e, j)
                            if 0 <= j - 2 < 16:
                                _router(e, j - 2)
                        if 0 <= j - 1 < 16:
                            @blk.vector
                            def _(e, j=j):
                                _hevac(e, j - 1)

                lg = ps[:, 4 * 512:4 * 512 + 128]
                lgv = r3(lg, 8)
                with nc.Block(no_gpsimd_drain=True) as blk:
                    @blk.vector
                    def _(e):
                        vfence(e, e.reduce_max(mus[:, 0:16], lgv, axis=AX.X))
                        e.tensor_tensor(r3(sq8[:, 0:128], 8), lgv,
                                        mus[:, 0:16].unsqueeze(2)
                                        .broadcast_to([128, 16, 8]),
                                        ALU.subtract)

                with nc.Block(no_gpsimd_drain=True) as blk:
                    @blk.scalar
                    def _(e):
                        e.activation(sq8[:, 128:256], sq8[:, 0:128], AF.Exp)

                with nc.Block(no_gpsimd_drain=True) as blk:
                    @blk.vector
                    def _(e):
                        ev = r3(sq8[:, 128:256], 8)
                        e.reduce_sum(mus[:, 0:16], ev, axis=AX.X)
                        vfence(e, e.reduce_max(mus[:, 16:32], ev, axis=AX.X))
                        vfence(e, e.reciprocal(mus[:, 0:16], mus[:, 0:16]))
                        vfence(e, e.tensor_tensor(mus[:, 32:48], mus[:, 16:32],
                                                  mus[:, 0:16], ALU.mult))
                        e.tensor_tensor(r3(gates[:, :], 8), ev,
                                        mus[:, 16:32].unsqueeze(2)
                                        .broadcast_to([128, 16, 8]), ALU.is_ge)
                        e.tensor_tensor(r3(gates[:, :], 8), r3(gates[:, :], 8),
                                        mus[:, 32:48].unsqueeze(2)
                                        .broadcast_to([128, 16, 8]), ALU.mult)

            if KDBG:
                with nc.Block(no_gpsimd_drain=True) as blk:
                    @blk.sync
                    def _(e):
                        dbg_dump(e, "gates", gates[:, :])

            with ExitStack() as _e2:
                q1 = _e2.enter_context(nc.sbuf_tensor("q1", [128, 8 * 2048], FP8))
                q2 = _e2.enter_context(nc.sbuf_tensor("q2", [128, 16 * 1024], FP8))
                accb = _e2.enter_context(nc.sbuf_tensor("accb", [128, 8 * 512], F32))
                scl1c = _e2.enter_context(nc.sbuf_tensor("scl1c", [128, 16], F32))
                scl2c = _e2.enter_context(nc.sbuf_tensor("scl2c", [128, 8], F32))
                sclrow = _e2.enter_context(nc.sbuf_tensor("sclrow", [1, 2048], F32))
                grow = sclrow
                def quant_weight(dram2d, nchunks, width, qdst, sclc, scl_n,
                                 wstf, fsc, thrb, nthrb):
                    nseg = width // 512
                    rows_per_grp = 2048 // width
                    ngrp = nchunks // rows_per_grp
                    gview = dram2d.rearrange("(g s p) w -> g p s w", p=128,
                                             s=rows_per_grp)

                    def _dma(e, g):
                        dma(e, r3(wstf[:, (g % 2) * 2048:(g % 2) * 2048 + 2048],
                                  width), gview[g], "sw")

                    def _abs(e, g):
                        e.activation(fsc[:, (g % 2) * 2048:(g % 2) * 2048 + 2048],
                                     wstf[:, (g % 2) * 2048:(g % 2) * 2048 + 2048],
                                     AF.Abs)

                    def _mm(e, g):
                        fb = (g % 2) * 2048
                        for s in range(rows_per_grp):
                            for sgi in range(nseg):
                                e.matmul(ps[0:1, 2048 + sgi * 512:
                                            2048 + sgi * 512 + 512],
                                         onescol[:, 0:1],
                                         fsc[:, fb + s * width + sgi * 512:
                                             fb + s * width + (sgi + 1) * 512],
                                         start=(g == 0 and s == 0),
                                         stop=(g == ngrp - 1 and
                                               s == rows_per_grp - 1))

                    waits = {}
                    for g in range(ngrp + 2):
                        with nc.Block(no_gpsimd_drain=True) as blk:
                            if g < ngrp:
                                @blk.sync
                                def _(e, g=g):
                                    _dma(e, g)
                                waits[g] = cnt["sw"]
                            if 0 <= g - 1 < ngrp:
                                @blk.scalar
                                def _(e, g=g):
                                    e.wait_ge(sems["sw"], waits[g - 1])
                                    _abs(e, g - 1)
                            if 0 <= g - 2 < ngrp:
                                @blk.tensor
                                def _(e, g=g):
                                    _mm(e, g - 2)

                    with nc.Block(no_gpsimd_drain=True) as blk:
                        @blk.vector
                        def _(e):
                            e.tensor_scalar(sclrow[0:1, 0:width],
                                            ps[0:1, 2048:2048 + width],
                                            1.0 / (nchunks * 128), 1e-5,
                                            ALU.mult, ALU.max)

                    with nc.Block(no_gpsimd_drain=True) as blk:
                        @blk.tensor
                        def _(e):
                            for q in range(scl_n):
                                e.matmul(ps[:, 2048 + q:2048 + q + 1],
                                         sclrow[0:1, q * 128:(q + 1) * 128],
                                         ones_r[0:1, 0:1], start=True, stop=True)
                            for sgi in range(nseg):
                                e.matmul(ps[:, sgi * 512:(sgi + 1) * 512],
                                         ones_r[0:1, :],
                                         sclrow[0:1, sgi * 512:(sgi + 1) * 512],
                                         start=True, stop=True)

                    with nc.Block(no_gpsimd_drain=True) as blk:
                        @blk.vector
                        def _(e):
                            e.tensor_copy(sclc[:, 0:scl_n],
                                          ps[:, 2048:2048 + scl_n])
                            e.tensor_scalar(thrb[:, 0:width], ps[:, 0:width],
                                            0.5, None, ALU.mult)
                            e.tensor_scalar(nthrb[:, 0:width], thrb[:, 0:width],
                                            -1.0, None, ALU.mult)

                    def _quant(e, g):
                        fb = (g % 2) * 2048
                        for s in range(rows_per_grp):
                            wsl = wstf[:, fb + s * width:fb + (s + 1) * width]
                            fsl = fsc[:, s * width:(s + 1) * width]
                            ch = g * rows_per_grp + s
                            e.tensor_tensor(fsl, wsl, thrb[:, 0:width], ALU.is_ge)
                            e.tensor_tensor(wsl, wsl, nthrb[:, 0:width], ALU.is_le)
                            e.tensor_tensor(qdst[:, ch * width:(ch + 1) * width],
                                            fsl, wsl, ALU.subtract)

                    for g in range(ngrp + 1):
                        with nc.Block(no_gpsimd_drain=True) as blk:
                            if g < ngrp:
                                @blk.sync
                                def _(e, g=g):
                                    _dma(e, g)
                                waits[g] = cnt["sw"]
                            if 0 <= g - 1 < ngrp:
                                @blk.vector
                                def _(e, g=g):
                                    e.wait_ge(sems["sw"], waits[g - 1])
                                    _quant(e, g - 1)

                for eidx in list(range(8))[:max(0, KEXP - 1)] + [8]:
                    shared = (eidx == 8)
                    w1d = sw1T if shared else ew1T[eidx, :, :]
                    w2d = sw2T if shared else ew2T[eidx, :, :]
                    _qe = ExitStack()
                    wstf = _qe.enter_context(
                        nc.sbuf_tensor(f"wstf{eidx}", [128, 4096], F32))
                    fsc = _qe.enter_context(
                        nc.sbuf_tensor(f"fsc{eidx}", [128, 4096], F32))
                    thrb = _qe.enter_context(
                        nc.sbuf_tensor(f"thrb{eidx}", [128, 2048], F32))
                    nthrb = _qe.enter_context(
                        nc.sbuf_tensor(f"nthrb{eidx}", [128, 2048], F32))
                    quant_weight(w1d, NCC, 2048, q1, scl1c, 16,
                                 wstf, fsc, thrb, nthrb)
                    quant_weight(w2d, 16, 1024, q2, scl2c, 8,
                                 wstf, fsc, thrb, nthrb)
                    _qe.close()

                    if not shared:
                        with nc.Block(no_gpsimd_drain=True) as blk:
                            @blk.tensor
                            def _(e, eidx=eidx):
                                for j in range(16):
                                    e.transpose(ps[0:1, j * 128:(j + 1) * 128],
                                                gates[:, j * 8 + eidx:
                                                      j * 8 + eidx + 1],
                                                ident[:, :])

                        with nc.Block(no_gpsimd_drain=True) as blk:
                            @blk.vector
                            def _(e):
                                e.tensor_copy(grow[0:1, :], ps[0:1, 0:2048])

                    if KDBG and eidx == 0:
                        with nc.Block(no_gpsimd_drain=True) as blk:
                            @blk.vector
                            def _(e):
                                e.tensor_copy(f1[:, 0:16], scl1c[:, :])
                                e.tensor_copy(f1[:, 16:24], scl2c[:, :])
                                e.tensor_copy(f2[:, 0:1024], q1[:, 0:1024])

                        with nc.Block(no_gpsimd_drain=True) as blk:
                            @blk.sync
                            def _(e):
                                dbg_dump(e, "scls", f1[:, 0:24])
                                dbg_dump(e, "q1c0", f2[:, 0:1024])

                    _me = ExitStack()
                    a1_hi = _me.enter_context(
                        nc.sbuf_tensor(f"a1h{eidx}", [128, 16 * 512], BF16))
                    a1_lo = _me.enter_context(
                        nc.sbuf_tensor(f"a1l{eidx}", [128, 16 * 512], BF16))
                    for tp in range(NTAU):
                        done_sac = cnt["sac"]
                        with nc.Block(no_gpsimd_drain=True) as blk:
                            @blk.sync
                            def _(e, tp=tp, eidx=eidx):
                                e.wait_ge(sems["sac"], done_sac)
                                if eidx > 0:
                                    dma(e, accb[:, :],
                                        accd[:, tp * 4096:(tp + 1) * 4096],
                                        "sx")

                            if not shared:
                                @blk.tensor
                                def _(e, tp=tp):
                                    e.matmul(ps[:, 7 * 512:8 * 512],
                                             ones_r[0:1, :],
                                             grow[0:1, tp * TAU:(tp + 1) * TAU],
                                             start=True, stop=True)
                        acc_ready = cnt["sx"]

                        def silu_pair(e, fcp):
                            b0 = (fcp % 2) * 2
                            for k2 in range(2):
                                fc = fcp * 2 + k2
                                e.activation(
                                    f1[:, (fcp % 2) * 1024 + k2 * 512:
                                       (fcp % 2) * 1024 + (k2 + 1) * 512],
                                    ps[:, (b0 + k2) * 512:(b0 + k2 + 1) * 512],
                                    AF.Silu, scale=scl1c[:, fc:fc + 1])

                        def gate_split_pair(e, fcp, shared=shared):
                            fb = (fcp % 2) * 1024
                            srcap = f1[:, fb:fb + 1024]
                            if not shared:
                                e.tensor_tensor(
                                    r3(f2[:, 0:1024], 512), r3(srcap, 512),
                                    ps[:, 7 * 512:8 * 512].unsqueeze(1)
                                    .broadcast_to([128, 2, 512]), ALU.mult)
                                srcap = f2[:, 0:1024]
                            o = fcp * 1024
                            e.tensor_copy(a1_hi[:, o:o + 1024], srcap)
                            e.tensor_tensor(a1_lo[:, o:o + 1024], srcap,
                                            a1_hi[:, o:o + 1024], ALU.subtract)

                        def mm1_pair(e, fcp, tp):
                            b0 = (fcp % 2) * 2
                            for k2 in range(2):
                                fc = fcp * 2 + k2
                                out = ps[:, (b0 + k2) * 512:(b0 + k2 + 1) * 512]
                                for cc in range(NCC):
                                    qsl = q1[:, cc * 2048 + fc * 128:
                                             cc * 2048 + (fc + 1) * 128]
                                    e.matmul(out, qsl,
                                             hv_hi[:, cc, tp * 512:(tp + 1) * 512],
                                             start=(cc == 0), stop=False)
                                    e.matmul(out, qsl,
                                             hv_lo[:, cc, tp * 512:(tp + 1) * 512],
                                             start=False, stop=(cc == NCC - 1))

                        for fcp in range(10):
                            with nc.Block(no_gpsimd_drain=True) as blk:
                                if fcp < 8:
                                    @blk.tensor
                                    def _(e, fcp=fcp, tp=tp):
                                        mm1_pair(e, fcp, tp)
                                if 0 <= fcp - 1 < 8:
                                    @blk.scalar
                                    def _(e, fcp=fcp):
                                        silu_pair(e, fcp - 1)
                                if 0 <= fcp - 2 < 8:
                                    @blk.vector
                                    def _(e, fcp=fcp):
                                        gate_split_pair(e, fcp - 2)

                        def mm2_pair(e, dcp, tp):
                            b0 = 2 + (dcp % 2) * 2
                            for k2 in range(2):
                                dc = dcp * 2 + k2
                                out = ps[:, (b0 + k2) * 512:(b0 + k2 + 1) * 512]
                                for fc in range(16):
                                    qsl = q2[:, fc * 1024 + dc * 128:
                                             fc * 1024 + (dc + 1) * 128]
                                    e.matmul(out, qsl,
                                             a1_hi[:, fc * 512:(fc + 1) * 512],
                                             start=(fc == 0), stop=False)
                                    e.matmul(out, qsl,
                                             a1_lo[:, fc * 512:(fc + 1) * 512],
                                             start=False, stop=(fc == 15))

                        def acc_pair(e, dcp, eidx=eidx, acc_ready=acc_ready):
                            if eidx > 0:
                                e.wait_ge(sems["sx"], acc_ready)
                            b0 = 2 + (dcp % 2) * 2
                            for k2 in range(2):
                                dc = dcp * 2 + k2
                                psl = ps[:, (b0 + k2) * 512:(b0 + k2 + 1) * 512]
                                asl = accb[:, dc * 512:(dc + 1) * 512]
                                if eidx == 0:
                                    e.tensor_scalar(asl, psl,
                                                    scl2c[:, dc:dc + 1],
                                                    None, ALU.mult)
                                else:
                                    e.scalar_tensor_tensor(
                                        asl, psl, scl2c[:, dc:dc + 1], asl,
                                        ALU.mult, ALU.add)

                        for dcp in range(5):
                            with nc.Block(no_gpsimd_drain=True) as blk:
                                if dcp < 4:
                                    @blk.tensor
                                    def _(e, dcp=dcp, tp=tp):
                                        mm2_pair(e, dcp, tp)
                                if dcp >= 1:
                                    @blk.vector
                                    def _(e, dcp=dcp):
                                        acc_pair(e, dcp - 1)

                        with nc.Block(no_gpsimd_drain=True) as blk:
                            @blk.sync
                            def _(e, tp=tp):
                                dma(e, accd[:, tp * 4096:(tp + 1) * 4096],
                                    accb[:, :], "sac")
                    _me.close()

            with ExitStack() as _z:
                accf = _z.enter_context(nc.sbuf_tensor("accf", [128, 8192], F32))
                x1f = _z.enter_context(nc.sbuf_tensor("x1f", [128, 8192], F32))
                zf = _z.enter_context(nc.sbuf_tensor("zf", [128, 8192], F32))
                sqf = _z.enter_context(nc.sbuf_tensor("sqf", [128, 4096], F32))
                musf = _z.enter_context(nc.sbuf_tensor("musf", [128, 48], F32))

                with nc.Block(no_gpsimd_drain=True) as blk:
                    @blk.sync
                    def _(e):
                        wt(e, "sac")

                for hf in range(2):
                    with nc.Block(no_gpsimd_drain=True) as blk:
                        @blk.sync
                        def _(e, hf=hf):
                            wt(e, "sout")
                            dma(e, accf[:, :],
                                accd[:, hf * 8192:(hf + 1) * 8192], "sx")
                            dma(e, r3(x1f[:, :], 1024),
                                x1sp[hf * 1024:(hf + 1) * 1024, :]
                                .rearrange("(j p) c -> p j c", p=128), "sx")

                    with nc.Block(no_gpsimd_drain=True) as blk:
                        @blk.vector
                        def _(e):
                            wt(e, "sx")

                    def _ftr(e, j8):
                        tp_l, sub = j8 // 4, j8 % 4
                        for dc in range(NCC):
                            e.transpose(
                                ps[:, (j8 % 2) * 1024 + dc * 128:
                                   (j8 % 2) * 1024 + (dc + 1) * 128],
                                accf[:, tp_l * 4096 + dc * 512 + sub * 128:
                                     tp_l * 4096 + dc * 512 + sub * 128 + 128],
                                ident[:, :])

                    def _fev(e, j8):
                        e.tensor_copy(
                            zf[:, j8 * 1024:(j8 + 1) * 1024],
                            ps[:, (j8 % 2) * 1024:(j8 % 2) * 1024 + 1024])

                    for j8 in range(9):
                        with nc.Block(no_gpsimd_drain=True) as blk:
                            if j8 < 8:
                                @blk.tensor
                                def _(e, j8=j8):
                                    _ftr(e, j8)
                            if j8 >= 1:
                                @blk.vector
                                def _(e, j8=j8):
                                    _fev(e, j8 - 1)

                    zv8 = r3(zf[:, :], 1024)
                    sq4 = r3(sqf[:, :], 1024)
                    with nc.Block(no_gpsimd_drain=True) as blk:
                        @blk.vector
                        def _(e):
                            e.reduce_sum(musf[:, 0:8], zv8, axis=AX.X)
                            e.tensor_tensor(sq4, zv8[:, 0:4], zv8[:, 0:4],
                                            ALU.mult)
                            e.reduce_sum(musf[:, 16:20], sq4, axis=AX.X)
                            e.tensor_tensor(sq4, zv8[:, 4:8], zv8[:, 4:8],
                                            ALU.mult)
                            vfence(e, e.reduce_sum(musf[:, 20:24], sq4,
                                                   axis=AX.X))
                            vfence(e, e.tensor_scalar(musf[:, 0:8], musf[:, 0:8],
                                                      1.0 / 1024, None, ALU.mult))
                            vfence(e, e.tensor_tensor(musf[:, 32:40],
                                                      musf[:, 0:8], musf[:, 0:8],
                                                      ALU.mult))
                            e.scalar_tensor_tensor(musf[:, 16:24], musf[:, 16:24],
                                                   1.0 / 1024, musf[:, 32:40],
                                                   ALU.mult, ALU.subtract)

                    with nc.Block(no_gpsimd_drain=True) as blk:
                        @blk.scalar
                        def _(e):
                            e.activation(musf[:, 16:24], musf[:, 16:24], AF.Sqrt,
                                         bias=epsc[:, 0:1])

                    with nc.Block(no_gpsimd_drain=True) as blk:
                        @blk.vector
                        def _(e):
                            vfence(e, e.reciprocal(musf[:, 16:24],
                                                   musf[:, 16:24]))
                            e.tensor_tensor(zv8, zv8,
                                            musf[:, 0:8].unsqueeze(2)
                                            .broadcast_to([128, 8, 1024]),
                                            ALU.subtract)
                            e.tensor_tensor(zv8, zv8,
                                            musf[:, 16:24].unsqueeze(2)
                                            .broadcast_to([128, 8, 1024]),
                                            ALU.mult)
                            e.tensor_tensor(zf[:, :], zf[:, :], x1f[:, :],
                                            ALU.add)

                    with nc.Block(no_gpsimd_drain=True) as blk:
                        @blk.sync
                        def _(e, hf=hf):
                            dma(e, xout[hf * 1024:(hf + 1) * 1024, :]
                                .rearrange("(j p) c -> p j c", p=128),
                                r3(zf[:, :], 1024), "sout")

            with nc.Block(no_gpsimd_drain=True) as blk:
                @blk.sync
                def _(e):
                    wt(e, "sout")

    return nc


_NC_CACHE = None


def kernel(**inputs):
    global _NC_CACHE
    if _NC_CACHE is None:
        _NC_CACHE = build_nc()
    nc = _NC_CACHE

    x = np.asarray(inputs["L0_attn_xq"], dtype=np.float32)
    tm = {m: np.asarray(inputs[f"L0_attn_time_mix_{m}"], np.float32).reshape(C)
          for m in ("k", "v", "r", "g")}
    tmix = np.zeros((128, 32), np.float32)
    for mi, m in enumerate(("k", "v", "r", "g")):
        tmix[:, mi * 8:(mi + 1) * 8] = tm[m].reshape(8, 128).T
    tdu = np.concatenate([
        np.asarray(inputs["L0_attn_time_decay"], np.float32),
        np.asarray(inputs["L0_attn_time_faaaa"], np.float32)]).reshape(1, 32)
    lnrow = np.concatenate([
        np.asarray(inputs["L0_attn_ln_w"], np.float32),
        np.asarray(inputs["L0_attn_ln_b"], np.float32)]).reshape(1, 2048)
    gbrow = np.concatenate([
        np.asarray(inputs["L0_ffn_ln_gamma"], np.float32),
        np.asarray(inputs["L0_ffn_ln_beta"], np.float32)]).reshape(1, 2048)

    def t2(a):
        return np.ascontiguousarray(np.asarray(a, np.float32).T)

    com = {
        "wrT": t2(inputs["L0_attn_W_r"]), "wkT": t2(inputs["L0_attn_W_k"]),
        "wvT": t2(inputs["L0_attn_W_v"]), "wgT": t2(inputs["L0_attn_W_g"]),
        "woT": t2(inputs["L0_attn_W_o"]),
        "tmix": tmix, "tdu": tdu, "lnrow": lnrow, "gbrow": gbrow,
        "sw1T": t2(inputs["L0_ffn_shared_w1"]),
        "sw2T": t2(inputs["L0_ffn_shared_w2"]),
        "ew1T": np.ascontiguousarray(
            np.asarray(inputs["L0_ffn_expert_w1"], np.float32)
            .transpose(0, 2, 1)),
        "ew2T": np.ascontiguousarray(
            np.asarray(inputs["L0_ffn_expert_w2"], np.float32)
            .transpose(0, 2, 1)),
        "rwT": t2(inputs["L0_ffn_router_w"]),
        "ident": np.eye(128, dtype=np.float32),
    }

    in_maps = []
    for core in range(8):
        b, half = core // 2, core % 2
        own = x[b, half * OWNTOK:(half + 1) * OWNTOK, :]
        ctx = (x[b, 0:OWNTOK, :] if half == 1
               else np.zeros((OWNTOK, C), np.float32))
        m = dict(com)
        m["xin"] = np.ascontiguousarray(np.concatenate([ctx, own], axis=0))
        in_maps.append(m)

    res = run_bass_kernel_spmd(nc, in_maps, core_ids=list(range(8)))
    out = np.empty((B, TT, C), np.float32)
    for core in range(8):
        b, half = core // 2, core % 2
        out[b, half * OWNTOK:(half + 1) * OWNTOK, :] = res.results[core]["xout"]
    return out


# version 11
# speedup vs baseline: 1.3009x; 1.0171x over previous
import os
import numpy as np
import concourse.bass as bass
import concourse.mybir as mybir
from concourse.bass_utils import run_bass_kernel_spmd

F32 = mybir.dt.float32
BF16 = mybir.dt.bfloat16
FP8 = mybir.dt.float8e4
ALU = mybir.AluOpType
AF = mybir.ActivationFunctionType
AX = mybir.AxisListType

B, TT, C = 4, 4096, 1024
H, T, K = 16, 128, 64
NCC = C // 128
OWN0 = 16
OWNTOK = 2048
FDIM = 2048
GEPS = 64e-5
EPS = 1e-5
TAU = 512
NTAU = OWNTOK // TAU

M_K, M_V, M_R, M_G = 0, 1, 2, 3
KPART = os.environ.get("KPART", "full")
KDBG = int(os.environ.get("KDBG", "0"))
DBGN = int(os.environ.get("DBGN", "16"))
KCTX = int(os.environ.get("KCTX", "16"))
KGRP = int(os.environ.get("KGRP", "8"))
KEXP = int(os.environ.get("KEXP", "9"))


def build_nc():
    nc = bass.Bass("TRN2", target_bir_lowering=False)

    xin = nc.dram_tensor("xin", [TT, C], F32, kind="ExternalInput").ap()
    wT = {}
    for nm in ("wr", "wk", "wv", "wg", "wo"):
        wT[nm] = nc.dram_tensor(nm + "T", [C, C], F32, kind="ExternalInput").ap()
    tmix_d = nc.dram_tensor("tmix", [128, 32], F32, kind="ExternalInput").ap()
    tdu_d = nc.dram_tensor("tdu", [1, 32], F32, kind="ExternalInput").ap()
    lnrow_d = nc.dram_tensor("lnrow", [1, 2048], F32, kind="ExternalInput").ap()
    gbrow_d = nc.dram_tensor("gbrow", [1, 2048], F32, kind="ExternalInput").ap()
    sw1T = nc.dram_tensor("sw1T", [C, FDIM], F32, kind="ExternalInput").ap()
    sw2T = nc.dram_tensor("sw2T", [FDIM, C], F32, kind="ExternalInput").ap()
    ew1T = nc.dram_tensor("ew1T", [8, C, FDIM], F32, kind="ExternalInput").ap()
    ew2T = nc.dram_tensor("ew2T", [8, FDIM, C], F32, kind="ExternalInput").ap()
    rwT_d = nc.dram_tensor("rwT", [C, 8], F32, kind="ExternalInput").ap()
    ident_d = nc.dram_tensor("ident", [128, 128], F32, kind="ExternalInput").ap()
    x1sp = nc.dram_tensor("x1sp", [OWNTOK, C], F32,
                          **({"kind": "ExternalOutput"} if KDBG else {})).ap()
    accd = nc.dram_tensor("accd", [128, 8 * OWNTOK], F32,
                          **({"kind": "ExternalOutput"} if KDBG else {})).ap()
    xout = nc.dram_tensor("xout", [OWNTOK, C], F32, kind="ExternalOutput").ap()

    sems = {}
    cnt = {}
    dbg_outs = {}

    def mkdbg(name, shape):
        dbg_outs[name] = nc.dram_tensor("dbg_" + name, shape, F32,
                                        kind="ExternalOutput").ap()
        return dbg_outs[name]

    def dma(e, out, in_, s):
        e.dma_start(out=out, in_=in_).then_inc(sems[s], 16)
        cnt[s] += 16

    def wt(e, s):
        e.wait_ge(sems[s], cnt[s])

    def dbg_dump(e, name, ap):
        if not KDBG:
            return
        if name not in dbg_outs:
            mkdbg(name, list(ap.shape))
        dma(e, dbg_outs[name][tuple(slice(0, s) for s in ap.shape)], ap, "sout")

    def rep3(ap_cols, blk):
        return ap_cols.unsqueeze(2).broadcast_to([128, ap_cols.shape[1], blk])

    def r3(ap2d, blk):
        return ap2d.rearrange("p (n b) -> p n b", b=blk)

    from contextlib import ExitStack
    with ExitStack() as _g:
        _sw = _g.enter_context(nc.semaphore("sw"))
        _sx = _g.enter_context(nc.semaphore("sx"))
        _ssp = _g.enter_context(nc.semaphore("ssp"))
        _sac = _g.enter_context(nc.semaphore("sac"))
        _sout = _g.enter_context(nc.semaphore("sout"))
        _svf = _g.enter_context(nc.semaphore("svf"))
        ident = _g.enter_context(nc.sbuf_tensor("ident_s", [128, 128], F32))
        ones_r = _g.enter_context(nc.sbuf_tensor("ones_r", [1, 128], F32))
        onescol = _g.enter_context(nc.sbuf_tensor("onescol", [128, 1], F32))
        tmv = _g.enter_context(nc.sbuf_tensor("tmv", [128, 32], F32))
        gates = _g.enter_context(nc.sbuf_tensor("gates", [128, 128], F32))
        stat = _g.enter_context(nc.sbuf_tensor("stat", [128, 12], F32))
        epsc = _g.enter_context(nc.sbuf_tensor("epsc", [128, 2], F32))
        ps = _g.enter_context(nc.psum_tensor("ps", [128, 4096], F32))
        sems.update(sw=_sw, sx=_sx, ssp=_ssp, sac=_sac, sout=_sout, vf=_svf)
        cnt.update(sw=0, sx=0, ssp=0, sac=0, sout=0, vf=0)

        def vfence(e, inst):
            inst.then_inc(sems["vf"], 1)
            cnt["vf"] += 1
            e.wait_ge(sems["vf"], cnt["vf"])

        def bank(i):
            return ps[:, i * 512:(i + 1) * 512]

        def bcast_row(row_ap, dst_sbuf_ap, width):
            with nc.Block(no_gpsimd_drain=True) as blk:
                @blk.tensor
                def _(e):
                    for sgi in range(width // 512):
                        e.matmul(ps[:, sgi * 512:sgi * 512 + 512],
                                 ones_r[0:1, :],
                                 row_ap[0:1, sgi * 512:(sgi + 1) * 512],
                                 start=True, stop=True)

            with nc.Block(no_gpsimd_drain=True) as blk:
                @blk.vector
                def _(e):
                    e.tensor_copy(dst_sbuf_ap, ps[:, 0:width])

        def ln_plain(xap, outap, scr, eps_ap=None):
            with nc.Block(no_gpsimd_drain=True) as blk:
                @blk.vector
                def _(e):
                    e.reduce_sum(stat[:, 0:1], xap, axis=AX.X)
                    e.tensor_tensor(scr, xap, xap, ALU.mult)
                    e.reduce_sum(stat[:, 1:2], scr, axis=AX.X)
                    vfence(e, e.tensor_scalar(stat[:, 0:1], stat[:, 0:1],
                                               1.0 / 1024, None, ALU.mult))
                    vfence(e, e.tensor_tensor(stat[:, 2:3], stat[:, 0:1],
                                              stat[:, 0:1], ALU.mult))
                    e.scalar_tensor_tensor(stat[:, 2:3], stat[:, 1:2], 1.0 / 1024,
                                           stat[:, 2:3], ALU.mult, ALU.subtract)

            with nc.Block(no_gpsimd_drain=True) as blk:
                @blk.scalar
                def _(e):
                    e.activation(stat[:, 3:4], stat[:, 2:3], AF.Sqrt,
                                 bias=(eps_ap if eps_ap is not None
                                       else epsc[:, 0:1]))

            with nc.Block(no_gpsimd_drain=True) as blk:
                @blk.vector
                def _(e):
                    vfence(e, e.reciprocal(stat[:, 2:3], stat[:, 3:4]))
                    e.tensor_scalar(outap, xap, stat[:, 0:1], stat[:, 2:3],
                                    ALU.subtract, ALU.mult)

        with ExitStack() as _a:
            wA_hi = _a.enter_context(nc.sbuf_tensor("wA_hi", [128, 8192], BF16))
            wA_lo = _a.enter_context(nc.sbuf_tensor("wA_lo", [128, 8192], BF16))
            wB_hi = _a.enter_context(nc.sbuf_tensor("wB_hi", [128, 8192], BF16))
            wB_lo = _a.enter_context(nc.sbuf_tensor("wB_lo", [128, 8192], BF16))
            wstage = _a.enter_context(nc.sbuf_tensor("wstage", [128, 2048], F32))
            x_tm = _a.enter_context(nc.sbuf_tensor("x_tm", [128, 1024], F32))
            xln_tm = _a.enter_context(nc.sbuf_tensor("xln_tm", [128, 1024], F32))
            xlnf = _a.enter_context(nc.sbuf_tensor("xlnf", [128, 2048], F32))
            mix_hi = _a.enter_context(nc.sbuf_tensor("mix_hi", [128, 1024], BF16))
            mix_lo = _a.enter_context(nc.sbuf_tensor("mix_lo", [128, 1024], BF16))
            mix2_hi = _a.enter_context(nc.sbuf_tensor("mix2_hi", [128, 1024], BF16))
            mix2_lo = _a.enter_context(nc.sbuf_tensor("mix2_lo", [128, 1024], BF16))
            atw2_hi = _a.enter_context(nc.sbuf_tensor("atw2_hi", [128, 1024], BF16))
            atw2_lo = _a.enter_context(nc.sbuf_tensor("atw2_lo", [128, 1024], BF16))
            dxf = _a.enter_context(nc.sbuf_tensor("dxf", [128, 1024], F32))
            s1 = _a.enter_context(nc.sbuf_tensor("s1", [128, 1024], F32))
            s2 = _a.enter_context(nc.sbuf_tensor("s2", [128, 1024], F32))
            s3 = _a.enter_context(nc.sbuf_tensor("s3", [128, 1024], F32))
            sbig = _a.enter_context(nc.sbuf_tensor("sbig", [128, 1024], F32))
            kwk_hi = _a.enter_context(nc.sbuf_tensor("kwk_hi", [128, 1024], BF16))
            kwk_lo = _a.enter_context(nc.sbuf_tensor("kwk_lo", [128, 1024], BF16))
            kfm_hi = _a.enter_context(nc.sbuf_tensor("kfm_hi", [64, 4096], BF16))
            kfm_lo = _a.enter_context(nc.sbuf_tensor("kfm_lo", [64, 4096], BF16))
            vtm_hi = _a.enter_context(nc.sbuf_tensor("vtm_hi", [128, 2048], BF16))
            vtm_lo = _a.enter_context(nc.sbuf_tensor("vtm_lo", [128, 2048], BF16))
            ssn_hi = _a.enter_context(nc.sbuf_tensor("ssn_hi", [64, 2048], BF16))
            ssn_lo = _a.enter_context(nc.sbuf_tensor("ssn_lo", [64, 2048], BF16))
            rfm_hi = _a.enter_context(nc.sbuf_tensor("rfm_hi", [64, 2048], BF16))
            rfm_lo = _a.enter_context(nc.sbuf_tensor("rfm_lo", [64, 2048], BF16))
            atw_hi = _a.enter_context(nc.sbuf_tensor("atw_hi", [128, 1024], BF16))
            atw_lo = _a.enter_context(nc.sbuf_tensor("atw_lo", [128, 1024], BF16))
            yg_hi = _a.enter_context(nc.sbuf_tensor("yg_hi", [128, 2048], BF16))
            yg_lo = _a.enter_context(nc.sbuf_tensor("yg_lo", [128, 2048], BF16))
            s_f32 = _a.enter_context(nc.sbuf_tensor("s_f32", [64, 1024], F32))
            s_hi = _a.enter_context(nc.sbuf_tensor("s_hi", [64, 1024], BF16))
            s_lo = _a.enter_context(nc.sbuf_tensor("s_lo", [64, 1024], BF16))
            wmT = _a.enter_context(nc.sbuf_tensor("wmT", [128, 2048], F32))
            bcol = _a.enter_context(nc.sbuf_tensor("bcol", [128, 8], F32))
            bhist = _a.enter_context(nc.sbuf_tensor("bhist", [128, 16], F32))
            lndb = _a.enter_context(nc.sbuf_tensor("lndb", [128, 16], F32))
            ub = _a.enter_context(nc.sbuf_tensor("ub", [128, 16], F32))
            wb_t = _a.enter_context(nc.sbuf_tensor("wb_t", [128, 16], F32))
            wk_t = _a.enter_context(nc.sbuf_tensor("wk_t", [128, 16], F32))
            ws_p = _a.enter_context(nc.sbuf_tensor("ws_p", [64, 16], F32))
            lnw_tm = _a.enter_context(nc.sbuf_tensor("lnw_tm", [128, 1024], F32))
            lnb_tm = _a.enter_context(nc.sbuf_tensor("lnb_tm", [128, 1024], F32))
            iotac = _a.enter_context(nc.sbuf_tensor("iotac", [128, 2], F32))
            csc = _a.enter_context(nc.sbuf_tensor("csc", [128, 48], F32))
            dbgst = (_a.enter_context(nc.sbuf_tensor("dbgst", [128, 1024], F32))
                     if KDBG else None)
            with nc.Block() as blk:
                @blk.sync
                def _(e):
                    dma(e, ident[:, :], ident_d[:, :], "sw")
                    dma(e, tmv[:, :], tmix_d[:, :], "sw")
                    dma(e, csc[0:1, 0:32], tdu_d[0:1, :], "sw")
                    dma(e, wstage[0:1, 0:2048], lnrow_d[0:1, :], "sw")

                @blk.gpsimd
                def _(e):
                    e.memset(ones_r[0:1, :], 1.0)
                    e.memset(onescol[:, :], 1.0)
                    e.memset(epsc[:, 0:1], EPS)
                    e.memset(epsc[:, 1:2], GEPS)
                    e.memset(s_f32[:, :], 0.0)
                    e.memset(s_hi[:, :], 0.0)
                    e.memset(s_lo[:, :], 0.0)
                    e.memset(bcol[:, :], 0.0)
                    e.iota(sbig[:, 0:128], [[1, 128]], channel_multiplier=-1,
                           allow_small_or_imprecise_dtypes=True)
                    e.iota(iotac[:, 0:1], [[1, 1]], channel_multiplier=1,
                           allow_small_or_imprecise_dtypes=True)

            with nc.Block(no_gpsimd_drain=True) as blk:
                @blk.scalar
                def _(e):
                    wt(e, "sw")
                    e.activation(csc[0:1, 0:16], csc[0:1, 0:16], AF.Exp)

                @blk.vector
                def _(e):
                    wt(e, "sw")
                    e.tensor_scalar(iotac[:, 1:2], iotac[:, 0:1], -1.0, 127.0,
                                    ALU.mult, ALU.add)
                    e.tensor_scalar(s1[:, 0:128], sbig[:, 0:128], 1.0, 0.0,
                                    ALU.subtract, ALU.max)
                    e.tensor_scalar(s2[:, 0:128], sbig[:, 0:128], 1.0, None,
                                    ALU.is_ge)
                    e.tensor_scalar(s3[:, 0:128], sbig[:, 0:128], 0.0, None,
                                    ALU.is_equal)

            with nc.Block(no_gpsimd_drain=True) as blk:
                @blk.vector
                def _(e):
                    e.tensor_scalar(csc[0:1, 0:16], csc[0:1, 0:16], -1.0, None,
                                    ALU.mult)

            with nc.Block(no_gpsimd_drain=True) as blk:
                @blk.tensor
                def _(e):
                    e.matmul(ps[:, 0:16], ones_r[0:1, :], csc[0:1, 0:16],
                             start=True, stop=False)
                    e.matmul(ps[:, 16:32], ones_r[0:1, :], csc[0:1, 16:32],
                             start=False, stop=True)

            with nc.Block(no_gpsimd_drain=True) as blk:
                @blk.vector
                def _(e):
                    e.tensor_copy(lndb[:, :], ps[:, 0:16])
                    e.tensor_copy(ub[:, :], ps[:, 16:32])

            with nc.Block(no_gpsimd_drain=True) as blk:
                @blk.scalar
                def _(e):
                    for h in range(H):
                        e.activation(wb_t[:, h:h + 1], iotac[:, 0:1], AF.Exp,
                                     scale=lndb[:, h:h + 1])
                        e.activation(wk_t[:, h:h + 1], iotac[:, 1:2], AF.Exp,
                                     scale=lndb[:, h:h + 1])
                    e.activation(csc[:, 0:16], lndb[:, :], AF.Exp, scale=128.0)
                    for h in range(H):
                        e.activation(wmT[:, h * 128:(h + 1) * 128], s1[:, 0:128],
                                     AF.Exp, scale=lndb[:, h:h + 1])

            with nc.Block(no_gpsimd_drain=True) as blk:
                @blk.vector
                def _(e):
                    e.tensor_copy(ws_p[0:64, :], csc[0:64, 0:16])
                    for h in range(H):
                        sl = wmT[:, h * 128:(h + 1) * 128]
                        e.tensor_tensor(sl, sl, s2[:, 0:128], ALU.mult)
                        e.scalar_tensor_tensor(sl, s3[:, 0:128], ub[:, h:h + 1],
                                               sl, ALU.mult, ALU.add)

            bcast_row(wstage[0:1, 0:1024], lnw_tm[:, :], 1024)
            bcast_row(wstage[0:1, 1024:2048], lnb_tm[:, :], 1024)

            def load_wsplit(dsth, dstl, dram):
                def _dma(e, step):
                    dma(e, wstage[:, (step % 2) * 1024:(step % 2) * 1024 + 1024],
                        dram[step * 128:(step + 1) * 128, :], "sw")

                def _split(e, step, when):
                    e.wait_ge(sems["sw"], when)
                    o = step * 1024
                    wsl = wstage[:, (step % 2) * 1024:(step % 2) * 1024 + 1024]
                    e.tensor_copy(dsth[:, o:o + 1024], wsl)
                    e.tensor_tensor(dstl[:, o:o + 1024], wsl,
                                    dsth[:, o:o + 1024], ALU.subtract)

                with nc.Block(no_gpsimd_drain=True) as blk:
                    @blk.sync
                    def _(e):
                        _dma(e, 0)
                for step in range(7):
                    done = cnt["sw"]
                    with nc.Block(no_gpsimd_drain=True) as blk:
                        @blk.sync
                        def _(e, step=step):
                            _dma(e, step + 1)

                        @blk.vector
                        def _(e, step=step, done=done):
                            _split(e, step, done)
                done = cnt["sw"]
                with nc.Block(no_gpsimd_drain=True) as blk:
                    @blk.vector
                    def _(e, done=done):
                        _split(e, 7, done)

            def mix_ops(e, slot, m, mh, ml, scr_a, scr_b):
                cur = r3(xlnf[:, slot * 1024:(slot + 1) * 1024], 128)
                bh = bhist[:, slot * 8:slot * 8 + 8].unsqueeze(2)
                tm3 = rep3(tmv[:, m * 8:(m + 1) * 8], 128)
                e.tensor_tensor(r3(scr_a, 128), r3(dxf[:, :], 128), tm3,
                                ALU.mult)
                e.tensor_tensor(r3(scr_b, 128)[:, :, 1:128], cur[:, :, 0:127],
                                r3(scr_a, 128)[:, :, 1:128], ALU.add)
                vfence(e, e.tensor_tensor(r3(scr_b, 128)[:, :, 0:1], bh,
                                           r3(scr_a, 128)[:, :, 0:1], ALU.add))
                e.tensor_copy(mh[:, :], scr_b)
                e.tensor_tensor(ml[:, :], scr_b, mh[:, :], ALU.subtract)

            def mix_split(slot, m):
                with nc.Block(no_gpsimd_drain=True) as blk:
                    @blk.vector
                    def _(e):
                        mix_ops(e, slot, m, mix_hi, mix_lo, s2[:, :], s3[:, :])

            def proj_ops(e, whi, wlo, pbank0, mh, ml):
                for half in range(2):
                    out = bank(pbank0 + half)
                    first = True
                    for cc in range(NCC):
                        a_hi = mh[:, cc * 128:(cc + 1) * 128]
                        a_lo = ml[:, cc * 128:(cc + 1) * 128]
                        woff = cc * 1024 + half * 512
                        e.matmul(out, a_hi, whi[:, woff:woff + 512],
                                 start=first, stop=False)
                        first = False
                        e.matmul(out, a_hi, wlo[:, woff:woff + 512],
                                 start=False, stop=False)
                        e.matmul(out, a_lo, whi[:, woff:woff + 512],
                                 start=False, stop=(cc == NCC - 1))

            def proj_tm(whi, wlo, pbank0):
                with nc.Block(no_gpsimd_drain=True) as blk:
                    @blk.tensor
                    def _(e):
                        proj_ops(e, whi, wlo, pbank0, mix_hi, mix_lo)

            def transpose8(src, pbank0):
                with nc.Block(no_gpsimd_drain=True) as blk:
                    @blk.tensor
                    def _(e):
                        for oc in range(NCC):
                            e.transpose(ps[:, pbank0 * 512 + oc * 128:
                                           pbank0 * 512 + (oc + 1) * 128],
                                        src[:, oc * 128:(oc + 1) * 128],
                                        ident[:, :])

            def chunk_part1(n, slot, is_ctx):
                tok0 = n * T
                with nc.Block(no_gpsimd_drain=True) as blk:
                    @blk.sync
                    def _(e):
                        dma(e, x_tm[:, :], xin[tok0:tok0 + T, :], "sx")

                with nc.Block(no_gpsimd_drain=True) as blk:
                    @blk.vector
                    def _(e):
                        wt(e, "sx")

                    @blk.sync
                    def _(e):
                        wt(e, "ssp")

                ln_plain(x_tm[:, :], xln_tm[:, :], s1[:, :])
                if KDBG and n == DBGN:
                    with nc.Block(no_gpsimd_drain=True) as blk:
                        @blk.sync
                        def _(e):
                            dbg_dump(e, "xln_tm", xln_tm[:, :])
                transpose8(xln_tm, 0)
                with nc.Block(no_gpsimd_drain=True) as blk:
                    @blk.vector
                    def _(e):
                        vfence(e, e.tensor_copy(bhist[:, slot * 8:slot * 8 + 8],
                                                bcol[:, :]))
                        vfence(e, e.tensor_copy(
                            xlnf[:, slot * 1024:(slot + 1) * 1024],
                            ps[:, 0:1024]))
                        cur = r3(xlnf[:, slot * 1024:(slot + 1) * 1024], 128)
                        vfence(e, e.tensor_copy(bcol[:, :].unsqueeze(2),
                                                cur[:, :, 127:128]))
                        e.tensor_tensor(r3(dxf[:, :], 128)[:, :, 1:128],
                                        cur[:, :, 1:128], cur[:, :, 0:127],
                                        ALU.subtract)
                        vfence(e, e.tensor_tensor(r3(dxf[:, :], 128)[:, :, 0:1],
                                                  cur[:, :, 0:1],
                                                  bhist[:, slot * 8:slot * 8 + 8]
                                                  .unsqueeze(2), ALU.subtract))
                        mix_ops(e, slot, M_K, mix_hi, mix_lo, s2[:, :], s3[:, :])

                with nc.Block(no_gpsimd_drain=True) as blk:
                    @blk.tensor
                    def _(e):
                        proj_ops(e, wA_hi, wA_lo, 0, mix_hi, mix_lo)

                    @blk.vector
                    def _(e):
                        mix_ops(e, slot, M_V, mix2_hi, mix2_lo, s2[:, :],
                                s3[:, :])

                with nc.Block(no_gpsimd_drain=True) as blk:
                    @blk.tensor
                    def _(e):
                        proj_ops(e, wB_hi, wB_lo, 4, mix2_hi, mix2_lo)

                    @blk.vector
                    def _(e):
                        e.tensor_copy(s1[:, :], ps[:, 0:1024])
                        e.tensor_tensor(r3(s2[:, :], 64), r3(s1[:, :], 64),
                                        rep3(wk_t[:, :], 64), ALU.mult)
                        e.tensor_copy(kwk_hi[:, :], s2[:, :])
                        e.tensor_tensor(kwk_lo[:, :], s2[:, :], kwk_hi[:, :],
                                        ALU.subtract)

                if KDBG and n == DBGN:
                    with nc.Block(no_gpsimd_drain=True) as blk:
                        @blk.sync
                        def _(e):
                            dbg_dump(e, "k_tm", s1[:, :])
                            dbg_dump(e, "mixv_chk", s3[:, :])
                with nc.Block(no_gpsimd_drain=True) as blk:
                    if not is_ctx:
                        @blk.tensor
                        def _(e):
                            for h in range(H):
                                e.transpose(ps[0:64, h * 128:(h + 1) * 128],
                                            s1[:, h * 64:(h + 1) * 64],
                                            ident[:, :])

                    @blk.vector
                    def _(e, is_ctx=is_ctx):
                        o = slot * 1024
                        e.tensor_copy(vtm_hi[:, o:o + 1024], ps[:, 2048:3072])
                        e.tensor_tensor(vtm_lo[:, o:o + 1024], ps[:, 2048:3072],
                                        vtm_hi[:, o:o + 1024], ALU.subtract)
                        if not is_ctx:
                            so = slot * 1024
                            e.tensor_copy(ssn_hi[0:64, so:so + 1024],
                                          s_hi[0:64, :])
                            e.tensor_copy(ssn_lo[0:64, so:so + 1024],
                                          s_lo[0:64, :])

                with nc.Block(no_gpsimd_drain=True) as blk:
                    @blk.tensor
                    def _(e, slot=slot):
                        for h in range(H):
                            o = slot * 1024 + h * 64
                            outsl = ps[0:64, 6 * 512 + h * 64:
                                       6 * 512 + (h + 1) * 64]
                            e.matmul(outsl, kwk_hi[:, h * 64:h * 64 + 64],
                                     vtm_hi[:, o:o + 64],
                                     start=(h % 8 == 0), stop=False)
                            e.matmul(outsl, kwk_hi[:, h * 64:h * 64 + 64],
                                     vtm_lo[:, o:o + 64], start=False,
                                     stop=False)
                            e.matmul(outsl, kwk_lo[:, h * 64:h * 64 + 64],
                                     vtm_hi[:, o:o + 64], start=False,
                                     stop=(h % 8 == 7))

                    if not is_ctx:
                        @blk.vector
                        def _(e):
                            o = slot * 2048
                            e.tensor_copy(kfm_hi[0:64, o:o + 2048],
                                          ps[0:64, 0:2048])
                            e.tensor_tensor(kfm_lo[0:64, o:o + 2048],
                                            ps[0:64, 0:2048],
                                            kfm_hi[0:64, o:o + 2048],
                                            ALU.subtract)

                if KDBG and n == DBGN:
                    with nc.Block(no_gpsimd_drain=True) as blk:
                        @blk.vector
                        def _(e, slot=slot):
                            o = slot * 1024
                            e.tensor_tensor(dbgst[:, 0:1024],
                                            vtm_hi[:, o:o + 1024],
                                            vtm_lo[:, o:o + 1024], ALU.add)

                    with nc.Block(no_gpsimd_drain=True) as blk:
                        @blk.sync
                        def _(e):
                            dbg_dump(e, "v_tm", dbgst[:, 0:1024])
                with nc.Block(no_gpsimd_drain=True) as blk:
                    @blk.vector
                    def _(e):
                        e.tensor_tensor(
                            s_f32[0:64, :].rearrange("p (n b) -> p n b", b=64),
                            s_f32[0:64, :].rearrange("p (n b) -> p n b", b=64),
                            ws_p[0:64, :].unsqueeze(2)
                            .broadcast_to([64, 16, 64]), ALU.mult)
                        e.tensor_tensor(s_f32[0:64, :], s_f32[0:64, :],
                                        ps[0:64, 6 * 512:8 * 512], ALU.add)
                        e.tensor_copy(s_hi[0:64, :], s_f32[0:64, :])
                        e.tensor_tensor(s_lo[0:64, :], s_f32[0:64, :],
                                        s_hi[0:64, :], ALU.subtract)

                if KDBG and n == DBGN:
                    with nc.Block(no_gpsimd_drain=True) as blk:
                        @blk.sync
                        def _(e):
                            dbg_dump(e, "s_next", s_f32[0:64, :])

            def _dump_s(n):
                if KDBG and n == DBGN:
                    with nc.Block(no_gpsimd_drain=True) as blk:
                        @blk.sync
                        def _(e):
                            dbg_dump(e, "s_next", s_f32[0:64, :])

            def chunk_part2(n, slot):
                with nc.Block(no_gpsimd_drain=True) as blk:
                    @blk.vector
                    def _(e):
                        cur = r3(xlnf[:, slot * 1024:(slot + 1) * 1024], 128)
                        e.tensor_tensor(r3(dxf[:, :], 128)[:, :, 1:128],
                                        cur[:, :, 1:128], cur[:, :, 0:127],
                                        ALU.subtract)
                        vfence(e, e.tensor_tensor(r3(dxf[:, :], 128)[:, :, 0:1],
                                                  cur[:, :, 0:1],
                                                  bhist[:, slot * 8:slot * 8 + 8]
                                                  .unsqueeze(2), ALU.subtract))
                        mix_ops(e, slot, M_R, mix_hi, mix_lo, s2[:, :], s3[:, :])

                with nc.Block(no_gpsimd_drain=True) as blk:
                    @blk.tensor
                    def _(e):
                        proj_ops(e, wA_hi, wA_lo, 0, mix_hi, mix_lo)

                    @blk.vector
                    def _(e):
                        mix_ops(e, slot, M_G, mix2_hi, mix2_lo, s2[:, :],
                                s3[:, :])

                with nc.Block(no_gpsimd_drain=True) as blk:
                    @blk.tensor
                    def _(e):
                        proj_ops(e, wB_hi, wB_lo, 4, mix2_hi, mix2_lo)

                    @blk.vector
                    def _(e):
                        e.tensor_copy(s1[:, :], ps[:, 0:1024])

                with nc.Block(no_gpsimd_drain=True) as blk:
                    @blk.tensor
                    def _(e):
                        for h in range(H):
                            e.transpose(ps[0:64, h * 128:(h + 1) * 128],
                                        s1[:, h * 64:(h + 1) * 64], ident[:, :])

                    @blk.scalar
                    def _(e):
                        e.activation(s3[:, :], ps[:, 2048:3072], AF.Silu)

                with nc.Block(no_gpsimd_drain=True) as blk:
                    @blk.vector
                    def _(e):
                        e.tensor_copy(rfm_hi[0:64, :], ps[0:64, 0:2048])
                        e.tensor_tensor(rfm_lo[0:64, :], ps[0:64, 0:2048],
                                        rfm_hi[0:64, :], ALU.subtract)

                def att_mms(e, hh, b0):
                    for hi_ in range(8):
                        h = hh * 8 + hi_
                        ko = slot * 2048 + h * 128
                        ro = h * 128
                        out = ps[:, b0 * 512 + hi_ * 128:
                                 b0 * 512 + (hi_ + 1) * 128]
                        e.matmul(out, kfm_hi[0:64, ko:ko + 128],
                                 rfm_hi[0:64, ro:ro + 128],
                                 start=(hi_ % 4 == 0), stop=False)
                        e.matmul(out, kfm_hi[0:64, ko:ko + 128],
                                 rfm_lo[0:64, ro:ro + 128],
                                 start=False, stop=False)
                        e.matmul(out, kfm_lo[0:64, ko:ko + 128],
                                 rfm_hi[0:64, ro:ro + 128],
                                 start=False, stop=(hi_ % 4 == 3))

                def mask_ops(e, b0, hh, ahi, alo):
                    e.tensor_tensor(sbig[:, :], ps[:, b0 * 512:b0 * 512 + 1024],
                                    wmT[:, hh * 1024:(hh + 1) * 1024], ALU.mult)
                    e.tensor_copy(ahi[:, :], sbig[:, :])
                    e.tensor_tensor(alo[:, :], sbig[:, :], ahi[:, :],
                                    ALU.subtract)

                def out_mms(e, hh, ahi, alo):
                    for hi_ in range(8):
                        h = hh * 8 + hi_
                        vo = slot * 1024 + h * 64
                        ro = h * 128
                        so = slot * 1024 + h * 64
                        o1 = ps[:, 4 * 512 + h * 64:4 * 512 + h * 64 + 64]
                        e.matmul(o1, ahi[:, hi_ * 128:(hi_ + 1) * 128],
                                 vtm_hi[:, vo:vo + 64],
                                 start=(hi_ == 0), stop=False)
                        e.matmul(o1, ahi[:, hi_ * 128:(hi_ + 1) * 128],
                                 vtm_lo[:, vo:vo + 64], start=False, stop=False)
                        e.matmul(o1, alo[:, hi_ * 128:(hi_ + 1) * 128],
                                 vtm_hi[:, vo:vo + 64], start=False,
                                 stop=(hi_ == 7))
                        o2 = ps[:, 6 * 512 + h * 64:6 * 512 + h * 64 + 64]
                        e.matmul(o2, rfm_hi[0:64, ro:ro + 128],
                                 ssn_hi[0:64, so:so + 64],
                                 start=(hi_ == 0), stop=False)
                        e.matmul(o2, rfm_hi[0:64, ro:ro + 128],
                                 ssn_lo[0:64, so:so + 64],
                                 start=False, stop=False)
                        e.matmul(o2, rfm_lo[0:64, ro:ro + 128],
                                 ssn_hi[0:64, so:so + 64],
                                 start=False, stop=(hi_ == 7))

                if KDBG and n == DBGN:
                    with nc.Block(no_gpsimd_drain=True) as blk:
                        @blk.sync
                        def _(e):
                            dbg_dump(e, "g_tm", s3[:, :])

                with nc.Block(no_gpsimd_drain=True) as blk:
                    @blk.tensor
                    def _(e):
                        att_mms(e, 0, 0)

                with nc.Block(no_gpsimd_drain=True) as blk:
                    @blk.tensor
                    def _(e):
                        att_mms(e, 1, 2)

                    @blk.vector
                    def _(e):
                        mask_ops(e, 0, 0, atw_hi, atw_lo)

                with nc.Block(no_gpsimd_drain=True) as blk:
                    @blk.tensor
                    def _(e):
                        out_mms(e, 0, atw_hi, atw_lo)

                    @blk.vector
                    def _(e):
                        mask_ops(e, 2, 1, atw2_hi, atw2_lo)

                with nc.Block(no_gpsimd_drain=True) as blk:
                    @blk.tensor
                    def _(e):
                        out_mms(e, 1, atw2_hi, atw2_lo)

                if KDBG and n == DBGN:
                    with nc.Block(no_gpsimd_drain=True) as blk:
                        @blk.sync
                        def _(e):
                            dbg_dump(e, "attw0", sbig[:, :])
                with nc.Block(no_gpsimd_drain=True) as blk:
                    @blk.vector
                    def _(e):
                        e.tensor_tensor(r3(s1[:, :], 64),
                                        r3(ps[:, 6 * 512:8 * 512], 64),
                                        rep3(wb_t[:, :], 64), ALU.mult)
                        e.tensor_tensor(s1[:, :], s1[:, :],
                                        ps[:, 4 * 512:6 * 512], ALU.add)
                        e.tensor_tensor(s2[:, :], s1[:, :], s1[:, :], ALU.mult)
                        e.reduce_sum(csc[:, 0:16], r3(s1[:, :], 64), axis=AX.X)
                        vfence(e, e.reduce_sum(csc[:, 16:32], r3(s2[:, :], 64),
                                               axis=AX.X))
                        vfence(e, e.tensor_scalar(csc[:, 0:16], csc[:, 0:16],
                                                  1.0 / 64, None, ALU.mult))
                        vfence(e, e.tensor_tensor(csc[:, 32:48], csc[:, 0:16],
                                                  csc[:, 0:16], ALU.mult))
                        e.scalar_tensor_tensor(csc[:, 16:32], csc[:, 16:32],
                                               1.0 / 64, csc[:, 32:48],
                                               ALU.mult, ALU.subtract)

                with nc.Block(no_gpsimd_drain=True) as blk:
                    @blk.scalar
                    def _(e):
                        e.activation(csc[:, 16:32], csc[:, 16:32], AF.Sqrt,
                                     bias=epsc[:, 1:2])

                with nc.Block(no_gpsimd_drain=True) as blk:
                    @blk.vector
                    def _(e):
                        vfence(e, e.reciprocal(csc[:, 16:32], csc[:, 16:32]))
                        e.tensor_tensor(r3(s1[:, :], 64), r3(s1[:, :], 64),
                                        rep3(csc[:, 0:16], 64), ALU.subtract)
                        e.tensor_tensor(r3(s1[:, :], 64), r3(s1[:, :], 64),
                                        rep3(csc[:, 16:32], 64), ALU.mult)
                        e.tensor_tensor(s1[:, :], s1[:, :], lnw_tm[:, :],
                                        ALU.mult)
                        e.tensor_tensor(s1[:, :], s1[:, :], lnb_tm[:, :],
                                        ALU.add)
                        e.tensor_tensor(s1[:, :], s1[:, :], s3[:, :], ALU.mult)

                if KDBG and n == DBGN:
                    with nc.Block(no_gpsimd_drain=True) as blk:
                        @blk.sync
                        def _(e):
                            dbg_dump(e, "yg_tm", s1[:, :])
                transpose8(s1, 0)
                with nc.Block(no_gpsimd_drain=True) as blk:
                    @blk.vector
                    def _(e):
                        o = slot * 1024
                        e.tensor_copy(yg_hi[:, o:o + 1024], ps[:, 0:1024])
                        e.tensor_tensor(yg_lo[:, o:o + 1024], ps[:, 0:1024],
                                        yg_hi[:, o:o + 1024], ALU.subtract)

            def chunk_part3(n, slot):
                tok0 = n * T
                own_t0 = (n - OWN0) * T
                with nc.Block(no_gpsimd_drain=True) as blk:
                    @blk.tensor
                    def _(e):
                        for half in range(2):
                            out = bank(half)
                            first = True
                            for cc in range(NCC):
                                o = slot * 1024 + cc * 128
                                woff = cc * 1024 + half * 512
                                e.matmul(out, yg_hi[:, o:o + 128],
                                         wA_hi[:, woff:woff + 512],
                                         start=first, stop=False)
                                first = False
                                e.matmul(out, yg_hi[:, o:o + 128],
                                         wA_lo[:, woff:woff + 512],
                                         start=False, stop=False)
                                e.matmul(out, yg_lo[:, o:o + 128],
                                         wA_hi[:, woff:woff + 512],
                                         start=False, stop=(cc == NCC - 1))

                    @blk.sync
                    def _(e):
                        dma(e, x_tm[:, :], xin[tok0:tok0 + T, :], "sx")

                with nc.Block(no_gpsimd_drain=True) as blk:
                    @blk.vector
                    def _(e):
                        e.tensor_copy(s1[:, :], ps[:, 0:1024])

                if KDBG and n == DBGN:
                    with nc.Block(no_gpsimd_drain=True) as blk:
                        @blk.sync
                        def _(e):
                            dbg_dump(e, "attnout", s1[:, :])

                ln_plain(s1[:, :], s2[:, :], s3[:, :])
                with nc.Block(no_gpsimd_drain=True) as blk:
                    @blk.vector
                    def _(e):
                        wt(e, "sx")
                        e.tensor_tensor(s1[:, :], x_tm[:, :], s2[:, :], ALU.add)

                with nc.Block(no_gpsimd_drain=True) as blk:
                    @blk.sync
                    def _(e):
                        dma(e, x1sp[own_t0:own_t0 + T, :], s1[:, :], "ssp")

            def partial_out():
                for j in range(16):
                    with nc.Block(no_gpsimd_drain=True) as blk:
                        @blk.sync
                        def _(e, j=j):
                            if KPART in ("attn",):
                                wt(e, "ssp")
                                dma(e, x_tm[:, :], x1sp[j * T:(j + 1) * T, :],
                                    "sx")
                            else:
                                dma(e, x_tm[:, :],
                                    xin[OWNTOK + j * T:OWNTOK + (j + 1) * T, :],
                                    "sx")

                    with nc.Block(no_gpsimd_drain=True) as blk:
                        @blk.sync
                        def _(e, j=j):
                            wt(e, "sx")
                            dma(e, xout[j * T:(j + 1) * T, :], x_tm[:, :],
                                "sout")

                with nc.Block(no_gpsimd_drain=True) as blk:
                    @blk.sync
                    def _(e):
                        wt(e, "sout")

            if KPART == "setup":
                partial_out()
                return nc
            load_wsplit(wA_hi, wA_lo, wT["wk"])
            load_wsplit(wB_hi, wB_lo, wT["wv"])
            for n in range(0, KCTX):
                chunk_part1(n, n % 2, is_ctx=True)
            if KPART == "ctx":
                partial_out()
                return nc
            for g in range(KGRP):
                n0 = OWN0 + 2 * g
                if g > 0:
                    load_wsplit(wA_hi, wA_lo, wT["wk"])
                    load_wsplit(wB_hi, wB_lo, wT["wv"])
                for i in range(2):
                    chunk_part1(n0 + i, i, is_ctx=False)
                if KPART == "ga":
                    partial_out()
                    return nc
                load_wsplit(wA_hi, wA_lo, wT["wr"])
                load_wsplit(wB_hi, wB_lo, wT["wg"])
                for i in range(2):
                    chunk_part2(n0 + i, i)
                if KPART == "gb":
                    partial_out()
                    return nc
                load_wsplit(wA_hi, wA_lo, wT["wo"])
                for i in range(2):
                    chunk_part3(n0 + i, i)
            if KPART == "attn":
                partial_out()
                return nc

        with ExitStack() as _f:
            h_hi = _f.enter_context(nc.sbuf_tensor("h_hi", [128, 8 * 2048], BF16))
            h_lo = _f.enter_context(nc.sbuf_tensor("h_lo", [128, 8 * 2048], BF16))
            rw_hi = _f.enter_context(nc.sbuf_tensor("rw_hi", [128, 64], BF16))
            rw_lo = _f.enter_context(nc.sbuf_tensor("rw_lo", [128, 64], BF16))
            f1 = _f.enter_context(nc.sbuf_tensor("f1", [128, 2048], F32))
            f2 = _f.enter_context(nc.sbuf_tensor("f2", [128, 1024], F32))
            f3 = _f.enter_context(nc.sbuf_tensor("f3", [128, 1024], F32))
            f4 = _f.enter_context(nc.sbuf_tensor("f4", [128, 1024], F32))
            hv_hi = h_hi[:, :].rearrange("p (cc x) -> p cc x", x=2048)
            hv_lo = h_lo[:, :].rearrange("p (cc x) -> p cc x", x=2048)

            with ExitStack() as _h:
                gamb = _h.enter_context(nc.sbuf_tensor("gamb", [128, 1024], F32))
                betb = _h.enter_context(nc.sbuf_tensor("betb", [128, 1024], F32))
                rowf = _h.enter_context(nc.sbuf_tensor("rowf", [1, 2048], F32))
                x1b = _h.enter_context(nc.sbuf_tensor("x1b", [128, 16384], F32))
                sq8 = _h.enter_context(nc.sbuf_tensor("sq8", [128, 8192], F32))
                mus = _h.enter_context(nc.sbuf_tensor("mus", [128, 64], F32))
                x1v = r3(x1b[:, :], 1024)

                with nc.Block(no_gpsimd_drain=True) as blk:
                    @blk.sync
                    def _(e):
                        dma(e, rowf[0:1, :], gbrow_d[0:1, :], "sw")
                        dma(e, r3(f2[:, 0:64], 8),
                            rwT_d[:, :].rearrange("(cc p) x -> p cc x", p=128),
                            "sw")
                        wt(e, "ssp")
                        dma(e, x1v,
                            x1sp.rearrange("(j p) c -> p j c", p=128), "sx")

                with nc.Block(no_gpsimd_drain=True) as blk:
                    @blk.vector
                    def _(e):
                        wt(e, "sw")
                        e.tensor_copy(rw_hi[:, :], f2[:, 0:64])
                        e.tensor_tensor(rw_lo[:, :], f2[:, 0:64], rw_hi[:, :],
                                        ALU.subtract)

                bcast_row(rowf[0:1, 0:1024], gamb[:, :], 1024)
                bcast_row(rowf[0:1, 1024:2048], betb[:, :], 1024)

                def ln_batch(xv, n, gb=None, bb=None):
                    h_ = n // 2
                    sqv = r3(sq8[:, 0:h_ * 1024], 1024)
                    with nc.Block(no_gpsimd_drain=True) as blk:
                        @blk.vector
                        def _(e):
                            e.reduce_sum(mus[:, 0:n], xv, axis=AX.X)
                            e.tensor_tensor(sqv, xv[:, 0:h_], xv[:, 0:h_],
                                            ALU.mult)
                            e.reduce_sum(mus[:, 16:16 + h_], sqv, axis=AX.X)
                            e.tensor_tensor(sqv, xv[:, h_:n], xv[:, h_:n],
                                            ALU.mult)
                            vfence(e, e.reduce_sum(mus[:, 16 + h_:16 + n], sqv,
                                                   axis=AX.X))
                            vfence(e, e.tensor_scalar(mus[:, 0:n], mus[:, 0:n],
                                                      1.0 / 1024, None, ALU.mult))
                            vfence(e, e.tensor_tensor(mus[:, 32:32 + n],
                                                      mus[:, 0:n], mus[:, 0:n],
                                                      ALU.mult))
                            e.scalar_tensor_tensor(mus[:, 16:16 + n],
                                                   mus[:, 16:16 + n], 1.0 / 1024,
                                                   mus[:, 32:32 + n],
                                                   ALU.mult, ALU.subtract)

                    with nc.Block(no_gpsimd_drain=True) as blk:
                        @blk.scalar
                        def _(e):
                            e.activation(mus[:, 16:16 + n], mus[:, 16:16 + n],
                                         AF.Sqrt, bias=epsc[:, 0:1])

                    with nc.Block(no_gpsimd_drain=True) as blk:
                        @blk.vector
                        def _(e):
                            vfence(e, e.reciprocal(mus[:, 16:16 + n],
                                                   mus[:, 16:16 + n]))
                            e.tensor_tensor(xv, xv,
                                            mus[:, 0:n].unsqueeze(2)
                                            .broadcast_to([128, n, 1024]),
                                            ALU.subtract)
                            e.tensor_tensor(xv, xv,
                                            mus[:, 16:16 + n].unsqueeze(2)
                                            .broadcast_to([128, n, 1024]),
                                            ALU.mult)
                            if gb is not None:
                                e.tensor_tensor(xv, xv,
                                                gb[:, :].unsqueeze(1)
                                                .broadcast_to([128, n, 1024]),
                                                ALU.mult)
                                e.tensor_tensor(xv, xv,
                                                bb[:, :].unsqueeze(1)
                                                .broadcast_to([128, n, 1024]),
                                                ALU.add)

                with nc.Block(no_gpsimd_drain=True) as blk:
                    @blk.vector
                    def _(e):
                        wt(e, "sx")

                ln_batch(x1v, 16)
                ln_batch(x1v, 16, gamb, betb)

                def _trans(e, j):
                    for oc in range(NCC):
                        e.transpose(ps[:, (j % 2) * 1024 + oc * 128:
                                       (j % 2) * 1024 + (oc + 1) * 128],
                                    x1b[:, j * 1024 + oc * 128:
                                        j * 1024 + (oc + 1) * 128], ident[:, :])

                def _hevac(e, j):
                    dsth = hv_hi[:, :, j * 128:(j + 1) * 128]
                    dstl = hv_lo[:, :, j * 128:(j + 1) * 128]
                    srcv = r3(ps[:, (j % 2) * 1024:(j % 2) * 1024 + 1024], 128)
                    e.tensor_copy(dsth, srcv)
                    e.tensor_tensor(dstl, srcv, dsth, ALU.subtract)

                def _router(e, j):
                    out = ps[:, 4 * 512 + j * 8:4 * 512 + (j + 1) * 8]
                    for cc in range(NCC):
                        a_hi = hv_hi[:, cc, j * 128:(j + 1) * 128]
                        a_lo = hv_lo[:, cc, j * 128:(j + 1) * 128]
                        e.matmul(out, a_hi, rw_hi[:, cc * 8:cc * 8 + 8],
                                 start=(cc == 0), stop=False)
                        e.matmul(out, a_hi, rw_lo[:, cc * 8:cc * 8 + 8],
                                 start=False, stop=False)
                        e.matmul(out, a_lo, rw_hi[:, cc * 8:cc * 8 + 8],
                                 start=False, stop=(cc == NCC - 1))

                for j in range(18):
                    with nc.Block(no_gpsimd_drain=True) as blk:
                        @blk.tensor
                        def _(e, j=j):
                            if j < 16:
                                _trans(e, j)
                            if 0 <= j - 2 < 16:
                                _router(e, j - 2)
                        if 0 <= j - 1 < 16:
                            @blk.vector
                            def _(e, j=j):
                                _hevac(e, j - 1)

                lg = ps[:, 4 * 512:4 * 512 + 128]
                lgv = r3(lg, 8)
                with nc.Block(no_gpsimd_drain=True) as blk:
                    @blk.vector
                    def _(e):
                        vfence(e, e.reduce_max(mus[:, 0:16], lgv, axis=AX.X))
                        e.tensor_tensor(r3(sq8[:, 0:128], 8), lgv,
                                        mus[:, 0:16].unsqueeze(2)
                                        .broadcast_to([128, 16, 8]),
                                        ALU.subtract)

                with nc.Block(no_gpsimd_drain=True) as blk:
                    @blk.scalar
                    def _(e):
                        e.activation(sq8[:, 128:256], sq8[:, 0:128], AF.Exp)

                with nc.Block(no_gpsimd_drain=True) as blk:
                    @blk.vector
                    def _(e):
                        ev = r3(sq8[:, 128:256], 8)
                        e.reduce_sum(mus[:, 0:16], ev, axis=AX.X)
                        vfence(e, e.reduce_max(mus[:, 16:32], ev, axis=AX.X))
                        vfence(e, e.reciprocal(mus[:, 0:16], mus[:, 0:16]))
                        vfence(e, e.tensor_tensor(mus[:, 32:48], mus[:, 16:32],
                                                  mus[:, 0:16], ALU.mult))
                        e.tensor_tensor(r3(gates[:, :], 8), ev,
                                        mus[:, 16:32].unsqueeze(2)
                                        .broadcast_to([128, 16, 8]), ALU.is_ge)
                        e.tensor_tensor(r3(gates[:, :], 8), r3(gates[:, :], 8),
                                        mus[:, 32:48].unsqueeze(2)
                                        .broadcast_to([128, 16, 8]), ALU.mult)

            if KDBG:
                with nc.Block(no_gpsimd_drain=True) as blk:
                    @blk.sync
                    def _(e):
                        dbg_dump(e, "gates", gates[:, :])

            with ExitStack() as _e2:
                q1 = _e2.enter_context(nc.sbuf_tensor("q1", [128, 8 * 2048], FP8))
                q2 = _e2.enter_context(nc.sbuf_tensor("q2", [128, 16 * 1024], FP8))
                accb = _e2.enter_context(nc.sbuf_tensor("accb", [128, 8 * 512], F32))
                scl1c = _e2.enter_context(nc.sbuf_tensor("scl1c", [128, 16], F32))
                scl2c = _e2.enter_context(nc.sbuf_tensor("scl2c", [128, 8], F32))
                sclrow = _e2.enter_context(nc.sbuf_tensor("sclrow", [1, 2048], F32))
                grow = sclrow
                def quant_weight(dram2d, nchunks, width, qdst, sclc, scl_n,
                                 wstf, fsc, thrb, nthrb):
                    nseg = width // 512
                    rows_per_grp = 2048 // width
                    ngrp = nchunks // rows_per_grp
                    gview = dram2d.rearrange("(g s p) w -> g p s w", p=128,
                                             s=rows_per_grp)

                    def _dma(e, g):
                        dma(e, r3(wstf[:, (g % 2) * 2048:(g % 2) * 2048 + 2048],
                                  width), gview[g], "sw")

                    def _abs(e, g):
                        e.activation(fsc[:, (g % 2) * 2048:(g % 2) * 2048 + 2048],
                                     wstf[:, (g % 2) * 2048:(g % 2) * 2048 + 2048],
                                     AF.Abs)

                    def _mm(e, g):
                        fb = (g % 2) * 2048
                        for s in range(rows_per_grp):
                            for sgi in range(nseg):
                                e.matmul(ps[0:1, 2048 + sgi * 512:
                                            2048 + sgi * 512 + 512],
                                         onescol[:, 0:1],
                                         fsc[:, fb + s * width + sgi * 512:
                                             fb + s * width + (sgi + 1) * 512],
                                         start=(g == 0 and s == 0),
                                         stop=(g == ngrp - 1 and
                                               s == rows_per_grp - 1))

                    waits = {}
                    for g in range(ngrp + 2):
                        with nc.Block(no_gpsimd_drain=True) as blk:
                            if g < ngrp:
                                @blk.sync
                                def _(e, g=g):
                                    _dma(e, g)
                                waits[g] = cnt["sw"]
                            if 0 <= g - 1 < ngrp:
                                @blk.scalar
                                def _(e, g=g):
                                    e.wait_ge(sems["sw"], waits[g - 1])
                                    _abs(e, g - 1)
                            if 0 <= g - 2 < ngrp:
                                @blk.tensor
                                def _(e, g=g):
                                    _mm(e, g - 2)

                    with nc.Block(no_gpsimd_drain=True) as blk:
                        @blk.vector
                        def _(e):
                            e.tensor_scalar(sclrow[0:1, 0:width],
                                            ps[0:1, 2048:2048 + width],
                                            1.0 / (nchunks * 128), 1e-5,
                                            ALU.mult, ALU.max)

                    with nc.Block(no_gpsimd_drain=True) as blk:
                        @blk.tensor
                        def _(e):
                            for q in range(scl_n):
                                e.matmul(ps[:, 2048 + q:2048 + q + 1],
                                         sclrow[0:1, q * 128:(q + 1) * 128],
                                         ones_r[0:1, 0:1], start=True, stop=True)
                            for sgi in range(nseg):
                                e.matmul(ps[:, sgi * 512:(sgi + 1) * 512],
                                         ones_r[0:1, :],
                                         sclrow[0:1, sgi * 512:(sgi + 1) * 512],
                                         start=True, stop=True)

                    with nc.Block(no_gpsimd_drain=True) as blk:
                        @blk.vector
                        def _(e):
                            e.tensor_copy(sclc[:, 0:scl_n],
                                          ps[:, 2048:2048 + scl_n])
                            e.tensor_scalar(thrb[:, 0:width], ps[:, 0:width],
                                            0.5, None, ALU.mult)
                            e.tensor_scalar(nthrb[:, 0:width], thrb[:, 0:width],
                                            -1.0, None, ALU.mult)

                    def _quant(e, g):
                        fb = (g % 2) * 2048
                        for s in range(rows_per_grp):
                            wsl = wstf[:, fb + s * width:fb + (s + 1) * width]
                            fsl = fsc[:, s * width:(s + 1) * width]
                            ch = g * rows_per_grp + s
                            e.tensor_tensor(fsl, wsl, thrb[:, 0:width], ALU.is_ge)
                            e.tensor_tensor(wsl, wsl, nthrb[:, 0:width], ALU.is_le)
                            e.tensor_tensor(qdst[:, ch * width:(ch + 1) * width],
                                            fsl, wsl, ALU.subtract)

                    for g in range(ngrp + 1):
                        with nc.Block(no_gpsimd_drain=True) as blk:
                            if g < ngrp:
                                @blk.sync
                                def _(e, g=g):
                                    _dma(e, g)
                                waits[g] = cnt["sw"]
                            if 0 <= g - 1 < ngrp:
                                @blk.vector
                                def _(e, g=g):
                                    e.wait_ge(sems["sw"], waits[g - 1])
                                    _quant(e, g - 1)

                for eidx in list(range(8))[:max(0, KEXP - 1)] + [8]:
                    shared = (eidx == 8)
                    w1d = sw1T if shared else ew1T[eidx, :, :]
                    w2d = sw2T if shared else ew2T[eidx, :, :]
                    _qe = ExitStack()
                    wstf = _qe.enter_context(
                        nc.sbuf_tensor(f"wstf{eidx}", [128, 4096], F32))
                    fsc = _qe.enter_context(
                        nc.sbuf_tensor(f"fsc{eidx}", [128, 4096], F32))
                    thrb = _qe.enter_context(
                        nc.sbuf_tensor(f"thrb{eidx}", [128, 2048], F32))
                    nthrb = _qe.enter_context(
                        nc.sbuf_tensor(f"nthrb{eidx}", [128, 2048], F32))
                    quant_weight(w1d, NCC, 2048, q1, scl1c, 16,
                                 wstf, fsc, thrb, nthrb)
                    quant_weight(w2d, 16, 1024, q2, scl2c, 8,
                                 wstf, fsc, thrb, nthrb)
                    _qe.close()

                    if not shared:
                        with nc.Block(no_gpsimd_drain=True) as blk:
                            @blk.tensor
                            def _(e, eidx=eidx):
                                for j in range(16):
                                    e.transpose(ps[0:1, j * 128:(j + 1) * 128],
                                                gates[:, j * 8 + eidx:
                                                      j * 8 + eidx + 1],
                                                ident[:, :])

                        with nc.Block(no_gpsimd_drain=True) as blk:
                            @blk.vector
                            def _(e):
                                e.tensor_copy(grow[0:1, :], ps[0:1, 0:2048])

                    if KDBG and eidx == 0:
                        with nc.Block(no_gpsimd_drain=True) as blk:
                            @blk.vector
                            def _(e):
                                e.tensor_copy(f1[:, 0:16], scl1c[:, :])
                                e.tensor_copy(f1[:, 16:24], scl2c[:, :])
                                e.tensor_copy(f2[:, 0:1024], q1[:, 0:1024])

                        with nc.Block(no_gpsimd_drain=True) as blk:
                            @blk.sync
                            def _(e):
                                dbg_dump(e, "scls", f1[:, 0:24])
                                dbg_dump(e, "q1c0", f2[:, 0:1024])

                    _me = ExitStack()
                    a1_hi = _me.enter_context(
                        nc.sbuf_tensor(f"a1h{eidx}", [128, 16 * 512], BF16))
                    a1_lo = _me.enter_context(
                        nc.sbuf_tensor(f"a1l{eidx}", [128, 16 * 512], BF16))
                    for tp in range(NTAU):
                        done_sac = cnt["sac"]
                        with nc.Block(no_gpsimd_drain=True) as blk:
                            @blk.sync
                            def _(e, tp=tp, eidx=eidx):
                                e.wait_ge(sems["sac"], done_sac)
                                if eidx > 0:
                                    dma(e, accb[:, :],
                                        accd[:, tp * 4096:(tp + 1) * 4096],
                                        "sx")

                            if not shared:
                                @blk.tensor
                                def _(e, tp=tp):
                                    e.matmul(ps[:, 7 * 512:8 * 512],
                                             ones_r[0:1, :],
                                             grow[0:1, tp * TAU:(tp + 1) * TAU],
                                             start=True, stop=True)
                        acc_ready = cnt["sx"]

                        def silu_pair(e, fcp):
                            b0 = (fcp % 2) * 2
                            for k2 in range(2):
                                fc = fcp * 2 + k2
                                e.activation(
                                    f1[:, (fcp % 2) * 1024 + k2 * 512:
                                       (fcp % 2) * 1024 + (k2 + 1) * 512],
                                    ps[:, (b0 + k2) * 512:(b0 + k2 + 1) * 512],
                                    AF.Silu, scale=scl1c[:, fc:fc + 1])

                        def gate_split_pair(e, fcp, shared=shared):
                            fb = (fcp % 2) * 1024
                            srcap = f1[:, fb:fb + 1024]
                            if not shared:
                                e.tensor_tensor(
                                    r3(f2[:, 0:1024], 512), r3(srcap, 512),
                                    ps[:, 7 * 512:8 * 512].unsqueeze(1)
                                    .broadcast_to([128, 2, 512]), ALU.mult)
                                srcap = f2[:, 0:1024]
                            o = fcp * 1024
                            e.tensor_copy(a1_hi[:, o:o + 1024], srcap)
                            e.tensor_tensor(a1_lo[:, o:o + 1024], srcap,
                                            a1_hi[:, o:o + 1024], ALU.subtract)

                        def mm1_pair(e, fcp, tp):
                            b0 = (fcp % 2) * 2
                            for k2 in range(2):
                                fc = fcp * 2 + k2
                                out = ps[:, (b0 + k2) * 512:(b0 + k2 + 1) * 512]
                                for cc in range(NCC):
                                    qsl = q1[:, cc * 2048 + fc * 128:
                                             cc * 2048 + (fc + 1) * 128]
                                    e.matmul(out, qsl,
                                             hv_hi[:, cc, tp * 512:(tp + 1) * 512],
                                             start=(cc == 0), stop=False)
                                    e.matmul(out, qsl,
                                             hv_lo[:, cc, tp * 512:(tp + 1) * 512],
                                             start=False, stop=(cc == NCC - 1))

                        for fcp in range(10):
                            with nc.Block(no_gpsimd_drain=True) as blk:
                                if fcp < 8:
                                    @blk.tensor
                                    def _(e, fcp=fcp, tp=tp):
                                        mm1_pair(e, fcp, tp)
                                if 0 <= fcp - 1 < 8:
                                    @blk.scalar
                                    def _(e, fcp=fcp):
                                        silu_pair(e, fcp - 1)
                                if 0 <= fcp - 2 < 8:
                                    @blk.vector
                                    def _(e, fcp=fcp):
                                        gate_split_pair(e, fcp - 2)

                        def mm2_pair(e, dcp, tp):
                            b0 = 2 + (dcp % 2) * 2
                            for k2 in range(2):
                                dc = dcp * 2 + k2
                                out = ps[:, (b0 + k2) * 512:(b0 + k2 + 1) * 512]
                                for fc in range(16):
                                    qsl = q2[:, fc * 1024 + dc * 128:
                                             fc * 1024 + (dc + 1) * 128]
                                    e.matmul(out, qsl,
                                             a1_hi[:, fc * 512:(fc + 1) * 512],
                                             start=(fc == 0), stop=False)
                                    e.matmul(out, qsl,
                                             a1_lo[:, fc * 512:(fc + 1) * 512],
                                             start=False, stop=(fc == 15))

                        def acc_pair(e, dcp, eidx=eidx, acc_ready=acc_ready):
                            if eidx > 0:
                                e.wait_ge(sems["sx"], acc_ready)
                            b0 = 2 + (dcp % 2) * 2
                            for k2 in range(2):
                                dc = dcp * 2 + k2
                                psl = ps[:, (b0 + k2) * 512:(b0 + k2 + 1) * 512]
                                asl = accb[:, dc * 512:(dc + 1) * 512]
                                if eidx == 0:
                                    e.tensor_scalar(asl, psl,
                                                    scl2c[:, dc:dc + 1],
                                                    None, ALU.mult)
                                else:
                                    e.scalar_tensor_tensor(
                                        asl, psl, scl2c[:, dc:dc + 1], asl,
                                        ALU.mult, ALU.add)

                        for dcp in range(5):
                            with nc.Block(no_gpsimd_drain=True) as blk:
                                if dcp < 4:
                                    @blk.tensor
                                    def _(e, dcp=dcp, tp=tp):
                                        mm2_pair(e, dcp, tp)
                                if dcp >= 1:
                                    @blk.vector
                                    def _(e, dcp=dcp):
                                        acc_pair(e, dcp - 1)

                        with nc.Block(no_gpsimd_drain=True) as blk:
                            @blk.sync
                            def _(e, tp=tp):
                                dma(e, accd[:, tp * 4096:(tp + 1) * 4096],
                                    accb[:, :], "sac")
                    _me.close()

            with ExitStack() as _z:
                accf = _z.enter_context(nc.sbuf_tensor("accf", [128, 8192], F32))
                x1f = _z.enter_context(nc.sbuf_tensor("x1f", [128, 8192], F32))
                zf = _z.enter_context(nc.sbuf_tensor("zf", [128, 8192], F32))
                sqf = _z.enter_context(nc.sbuf_tensor("sqf", [128, 4096], F32))
                musf = _z.enter_context(nc.sbuf_tensor("musf", [128, 48], F32))

                with nc.Block(no_gpsimd_drain=True) as blk:
                    @blk.sync
                    def _(e):
                        wt(e, "sac")

                for hf in range(2):
                    with nc.Block(no_gpsimd_drain=True) as blk:
                        @blk.sync
                        def _(e, hf=hf):
                            wt(e, "sout")
                            dma(e, accf[:, :],
                                accd[:, hf * 8192:(hf + 1) * 8192], "sx")
                            dma(e, r3(x1f[:, :], 1024),
                                x1sp[hf * 1024:(hf + 1) * 1024, :]
                                .rearrange("(j p) c -> p j c", p=128), "sx")

                    with nc.Block(no_gpsimd_drain=True) as blk:
                        @blk.vector
                        def _(e):
                            wt(e, "sx")

                    def _ftr(e, j8):
                        tp_l, sub = j8 // 4, j8 % 4
                        for dc in range(NCC):
                            e.transpose(
                                ps[:, (j8 % 2) * 1024 + dc * 128:
                                   (j8 % 2) * 1024 + (dc + 1) * 128],
                                accf[:, tp_l * 4096 + dc * 512 + sub * 128:
                                     tp_l * 4096 + dc * 512 + sub * 128 + 128],
                                ident[:, :])

                    def _fev(e, j8):
                        e.tensor_copy(
                            zf[:, j8 * 1024:(j8 + 1) * 1024],
                            ps[:, (j8 % 2) * 1024:(j8 % 2) * 1024 + 1024])

                    for j8 in range(9):
                        with nc.Block(no_gpsimd_drain=True) as blk:
                            if j8 < 8:
                                @blk.tensor
                                def _(e, j8=j8):
                                    _ftr(e, j8)
                            if j8 >= 1:
                                @blk.vector
                                def _(e, j8=j8):
                                    _fev(e, j8 - 1)

                    zv8 = r3(zf[:, :], 1024)
                    sq4 = r3(sqf[:, :], 1024)
                    with nc.Block(no_gpsimd_drain=True) as blk:
                        @blk.vector
                        def _(e):
                            e.reduce_sum(musf[:, 0:8], zv8, axis=AX.X)
                            e.tensor_tensor(sq4, zv8[:, 0:4], zv8[:, 0:4],
                                            ALU.mult)
                            e.reduce_sum(musf[:, 16:20], sq4, axis=AX.X)
                            e.tensor_tensor(sq4, zv8[:, 4:8], zv8[:, 4:8],
                                            ALU.mult)
                            vfence(e, e.reduce_sum(musf[:, 20:24], sq4,
                                                   axis=AX.X))
                            vfence(e, e.tensor_scalar(musf[:, 0:8], musf[:, 0:8],
                                                      1.0 / 1024, None, ALU.mult))
                            vfence(e, e.tensor_tensor(musf[:, 32:40],
                                                      musf[:, 0:8], musf[:, 0:8],
                                                      ALU.mult))
                            e.scalar_tensor_tensor(musf[:, 16:24], musf[:, 16:24],
                                                   1.0 / 1024, musf[:, 32:40],
                                                   ALU.mult, ALU.subtract)

                    with nc.Block(no_gpsimd_drain=True) as blk:
                        @blk.scalar
                        def _(e):
                            e.activation(musf[:, 16:24], musf[:, 16:24], AF.Sqrt,
                                         bias=epsc[:, 0:1])

                    with nc.Block(no_gpsimd_drain=True) as blk:
                        @blk.vector
                        def _(e):
                            vfence(e, e.reciprocal(musf[:, 16:24],
                                                   musf[:, 16:24]))
                            e.tensor_tensor(zv8, zv8,
                                            musf[:, 0:8].unsqueeze(2)
                                            .broadcast_to([128, 8, 1024]),
                                            ALU.subtract)
                            e.tensor_tensor(zv8, zv8,
                                            musf[:, 16:24].unsqueeze(2)
                                            .broadcast_to([128, 8, 1024]),
                                            ALU.mult)
                            e.tensor_tensor(zf[:, :], zf[:, :], x1f[:, :],
                                            ALU.add)

                    with nc.Block(no_gpsimd_drain=True) as blk:
                        @blk.sync
                        def _(e, hf=hf):
                            dma(e, xout[hf * 1024:(hf + 1) * 1024, :]
                                .rearrange("(j p) c -> p j c", p=128),
                                r3(zf[:, :], 1024), "sout")

            with nc.Block(no_gpsimd_drain=True) as blk:
                @blk.sync
                def _(e):
                    wt(e, "sout")

    return nc


_NC_CACHE = None


def kernel(**inputs):
    global _NC_CACHE
    if _NC_CACHE is None:
        _NC_CACHE = build_nc()
    nc = _NC_CACHE

    x = np.asarray(inputs["L0_attn_xq"], dtype=np.float32)
    tm = {m: np.asarray(inputs[f"L0_attn_time_mix_{m}"], np.float32).reshape(C)
          for m in ("k", "v", "r", "g")}
    tmix = np.zeros((128, 32), np.float32)
    for mi, m in enumerate(("k", "v", "r", "g")):
        tmix[:, mi * 8:(mi + 1) * 8] = tm[m].reshape(8, 128).T
    tdu = np.concatenate([
        np.asarray(inputs["L0_attn_time_decay"], np.float32),
        np.asarray(inputs["L0_attn_time_faaaa"], np.float32)]).reshape(1, 32)
    lnrow = np.concatenate([
        np.asarray(inputs["L0_attn_ln_w"], np.float32),
        np.asarray(inputs["L0_attn_ln_b"], np.float32)]).reshape(1, 2048)
    gbrow = np.concatenate([
        np.asarray(inputs["L0_ffn_ln_gamma"], np.float32),
        np.asarray(inputs["L0_ffn_ln_beta"], np.float32)]).reshape(1, 2048)

    def t2(a):
        return np.ascontiguousarray(np.asarray(a, np.float32).T)

    com = {
        "wrT": t2(inputs["L0_attn_W_r"]), "wkT": t2(inputs["L0_attn_W_k"]),
        "wvT": t2(inputs["L0_attn_W_v"]), "wgT": t2(inputs["L0_attn_W_g"]),
        "woT": t2(inputs["L0_attn_W_o"]),
        "tmix": tmix, "tdu": tdu, "lnrow": lnrow, "gbrow": gbrow,
        "sw1T": t2(inputs["L0_ffn_shared_w1"]),
        "sw2T": t2(inputs["L0_ffn_shared_w2"]),
        "ew1T": np.ascontiguousarray(
            np.asarray(inputs["L0_ffn_expert_w1"], np.float32)
            .transpose(0, 2, 1)),
        "ew2T": np.ascontiguousarray(
            np.asarray(inputs["L0_ffn_expert_w2"], np.float32)
            .transpose(0, 2, 1)),
        "rwT": t2(inputs["L0_ffn_router_w"]),
        "ident": np.eye(128, dtype=np.float32),
    }

    in_maps = []
    for core in range(8):
        b, half = core // 2, core % 2
        own = x[b, half * OWNTOK:(half + 1) * OWNTOK, :]
        ctx = (x[b, 0:OWNTOK, :] if half == 1
               else np.zeros((OWNTOK, C), np.float32))
        m = dict(com)
        m["xin"] = np.ascontiguousarray(np.concatenate([ctx, own], axis=0))
        in_maps.append(m)

    res = run_bass_kernel_spmd(nc, in_maps, core_ids=list(range(8)))
    out = np.empty((B, TT, C), np.float32)
    for core in range(8):
        b, half = core // 2, core % 2
        out[b, half * OWNTOK:(half + 1) * OWNTOK, :] = res.results[core]["xout"]
    return out
